# version 08
# baseline (speedup 1.0000x reference)
import math
from contextlib import ExitStack

import numpy as np
import ml_dtypes

import concourse.bass as bass
import concourse.bass_utils as _bass_utils
import concourse.mybir as mybir
from concourse.bass_utils import run_bass_kernel_spmd

import os as _os
if _os.environ.get("KB_NO_LDWOPT", "") == "":
    _LDW_ENABLE = True
else:
    _LDW_ENABLE = False
if _LDW_ENABLE and not getattr(_bass_utils, "_ldw_opt_patched", False):
    _orig_run_command = _bass_utils.run_command

    def _run_command_ldw(argv, **kw):
        argv = ["--enable-ldw-opt=true" if a == "--enable-ldw-opt=false" else a for a in argv]
        return _orig_run_command(argv, **kw)

    _bass_utils.run_command = _run_command_ldw
    _bass_utils._ldw_opt_patched = True

F32 = mybir.dt.float32
BF16 = mybir.dt.bfloat16
AF = mybir.ActivationFunctionType
ALU = mybir.AluOpType
AX = mybir.AxisListType

NCORES = 8
B, S, DM = 16, 2048, 2048
T = (B // NCORES) * S
NCH = T // 128
KT = DM // 128
H, HKV, HD, F = 16, 4, 128, 768
NE, I2, II = 16, 2048, 1024
EPS = 1e-8

DEV_PHASES = ("A", "B", "D", "C")
DEV_DUMP = False


def _cdma(q, out, in_):
    return q.dma_start(out=out, in_=in_)


def build_program(phases=DEV_PHASES, dump=DEV_DUMP):
    nc = bass.Bass("TRN2", target_bir_lowering=False, debug=False)

    x_h = nc.dram_tensor("x", [T, DM], F32, kind="ExternalInput").ap()
    out_h = nc.dram_tensor("out", [T, DM], F32, kind="ExternalOutput").ap()

    wqT_h = nc.dram_tensor("wqT", [KT, 128, DM], F32, kind="ExternalInput").ap()
    wgT_h = nc.dram_tensor("wgT", [KT, 128, DM], F32, kind="ExternalInput").ap()
    wmbT_h = nc.dram_tensor("wmbT", [KT, 128, DM], F32, kind="ExternalInput").ap()
    wkT_h = nc.dram_tensor("wkT", [KT, 128, 512], F32, kind="ExternalInput").ap()
    wvT_h = nc.dram_tensor("wvT", [KT, 128, 512], F32, kind="ExternalInput").ap()
    bq_h = nc.dram_tensor("bq", [1, DM], F32, kind="ExternalInput").ap()
    bg_h = nc.dram_tensor("bg", [1, DM], F32, kind="ExternalInput").ap()
    bmb_h = nc.dram_tensor("bmb", [1, DM], F32, kind="ExternalInput").ap()
    bk_h = nc.dram_tensor("bk", [1, 512], F32, kind="ExternalInput").ap()
    bv_h = nc.dram_tensor("bv", [1, 512], F32, kind="ExternalInput").ap()
    memcat_h = nc.dram_tensor("memcat", [H, F, 129], F32, kind="ExternalInput").ap()
    normr_h = nc.dram_tensor("normr", [H, F], F32, kind="ExternalInput").ap()
    gwT_h = nc.dram_tensor("gwT", [KT, 128, NE], F32, kind="ExternalInput").ap()
    wgu_h = nc.dram_tensor("wgu", [NE, DM, I2], F32, kind="ExternalInput").ap()
    wdn_h = nc.dram_tensor("wdn", [NE, II, DM], F32, kind="ExternalInput").ap()
    identf_h = nc.dram_tensor("identf", [128, 128], F32, kind="ExternalInput").ap()
    identb_h = nc.dram_tensor("identb", [128, 128], BF16, kind="ExternalInput").ap()
    onesb_h = nc.dram_tensor("onesb", [1, 128], BF16, kind="ExternalInput").ap()

    xnT_hbm = nc.dram_tensor("xnT_hbm", [KT, 128, T], BF16).ap()
    mh_hbm = nc.dram_tensor("mh_hbm", [T, DM], F32).ap()
    dcc_in = nc.dram_tensor("dcc_in", [128, 12384], F32)
    dcc_out = nc.dram_tensor("dcc_out", [128, 12384], F32)

    dbg = {}
    if dump:
        dbg["xnT"] = nc.dram_tensor("dbg_xnT", [KT, 128, T], BF16, kind="ExternalOutput").ap()
        dbg["combine"] = nc.dram_tensor("dbg_combine", [128, NCH, NE], F32, kind="ExternalOutput").ap()
        dbg["newmem"] = nc.dram_tensor("dbg_newmem", [128, H, 6, 129], BF16, kind="ExternalOutput").ap()
        dbg["dccout"] = nc.dram_tensor("dbg_dccout", [128, 12384], F32, kind="ExternalOutput").ap()
        dbg["dccin"] = nc.dram_tensor("dbg_dccin", [128, 12384], F32, kind="ExternalOutput").ap()
        dbg["bb"] = nc.dram_tensor("dbg_bb", [128, 1043], F32, kind="ExternalOutput").ap()
        dbg["bb_sb"] = nc.alloc_sbuf_tensor("dbg_bb_s", [128, 1043], F32)
        dbg["fTm"] = nc.dram_tensor("dbg_fTm", [128, NCH, F], BF16, kind="ExternalOutput").ap()
        dbg["vmb"] = nc.dram_tensor("dbg_vmb", [128, NCH, 640], BF16, kind="ExternalOutput").ap()

    identf = nc.alloc_sbuf_tensor("identf_s", [128, 128], F32)
    identb = nc.alloc_sbuf_tensor("identb_s", [128, 128], BF16)
    onesb = nc.alloc_sbuf_tensor("onesb_s", [1, 128], BF16)
    combine = nc.alloc_sbuf_tensor("combine", [128, NCH, NE], F32)

    sem_names = [
        "sLD", "sGD", "sOUT", "sXW", "sXR",
        "sXL0", "sXL1", "sGW", "sXW0", "sXW1", "sOUT0", "sOUT1",
        "sVs", "sVq", "sAr", "sVxa", "sVxn", "sVxt", "sVro",
        "sPt", "sPr", "sAe", "sAp",
        "sB", "sCC", "sDD",
        "sPk", "sPv", "sPm", "sPn", "sPd", "sVm", "sVv", "sVf", "sVw", "sAq", "sAg",
        "sPq", "sPg", "sPu", "sPy", "sVi", "sVy", "sAs", "sPB", "sAB", "sVN", "sPN", "sAt", "sVn", "sGu", "sGw",
    ]
    sem = {n: nc.alloc_semaphore(n) for n in sem_names}
    cnt = {n: 0 for n in sem_names}

    ENGS = ["sync", "gpsimd", "vector", "scalar", "tensor"]

    def barrier(block, nb):
        target = 5 * nb
        for name in ENGS:
            eng = getattr(block, name)

            @eng
            def _(e, target=target):
                e.nop().then_inc(sem["sB"], 1)
                e.wait_ge(sem["sB"], target)

    with nc.Block() as block:
        @block.sync
        def _(sync):
            sync.dma_start(out=identf[:], in_=identf_h).then_inc(sem["sLD"], 16)
            sync.dma_start(out=identb[:], in_=identb_h).then_inc(sem["sLD"], 16)
            sync.dma_start(out=onesb[:], in_=onesb_h).then_inc(sem["sLD"], 16)
        cnt["sLD"] += 48

        nbar = 0
        if "A" in phases:
            emit_phase_A(nc, block, sem, cnt, locals())
        nbar += 1
        barrier(block, nbar)
        if "B" in phases:
            emit_phase_B(nc, block, sem, cnt, locals())
        nbar += 1
        barrier(block, nbar)
        if "D" in phases:
            emit_phase_D(nc, block, sem, cnt, locals())
        nbar += 1
        barrier(block, nbar)
        if "C" in phases:
            emit_phase_C(nc, block, sem, cnt, locals())
        nbar += 1
        barrier(block, nbar)

        n_out = cnt["sOUT"]
        n_out0 = cnt["sOUT0"]
        n_out1 = cnt["sOUT1"]

        @block.gpsimd
        def _(gpsimd):
            gpsimd.wait_ge(sem["sOUT"], n_out)
            gpsimd.wait_ge(sem["sOUT0"], n_out0)
            gpsimd.wait_ge(sem["sOUT1"], n_out1)

        if dump:
            emit_dumps(nc, block, sem, cnt, locals(), dbg, phases)

    return nc


def emit_phase_A(nc, block, sem, cnt, env):
    dbg = env.get("dbg", {})
    x_h, out_h, xnT_hbm = env["x_h"], env["out_h"], env["xnT_hbm"]
    gwT_h = env["gwT_h"]
    identf, combine = env["identf"], env["combine"]

    st = ExitStack()
    sb = lambda n, s, d: st.enter_context(nc.sbuf_tensor(n, s, d))
    ps = lambda n, s, d: st.enter_context(nc.psum_tensor(n, s, d))

    xch = sb("A_xch", [128, 2, DM], F32)
    xnf = sb("A_xnf", [128, 2, DM], F32)
    xa = sb("A_xa", [128, 2, DM], F32)
    dumpA = sb("A_dump", [128, DM], F32)
    s1 = sb("A_s1", [128, 1], F32)
    s2 = sb("A_s2", [128, 1], F32)
    mcol = sb("A_m", [128, 1], F32)
    msq = sb("A_msq", [128, 1], F32)
    s2n = sb("A_s2n", [128, 1], F32)
    ve = sb("A_ve", [128, 1], F32)
    rv = sb("A_rv", [128, 1], F32)
    rstd = sb("A_rstd", [128, 1], F32)
    xnTf = sb("A_xnTf", [128, KT, 128], F32)
    xnT16 = sb("A_xnT16", [128, 2, KT, 128], BF16)
    gwT = sb("A_gwT", [128, KT, NE], F32)
    pr = sb("A_pr", [128, NE], F32)
    m1 = sb("A_m1", [128, 1], F32)
    m2 = sb("A_m2", [128, 1], F32)
    msk = sb("A_msk", [128, NE], F32)
    pm = sb("A_pm", [128, NE], F32)
    wsum = sb("A_wsum", [128, 1], F32)
    winv = sb("A_winv", [128, 1], F32)
    c1t = sb("A_c1t", [128, NE], F32)

    trp = [ps(f"A_tr{j}", [128, 128], F32) for j in range(4)]
    rps = [ps(f"A_rt{j}", [128, NE], F32) for j in range(2)]

    L = {k: cnt[k] for k in cnt}

    @block.sync
    def _(sync):
        sync.dma_start(out=gwT[:], in_=gwT_h.rearrange("k p e -> p k e")).then_inc(sem["sGW"], 16)
        for i in range(NCH):
            b = i % 2
            if i >= 2:
                sync.wait_ge(sem["sVxa"], L["sVxa"] + i - 1)
            sync.dma_start(out=xch[:, b, :], in_=x_h[i * 128:(i + 1) * 128, :]).then_inc(sem["sXL0" if b == 0 else "sXL1"], 16)
    cnt["sGW"] += 16
    cnt["sXL0"] += 16 * (NCH // 2)
    cnt["sXL1"] += 16 * (NCH // 2)

    vs = [L["sVs"]]

    @block.vector
    def _(vector):
        def vedge(res):
            vs[0] += 1
            res.then_inc(sem["sVs"], 1)
            vector.wait_ge(sem["sVs"], vs[0])

        for i in range(NCH):
            b = i % 2
            vector.wait_ge(sem["sXL0" if i % 2 == 0 else "sXL1"], L["sXL0" if i % 2 == 0 else "sXL1"] + 16 * (i // 2 + 1))
            vector.reduce_sum(s1[:], xch[:, b, :], AX.X)
            vedge(vector.scalar_tensor_tensor(dumpA[:], xch[:, b, :], 1.0, xch[:, b, :],
                                              ALU.mult, ALU.mult, accum_out=s2[:]))
            vector.tensor_scalar(mcol[:], s1[:], 1.0 / DM, None, ALU.mult)
            vedge(vector.tensor_scalar(s2n[:], s2[:], 1.0 / DM, 1e-5, ALU.mult, ALU.add))
            vedge(vector.scalar_tensor_tensor(ve[:], mcol[:], mcol[:], s2n[:], ALU.mult, ALU.subtract))
            vector.reciprocal(rv[:], ve[:]).then_inc(sem["sVq"], 1)
            vector.wait_ge(sem["sAr"], L["sAr"] + i + 1)
            if i >= 2:
                vector.wait_ge(sem["sPt"], L["sPt"] + 16 * (i - 1))
            vector.tensor_scalar(xnf[:, b, :], xch[:, b, :], mcol[:], rstd[:], ALU.subtract, ALU.mult).then_inc(sem["sVxn"], 1)
            if i >= 2:
                vector.wait_ge(sem["sOUT0" if i % 2 == 0 else "sOUT1"],
                               L["sOUT0" if i % 2 == 0 else "sOUT1"] + 16 * (i // 2))
            vector.tensor_tensor(xa[:, b, :], xch[:, b, :], xnf[:, b, :], ALU.add).then_inc(sem["sVxa"], 1)
            vector.wait_ge(sem["sAe"], L["sAe"] + 16 * (i + 1))
            if i >= 2:
                vector.wait_ge(sem["sXW0" if i % 2 == 0 else "sXW1"],
                               L["sXW0" if i % 2 == 0 else "sXW1"] + 16 * (i // 2))
            vector.tensor_copy(xnT16[:, b, :, :], xnTf[:, :, :]).then_inc(sem["sVxt"], 1)
            vector.wait_ge(sem["sAp"], L["sAp"] + i + 1)
            vedge(vector.reduce_max(m1[:], pr[:], AX.X))
            vedge(vector.tensor_scalar(msk[:], pr[:], m1[:], None, ALU.is_ge))
            vedge(vector.scalar_tensor_tensor(pm[:], msk[:], -1e30, pr[:], ALU.mult, ALU.add))
            vedge(vector.reduce_max(m2[:], pm[:], AX.X))
            vedge(vector.tensor_tensor(wsum[:], m1[:], m2[:], ALU.add))
            vedge(vector.reciprocal(winv[:], wsum[:]))
            vedge(vector.tensor_scalar(msk[:], pr[:], m2[:], None, ALU.is_ge))
            vedge(vector.tensor_tensor(c1t[:], msk[:], pr[:], ALU.mult))
            vector.tensor_scalar(combine[:, i, :], c1t[:], winv[:], None, ALU.mult).then_inc(sem["sVro"], 1)
    cnt["sVs"] = vs[0]
    cnt["sVq"] += NCH
    cnt["sVxa"] += NCH
    cnt["sVxn"] += NCH
    cnt["sVxt"] += NCH
    cnt["sVro"] += NCH

    @block.scalar
    def _(scalar):
        for i in range(NCH):
            scalar.wait_ge(sem["sVq"], L["sVq"] + i + 1)
            scalar.activation(rstd[:], rv[:], AF.Sqrt, bias=0.0, scale=-1.0).then_inc(sem["sAr"], 1)
            for k in range(KT):
                scalar.wait_ge(sem["sPt"], L["sPt"] + 16 * i + k + 1)
                if i > 0 and k == 0:
                    scalar.wait_ge(sem["sPr"], L["sPr"] + i)
                    scalar.wait_ge(sem["sVxt"], L["sVxt"] + i)
                scalar.copy(xnTf[:, k, :], trp[k % 4][:]).then_inc(sem["sAe"], 1)
            scalar.wait_ge(sem["sPr"], L["sPr"] + i + 1)
            if i >= 1:
                scalar.wait_ge(sem["sVro"], L["sVro"] + i)
            scalar.activation(pr[:], rps[i % 2][:], AF.Sigmoid).then_inc(sem["sAp"], 1)
    cnt["sAr"] += NCH
    cnt["sAe"] += 16 * NCH
    cnt["sAp"] += NCH

    @block.tensor
    def _(tensor):
        tensor.wait_ge(sem["sLD"], L["sLD"])
        tensor.wait_ge(sem["sGW"], L["sGW"] + 16)
        for i in range(NCH):
            b = i % 2
            tensor.wait_ge(sem["sVxn"], L["sVxn"] + i + 1)
            for k in range(KT):
                t = 16 * i + k
                if t >= 4:
                    tensor.wait_ge(sem["sAe"], L["sAe"] + t - 3)
                tensor.transpose(trp[k % 4][:], xnf[:, b, 128 * k:128 * (k + 1)], identf[:]).then_inc(sem["sPt"], 1)
            tensor.wait_ge(sem["sAe"], L["sAe"] + 16 * (i + 1))
            if i >= 2:
                tensor.wait_ge(sem["sAp"], L["sAp"] + i - 1)
            for k in range(KT):
                mm = tensor.matmul(rps[i % 2][:], xnTf[:, k, :], gwT[:, k, :],
                                   start=(k == 0), stop=(k == KT - 1))
            mm.then_inc(sem["sPr"], 1)
    cnt["sPt"] += 16 * NCH
    cnt["sPr"] += NCH

    @block.gpsimd
    def _(gpsimd):
        for i in range(NCH):
            b = i % 2
            gpsimd.wait_ge(sem["sVxa"], L["sVxa"] + i + 1)
            gpsimd.dma_start(out=out_h[i * 128:(i + 1) * 128, :], in_=xa[:, b, :]).then_inc(sem["sOUT0" if b == 0 else "sOUT1"], 16)
            gpsimd.wait_ge(sem["sVxt"], L["sVxt"] + i + 1)
            gpsimd.dma_start(out=xnT_hbm[:, :, i * 128:(i + 1) * 128].rearrange("k p t -> p k t"),
                             in_=xnT16[:, b, :, :]).then_inc(sem["sXW0" if b == 0 else "sXW1"], 16)
    cnt["sOUT0"] += 16 * (NCH // 2)
    cnt["sOUT1"] += 16 * (NCH // 2)
    cnt["sXW0"] += 16 * (NCH // 2)
    cnt["sXW1"] += 16 * (NCH // 2)

    st.close()


def emit_phase_B(nc, block, sem, cnt, env):
    xnT_hbm = env["xnT_hbm"]
    wkT_h, wvT_h, wmbT_h = env["wkT_h"], env["wvT_h"], env["wmbT_h"]
    bk_h, bv_h, bmb_h = env["bk_h"], env["bv_h"], env["bmb_h"]
    memcat_h, normr_h = env["memcat_h"], env["normr_h"]
    dcc_in, dcc_out = env["dcc_in"], env["dcc_out"]
    identb, onesb = env["identb"], env["onesb"]

    st = ExitStack()
    sb = lambda n, s, d: st.enter_context(nc.sbuf_tensor(n, s, d))

    fTm = sb("B_fTm", [128, NCH, F], BF16)
    fT = sb("B_fT", [128, 6, T], BF16)
    v_sb = sb("B_v", [128, NCH, HD], BF16)
    mb_sb = sb("B_mb", [128, NCH, 512], BF16)
    s_sb = sb("B_s", [128, NCH], F32)
    nmax_sb = sb("B_nmax", [128, NCH], F32)
    invn_sb = sb("B_invn", [128, NCH], F32)
    coefi_sb = sb("B_coefi", [128, NCH, 4], BF16)
    wk_g = sb("B_wk", [128, KT, 128], BF16)
    wv_g = sb("B_wv", [128, KT, 128], BF16)
    wmb_g = sb("B_wmb", [128, KT, 512], BF16)
    bk_g = sb("B_bk", [1, 128], BF16)
    bv_g = sb("B_bv", [1, 128], BF16)
    bmb_g = sb("B_bmb", [1, 512], BF16)
    mem_rhs = sb("B_memrhs", [128, 4, 6, 129], BF16)
    normb = sb("B_normb", [1, 4, F], BF16)
    norm_bc = sb("B_normbc", [128, 4, F], BF16)
    xnt = sb("B_xnt", [128, 2, KT, 128], BF16)
    rq = sb("B_rq", [128, 2, 256], BF16)
    dumpB = sb("B_dump", [128, F], BF16)
    sqn = sb("B_sqn", [128, 1], F32)
    den4 = sb("B_den4", [128, 4], F32)
    ne1 = sb("B_ne1", [128, 1], F32)
    d4 = sb("B_d4", [128, 4], F32)
    rec4 = sb("B_rec4", [128, 4], F32)
    nri4 = sb("B_nri4", [128, 4], F32)
    dr4 = sb("B_dr4", [128, 4], F32)
    msq1 = sb("B_msq1", [128, 1], F32)
    rmsq = sb("B_rmsq", [128, 1], F32)
    co4 = sb("B_co4", [128, 4], F32)
    co4b = sb("B_co4b", [128, 4], F32)
    mvs = sb("B_mvs", [128, 512], F32)
    wmv4 = sb("B_wmv4", [128, 2, 512], BF16)
    dbounce = sb("B_dbounce", [128, 2, 512], F32)
    dnbounce = sb("B_dnb", [128, 24], F32)

    import os
    glist = range(int(os.environ.get('KB_GONLY', HKV)))
    for g in glist:
        _emit_Ba(nc, block, sem, cnt, env, locals(), g)
        _emit_Bb(nc, block, sem, cnt, env, locals(), g)

    if env.get("dbg"):
        env["dbg"]["Bt"] = {"fTm": fTm, "v_sb": v_sb, "mb_sb": mb_sb}

    nDD = cnt["sDD"]

    @block.gpsimd
    def _(gpsimd):
        gpsimd.wait_ge(sem["sDD"], nDD)
        gpsimd.collective_compute(
            "AllReduce", ALU.add,
            replica_groups=[list(range(NCORES))],
            ins=[dcc_in.ap().opt()],
            outs=[dcc_out.ap().opt()],
        ).then_inc(sem["sCC"], 1)
    cnt["sCC"] += 1

    if not env.get("dbg"):
        st.close()


def _emit_Ba(nc, block, sem, cnt, env, B, g):
    wkT_h, wvT_h, wmbT_h = env["wkT_h"], env["wvT_h"], env["wmbT_h"]
    bk_h, bv_h, bmb_h = env["bk_h"], env["bv_h"], env["bmb_h"]
    memcat_h, normr_h = env["memcat_h"], env["normr_h"]
    xnT_hbm = env["xnT_hbm"]
    identb, onesb = env["identb"], env["onesb"]
    (fTm, fT, v_sb, mb_sb, s_sb, nmax_sb, invn_sb, wk_g, wv_g, wmb_g, bk_g, bv_g,
     bmb_g, mem_rhs, normb, norm_bc, xnt, rq, dumpB, sqn) = (
        B["fTm"], B["fT"], B["v_sb"], B["mb_sb"], B["s_sb"], B["nmax_sb"], B["invn_sb"],
        B["wk_g"], B["wv_g"], B["wmb_g"], B["bk_g"], B["bv_g"], B["bmb_g"],
        B["mem_rhs"], B["normb"], B["norm_bc"], B["xnt"], B["rq"], B["dumpB"], B["sqn"])
    del B

    st = ExitStack()
    ps = lambda n, s, d: st.enter_context(nc.psum_tensor(n, s, d))
    psK = [ps(f"Ba{g}_k{j}", [128, 128], F32) for j in range(2)]
    psV = [ps(f"Ba{g}_v{j}", [128, 128], F32) for j in range(2)]
    psM = [ps(f"Ba{g}_m{j}", [128, 512], F32) for j in range(2)]
    psT = [ps(f"Ba{g}_t{j}", [128, 128], BF16) for j in range(2)]

    L = {k: cnt[k] for k in cnt}

    @block.gpsimd
    def _(gpsimd):
        if g == 0:
            gpsimd.wait_ge(sem["sVro"], L["sVro"])
        gpsimd.dma_start(out=wk_g[:], in_=wkT_h[:, :, g * 128:(g + 1) * 128].rearrange("k p n -> p k n")).then_inc(sem["sGD"], 16)
        gpsimd.dma_start(out=wv_g[:], in_=wvT_h[:, :, g * 128:(g + 1) * 128].rearrange("k p n -> p k n")).then_inc(sem["sGD"], 16)
        gpsimd.dma_start(out=wmb_g[:], in_=wmbT_h[:, :, g * 512:(g + 1) * 512].rearrange("k p n -> p k n")).then_inc(sem["sGD"], 16)
        gpsimd.dma_start(out=bk_g[:], in_=bk_h[:, g * 128:(g + 1) * 128]).then_inc(sem["sGD"], 16)
        gpsimd.dma_start(out=bv_g[:], in_=bv_h[:, g * 128:(g + 1) * 128]).then_inc(sem["sGD"], 16)
        gpsimd.dma_start(out=bmb_g[:], in_=bmb_h[:, g * 512:(g + 1) * 512]).then_inc(sem["sGD"], 16)
        gpsimd.dma_start(out=mem_rhs[:], in_=memcat_h[4 * g:4 * (g + 1)].rearrange("h (fc p) c -> p h fc c", p=128)).then_inc(sem["sGD"], 16)
        for h in range(4):
            gpsimd.dma_start(out=normb[:, h, :], in_=normr_h[4 * g + h]).then_inc(sem["sGD"], 16)
    cnt["sGD"] += 16 * 11
    nGD = cnt["sGD"]

    @block.sync
    def _(sync):
        for i in range(NCH):
            b = i % 2
            if i >= 2:
                sync.wait_ge(sem["sPB"], L["sPB"] + 9 * (i - 2) + 3)
            else:
                sync.wait_ge(sem["sPB"], L["sPB"])
            if g == 0 and i == 0:
                sync.wait_ge(sem["sXW0"], L["sXW0"])
                sync.wait_ge(sem["sXW1"], L["sXW1"])
            sync.dma_start(out=xnt[:, b, :, :],
                           in_=xnT_hbm[:, :, i * 128:(i + 1) * 128].rearrange("k p t -> p k t")
                           ).then_inc(sem["sXL0" if b == 0 else "sXL1"], 16)
    cnt["sXL0"] += 16 * (NCH // 2)
    cnt["sXL1"] += 16 * (NCH // 2)

    @block.tensor
    def _(tensor):
        tensor.wait_ge(sem["sGD"], nGD)
        tensor.wait_ge(sem["sVy"], L["sVy"])
        tensor.wait_ge(sem["sAp"], L["sAp"])
        tensor.wait_ge(sem["sAe"], L["sAe"])
        for h in range(4):
            tensor.matmul(psM[0][:, 0:512], onesb[:, :], normb[:, h, 0:512], start=True, stop=True)
            tensor.matmul(psM[1][:, 0:256], onesb[:, :], normb[:, h, 512:768], start=True, stop=True).then_inc(sem["sPn"], 1)
            tensor.wait_ge(sem["sAg"], L["sAg"] + h + 1)
        for i in range(NCH):
            b = i % 2
            tensor.wait_ge(sem["sXL0" if b == 0 else "sXL1"], L["sXL0" if b == 0 else "sXL1"] + 16 * (i // 2 + 1))
            if i >= 2:
                tensor.wait_ge(sem["sAB"], L["sAB"] + i - 1)
                tensor.wait_ge(sem["sVv"], L["sVv"] + i - 1)
                tensor.wait_ge(sem["sVm"], L["sVm"] + i - 1)
            for k in range(KT):
                tensor.matmul(psK[b][:], xnt[:, b, k, :], wk_g[:, k, :], start=(k == 0), stop=False)
            tensor.matmul(psK[b][:], onesb[:, :], bk_g[:], start=False, stop=True).then_inc(sem["sPB"], 1)
            for k in range(KT):
                tensor.matmul(psV[b][:], xnt[:, b, k, :], wv_g[:, k, :], start=(k == 0), stop=False)
            tensor.matmul(psV[b][:], onesb[:, :], bv_g[:], start=False, stop=True).then_inc(sem["sPB"], 1)
            for k in range(KT):
                tensor.matmul(psM[b][:], xnt[:, b, k, :], wmb_g[:, k, :], start=(k == 0), stop=False)
            tensor.matmul(psM[b][:], onesb[:, :], bmb_g[:], start=False, stop=True).then_inc(sem["sPB"], 1)
            tensor.wait_ge(sem["sVf"], L["sVf"] + i + 1)
            for j in range(6):
                jt = 6 * i + j
                if jt >= 2:
                    tensor.wait_ge(sem["sAq"], L["sAq"] + jt - 1)
                tensor.transpose(psT[jt % 2][:], fTm[:, i, 128 * j:128 * (j + 1)], identb[:]).then_inc(sem["sPB"], 1)
    cnt["sPn"] += 4
    cnt["sPB"] += 9 * NCH

    @block.scalar
    def _(scalar):
        for h in range(4):
            scalar.wait_ge(sem["sPn"], L["sPn"] + h + 1)
            scalar.copy(norm_bc[:, h, 0:512], psM[0][:, 0:512])
            scalar.copy(norm_bc[:, h, 512:768], psM[1][:, 0:256]).then_inc(sem["sAg"], 1)
        for i in range(NCH):
            b = i % 2
            scalar.wait_ge(sem["sPB"], L["sPB"] + 9 * i + 1)
            if i >= 2:
                scalar.wait_ge(sem["sVf"], L["sVf"] + i - 1)
            scalar.activation(rq[:, b, 0:128], psK[b][:], AF.Relu)
            scalar.activation(rq[:, b, 128:256], psK[b][:], AF.Relu, bias=0.0, scale=-1.0).then_inc(sem["sAB"], 1)
            scalar.wait_ge(sem["sVq"], L["sVq"] + i + 1)
            scalar.activation(sqn[:], s_sb[:, i:i + 1], AF.Sqrt).then_inc(sem["sAr"], 1)
            scalar.wait_ge(sem["sPB"], L["sPB"] + 9 * i + 2)
            scalar.copy(v_sb[:, i, :], psV[b][:]).then_inc(sem["sVv"], 1)
            scalar.wait_ge(sem["sPB"], L["sPB"] + 9 * i + 3)
            scalar.activation(mb_sb[:, i, :], psM[b][:], AF.Sigmoid).then_inc(sem["sVm"], 1)
            for j in range(6):
                jt = 6 * i + j
                scalar.wait_ge(sem["sPB"], L["sPB"] + 9 * i + 4 + j)
                scalar.copy(fT[:, j, i * 128:(i + 1) * 128], psT[jt % 2][:]).then_inc(sem["sAq"], 1)
    cnt["sAg"] += 4
    cnt["sAB"] += NCH
    cnt["sAr"] += NCH
    cnt["sAq"] += 6 * NCH
    cnt["sVm"] += NCH
    cnt["sVv"] += NCH

    vs = [L["sVs"]]

    @block.vector
    def _(vector):
        def vedge(res):
            vs[0] += 1
            res.then_inc(sem["sVs"], 1)
            vector.wait_ge(sem["sVs"], vs[0])

        for i in range(NCH):
            b = i % 2
            vector.wait_ge(sem["sAB"], L["sAB"] + i + 1)
            for j in (1, 2, 3):
                base = 256 * (j - 1)
                vector.tensor_tensor(fTm[:, i, base:base + j], rq[:, b, 0:j], rq[:, b, 256 - j:256], ALU.mult)
                res = vector.tensor_tensor(fTm[:, i, base + j:base + 256], rq[:, b, j:256], rq[:, b, 0:256 - j], ALU.mult)
            res.then_inc(sem["sVf"], 1)
            vector.wait_ge(sem["sVf"], L["sVf"] + i + 1)
            vedge(vector.scalar_tensor_tensor(dumpB[:], fTm[:, i, :], 1.0, fTm[:, i, :],
                                              ALU.mult, ALU.mult, accum_out=s_sb[:, i:i + 1]))
            vector.nop().then_inc(sem["sVq"], 1)
            vector.wait_ge(sem["sAr"], L["sAr"] + i + 1)
            vedge(vector.tensor_scalar(nmax_sb[:, i:i + 1], sqn[:], 1e-12, None, ALU.max))
            vedge(vector.reciprocal(invn_sb[:, i:i + 1], nmax_sb[:, i:i + 1]))

    cnt["sVs"] = vs[0]
    cnt["sVf"] += NCH
    cnt["sVq"] += NCH

    st.close()


def _emit_Bb(nc, block, sem, cnt, env, B, g):
    dbg_bb = env.get("dbg", {}).get("bb_sb")
    dcc_in = env["dcc_in"]
    (fTm, fT, v_sb, mb_sb, s_sb, nmax_sb, invn_sb, coefi_sb, mem_rhs, norm_bc,
     den4, ne1, d4, rec4, nri4, dr4, msq1, rmsq, co4, co4b, mvs, wmv4, dbounce,
     dnbounce, dumpB) = (
        B["fTm"], B["fT"], B["v_sb"], B["mb_sb"], B["s_sb"], B["nmax_sb"], B["invn_sb"],
        B["coefi_sb"], B["mem_rhs"], B["norm_bc"], B["den4"], B["ne1"], B["d4"],
        B["rec4"], B["nri4"], B["dr4"], B["msq1"], B["rmsq"], B["co4"], B["co4b"],
        B["mvs"], B["wmv4"], B["dbounce"], B["dnbounce"], B["dumpB"])

    st = ExitStack()
    ps = lambda n, s, d: st.enter_context(nc.psum_tensor(n, s, d))
    psD = [ps(f"Bb{g}_d{j}", [128, 512], F32) for j in range(6)]
    psN = [ps(f"Bb{g}_n{j}", [128, 512], F32) for j in range(2)]

    L = {k: cnt[k] for k in cnt}

    @block.tensor
    def _(tensor):
        tensor.wait_ge(sem["sAq"], L["sAq"])
        tensor.wait_ge(sem["sVv"], L["sVv"])
        tensor.wait_ge(sem["sVm"], L["sVm"])
        tensor.wait_ge(sem["sAB"], L["sAB"])
        for i in range(NCH):
            b = i % 2
            if i >= 2:
                tensor.wait_ge(sem["sVw"], L["sVw"] + i - 1)
            for fc in range(6):
                mm = tensor.matmul(psN[b][:, 0:512],
                                   fT[:, fc, i * 128:(i + 1) * 128],
                                   mem_rhs[:, 0:4, fc, 0:128],
                                   start=(fc == 0), stop=(fc == 5))
            mm.then_inc(sem["sPN"], 1)
            tensor.wait_ge(sem["sVw"], L["sVw"] + i + 1)
            for fc in range(6):
                mm = tensor.matmul(psD[fc][:], fTm[:, i, 128 * fc:128 * (fc + 1)], wmv4[:, b, :],
                                   start=(i == 0), stop=(i == NCH - 1))
            mm.then_inc(sem["sPd"], 1)
        tensor.wait_ge(sem["sVw"], L["sVw"] + NCH)
        tensor.wait_ge(sem["sVy"], L["sVy"] + 6)
        for fc in range(6):
            for i in range(NCH):
                mm = tensor.matmul(psN[0][:, 4 * fc:4 * (fc + 1)], fTm[:, i, 128 * fc:128 * (fc + 1)],
                                   coefi_sb[:, i, :], start=(i == 0), stop=(i == NCH - 1))
        mm.then_inc(sem["sPd"], 1)
    cnt["sPN"] += NCH
    cnt["sPd"] += NCH + 1

    vs = [L["sVs"]]

    @block.vector
    def _(vector):
        def vedge(res):
            vs[0] += 1
            res.then_inc(sem["sVs"], 1)
            vector.wait_ge(sem["sVs"], vs[0])

        for i in range(NCH):
            b = i % 2
            for h in range(4):
                res = vector.scalar_tensor_tensor(dumpB[:], fTm[:, i, :], 1.0, norm_bc[:, h, :],
                                                  ALU.mult, ALU.mult, accum_out=den4[:, h:h + 1])
            vedge(res)
            vedge(vector.tensor_scalar(ne1[:], nmax_sb[:, i:i + 1], EPS, None, ALU.mult))
            vedge(vector.tensor_scalar(d4[:], den4[:], ne1[:], None, ALU.add))
            vedge(vector.reciprocal(rec4[:], d4[:]))
            vedge(vector.tensor_scalar(nri4[:], rec4[:], -1.0, None, ALU.mult))
            vedge(vector.tensor_scalar(dr4[:], den4[:], invn_sb[:, i:i + 1], EPS, ALU.mult, ALU.add))
            vedge(vector.tensor_scalar(msq1[:], s_sb[:, i:i + 1], invn_sb[:, i:i + 1], invn_sb[:, i:i + 1], ALU.mult, ALU.mult))
            vedge(vector.reciprocal(rmsq[:], msq1[:]))
            vedge(vector.tensor_scalar(co4[:], dr4[:], rmsq[:], -1.0, ALU.mult, ALU.mult))
            vedge(vector.tensor_scalar(co4b[:], co4[:], 1.0, 0.0, ALU.add, ALU.max))
            vedge(vector.tensor_scalar(coefi_sb[:, i, :], co4b[:], 1.0, invn_sb[:, i:i + 1], ALU.min, ALU.mult))
            vector.wait_ge(sem["sPN"], L["sPN"] + i + 1)
            if i >= 2:
                vector.wait_ge(sem["sPd"], L["sPd"] + i - 1)
            for h in range(4):
                vector.scalar_tensor_tensor(mvs[:, 128 * h:128 * (h + 1)], psN[b][:, 128 * h:128 * (h + 1)],
                                            nri4[:, h:h + 1], v_sb[:, i, :], ALU.mult, ALU.add)
            for h in range(4):
                res = vector.scalar_tensor_tensor(wmv4[:, b, 128 * h:128 * (h + 1)], mvs[:, 128 * h:128 * (h + 1)],
                                                  invn_sb[:, i:i + 1], mb_sb[:, i, 128 * h:128 * (h + 1)],
                                                  ALU.mult, ALU.mult)
            if dbg_bb is not None and g == 3 and i == 5:
                vector.tensor_copy(dbg_bb[:, 0:512], psN[b][:, 0:512])
                vector.tensor_copy(dbg_bb[:, 512:516], den4[:])
                vector.tensor_copy(dbg_bb[:, 516:520], d4[:])
                vector.tensor_copy(dbg_bb[:, 520:524], rec4[:])
                vector.tensor_copy(dbg_bb[:, 524:528], nri4[:])
                vector.tensor_copy(dbg_bb[:, 528:1040], wmv4[:, b, :])
                vector.tensor_copy(dbg_bb[:, 1040:1041], invn_sb[:, i:i + 1])
                vector.tensor_copy(dbg_bb[:, 1041:1042], nmax_sb[:, i:i + 1])
                vector.tensor_copy(dbg_bb[:, 1042:1043], s_sb[:, i:i + 1])
            res.then_inc(sem["sVw"], 1)
        vector.wait_ge(sem["sPd"], L["sPd"] + NCH)
        for fc in range(6):
            if fc >= 2:
                vector.wait_ge(sem["sDD"], L["sDD"] + 16 * (fc - 1))
            else:
                vector.wait_ge(sem["sDD"], L["sDD"])
            vector.tensor_copy(dbounce[:, fc % 2, :], psD[fc][:]).then_inc(sem["sVy"], 1)
        vector.wait_ge(sem["sPd"], L["sPd"] + NCH + 1)
        vector.tensor_copy(dnbounce[:], psN[0][:, 0:24]).then_inc(sem["sVy"], 1)
    cnt["sVs"] = vs[0]
    cnt["sVw"] += NCH
    cnt["sVy"] += 7

    @block.gpsimd
    def _(gpsimd):
        for fc in range(6):
            gpsimd.wait_ge(sem["sVy"], L["sVy"] + fc + 1)
            gpsimd.dma_start(out=dcc_in[:, (g * 6 + fc) * 512:(g * 6 + fc + 1) * 512],
                             in_=dbounce[:, fc % 2, :]).then_inc(sem["sDD"], 16)
        gpsimd.wait_ge(sem["sVy"], L["sVy"] + 7)
        gpsimd.dma_start(out=dcc_in[:, 12288 + g * 24:12288 + (g + 1) * 24],
                         in_=dnbounce[:]).then_inc(sem["sDD"], 16)
    cnt["sDD"] += 16 * 7

    st.close()


def emit_phase_D(nc, block, sem, cnt, env):
    xnT_hbm, out_h = env["xnT_hbm"], env["out_h"]
    wgu_h, wdn_h = env["wgu_h"], env["wdn_h"]
    combine = env["combine"]

    st = ExitStack()
    sb = lambda n, s, d: st.enter_context(nc.sbuf_tensor(n, s, d))
    ps = lambda n, s, d: st.enter_context(nc.psum_tensor(n, s, d))

    wgu_sb = sb("D_wgu", [128, KT, I2], BF16)
    wdn_sb = sb("D_wdn", [128, 2, 8, DM], BF16)
    xnt = sb("D_xnt", [128, 2, KT, 512], BF16)
    inter = sb("D_inter", [128, 2, 8, 512], BF16)
    silu_b = sb("D_silu", [128, 2, 512], F32)
    ystage = sb("D_ys", [128, 2, DM], F32)
    psG = [ps(f"D_g{j}", [128, 512], F32) for j in range(2)]
    psU = [ps(f"D_u{j}", [128, 512], F32) for j in range(2)]
    psY = [ps(f"D_y{j}", [128, 512], F32) for j in range(2)]

    import os
    NE_D = int(os.environ.get("KD_EONLY", NE))
    SER = os.environ.get("KD_SERIAL", "")
    L = {k: cnt[k] for k in cnt}
    NTB = T // 512
    NJ = 8
    nTB = NE_D * NTB

    def load_wgu(gpsimd, e):
        for q in range(4):
            gpsimd.dma_start(out=wgu_sb[:, 4 * q:4 * (q + 1), :],
                             in_=wgu_h[e, 512 * q:512 * (q + 1), :].rearrange("(k p) m -> p k m", p=128)).then_inc(sem["sGu"], 16)

    def load_wdn(gpsimd, e):
        for q in range(2):
            gpsimd.dma_start(out=wdn_sb[:, e % 2, 4 * q:4 * (q + 1), :],
                             in_=wdn_h[e, 512 * q:512 * (q + 1), :].rearrange("(k p) m -> p k m", p=128)).then_inc(sem["sGw"], 16)

    @block.gpsimd
    def _(gpsimd):
        gpsimd.wait_ge(sem["sPd"], L["sPd"])
        load_wgu(gpsimd, 0)
        load_wdn(gpsimd, 0)
        for e in range(NE_D):
            for tb in range(NTB):
                for tc in range(4):
                    tcg = e * 32 + tb * 4 + tc
                    ch = tb * 4 + tc
                    b2 = tcg % 2
                    gpsimd.wait_ge(sem["sVy"], L["sVy"] + 4 * (tcg + 1))
                    gpsimd.dma_start(out=out_h[ch * 128:(ch + 1) * 128, :], in_=ystage[:, b2, :],
                                     accum_op=ALU.add).then_inc(sem["sOUT0" if b2 == 0 else "sOUT1"], 16)
                    if tb * 4 + tc == 27 and e + 1 < NE_D:
                        gpsimd.wait_ge(sem["sPg"], L["sPg"] + 16 * NTB * (e + 1))
                        load_wgu(gpsimd, e + 1)
                        gpsimd.wait_ge(sem["sPy"], L["sPy"] + 16 * NTB * e)
                        load_wdn(gpsimd, e + 1)
    cnt["sGu"] += 16 * 4 * NE_D
    cnt["sGw"] += 16 * 2 * NE_D
    n_out_d = NE_D * 32
    cnt["sOUT0"] += 16 * (n_out_d // 2)
    cnt["sOUT1"] += 16 * (n_out_d // 2)

    @block.sync
    def _(sync):
        for e in range(NE_D):
            for tb in range(NTB):
                tbg = e * NTB + tb
                b = tbg % 2
                if tbg >= 2:
                    sync.wait_ge(sem["sPg"], L["sPg"] + 16 * (tbg - 1))
                if tbg == 0:
                    sync.wait_ge(sem["sXW0"], L["sXW0"])
                    sync.wait_ge(sem["sXW1"], L["sXW1"])
                sync.dma_start(out=xnt[:, b, :, :],
                               in_=xnT_hbm[:, :, tb * 512:(tb + 1) * 512].rearrange("k p t -> p k t")
                               ).then_inc(sem["sXL0" if b == 0 else "sXL1"], 16)
    cnt["sXL0"] += 16 * (nTB // 2)
    cnt["sXL1"] += 16 * (nTB // 2)

    @block.tensor
    def _(tensor):
        def emit_down(tbg):
            ib = tbg % 2
            for tc in range(4):
                tcg = tbg * 4 + tc
                for nb in range(4):
                    yt = tcg * 4 + nb
                    yb = yt % 2
                    if yt >= 2:
                        tensor.wait_ge(sem["sVy"], L["sVy"] + yt - 1)
                    for kc in range(8):
                        if tc == 0 and nb == 0:
                            tensor.wait_ge(sem["sVi"], L["sVi"] + tbg * NJ + kc + 1)
                        mm = tensor.matmul(psY[yb][:], inter[:, ib, kc, 128 * tc:128 * (tc + 1)],
                                           wdn_sb[:, (tbg // NTB) % 2, kc, 512 * nb:512 * (nb + 1)],
                                           start=(kc == 0), stop=(kc == 7))
                    mm.then_inc(sem["sPy"], 1)

        tensor.wait_ge(sem["sVy"], L["sVy"])
        for e in range(NE_D):
            tensor.wait_ge(sem["sGu"], L["sGu"] + 64 * (e + 1))
            for tb in range(NTB):
                tbg = e * NTB + tb
                b = tbg % 2
                tensor.wait_ge(sem["sXL0" if b == 0 else "sXL1"],
                               L["sXL0" if b == 0 else "sXL1"] + 16 * (tbg // 2 + 1))
                for j in range(NJ):
                    jg = tbg * NJ + j
                    jb = jg % 2
                    if jg >= 2:
                        tensor.wait_ge(sem["sVi"], L["sVi"] + jg - 1)
                    for kc in range(KT):
                        mm = tensor.matmul(psG[jb][:], wgu_sb[:, kc, 256 * j:256 * j + 128], xnt[:, b, kc, :],
                                           start=(kc == 0), stop=(kc == KT - 1))
                    mm.then_inc(sem["sPg"], 1)
                    for kc in range(KT):
                        mm = tensor.matmul(psU[jb][:], wgu_sb[:, kc, 256 * j + 128:256 * (j + 1)], xnt[:, b, kc, :],
                                           start=(kc == 0), stop=(kc == KT - 1))
                    mm.then_inc(sem["sPg"], 1)
                if tbg >= 1:
                    if (tbg - 1) % NTB == 0:
                        tensor.wait_ge(sem["sGw"], L["sGw"] + 32 * ((tbg - 1) // NTB + 1))
                    emit_down(tbg - 1)
        tensor.wait_ge(sem["sGw"], L["sGw"] + 32 * NE_D)
        emit_down(NE_D * NTB - 1)
    cnt["sPg"] += 2 * NJ * nTB
    cnt["sPy"] += 16 * nTB

    @block.scalar
    def _(scalar):
        for e in range(NE_D):
            for tb in range(NTB):
                for j in range(NJ):
                    jg = (e * NTB + tb) * NJ + j
                    jb = jg % 2
                    scalar.wait_ge(sem["sPg"], L["sPg"] + 2 * jg + 1)
                    if jg >= 2:
                        scalar.wait_ge(sem["sVi"], L["sVi"] + jg - 1)
                    scalar.activation(silu_b[:, jb, :], psG[jb][:], AF.Silu).then_inc(sem["sAs"], 1)
    cnt["sAs"] += NJ * nTB

    @block.vector
    def _(vector):
        def emit_ystage(tbg):
            e2, tb2 = tbg // NTB, tbg % NTB
            for tc in range(4):
                tcg = tbg * 4 + tc
                ch = tb2 * 4 + tc
                b2 = tcg % 2
                if tcg >= 2:
                    vector.wait_ge(sem["sOUT0" if b2 == 0 else "sOUT1"],
                                   L["sOUT0" if b2 == 0 else "sOUT1"] + 16 * (tcg // 2))
                for nb in range(4):
                    yt = tcg * 4 + nb
                    yb = yt % 2
                    vector.wait_ge(sem["sPy"], L["sPy"] + yt + 1)
                    vector.tensor_scalar(ystage[:, b2, 512 * nb:512 * (nb + 1)], psY[yb][:],
                                         combine[:, ch, e2:e2 + 1], None, ALU.mult).then_inc(sem["sVy"], 1)

        for e in range(NE_D):
            for tb in range(NTB):
                tbg = e * NTB + tb
                for j in range(NJ):
                    jg = tbg * NJ + j
                    jb = jg % 2
                    vector.wait_ge(sem["sAs"], L["sAs"] + jg + 1)
                    vector.wait_ge(sem["sPg"], L["sPg"] + 2 * jg + 2)
                    if tbg >= 2 and j == 0:
                        vector.wait_ge(sem["sPy"], L["sPy"] + 16 * (tbg - 1))
                    vector.tensor_tensor(inter[:, tbg % 2, j, :], silu_b[:, jb, :], psU[jb][:], ALU.mult).then_inc(sem["sVi"], 1)
                if tbg >= 1:
                    emit_ystage(tbg - 1)
        emit_ystage(NE_D * NTB - 1)
    cnt["sVi"] += NJ * nTB
    cnt["sVy"] += 16 * nTB

    st.close()


def emit_phase_C(nc, block, sem, cnt, env):
    xnT_hbm, out_h, mh_hbm = env["xnT_hbm"], env["out_h"], env["mh_hbm"]
    wqT_h, wgT_h, bq_h, bg_h = env["wqT_h"], env["wgT_h"], env["bq_h"], env["bg_h"]
    memcat_h, dcc_out = env["memcat_h"], env["dcc_out"]
    identb, onesb = env["identb"], env["onesb"]

    st = ExitStack()
    sb = lambda n, s, d: st.enter_context(nc.sbuf_tensor(n, s, d))

    new_mem = sb("C_newmem", [128, H, 6, 129], BF16)
    mpc = sb("C_mpc", [128, 2, 4, 129], F32)
    dpc = sb("C_dpc", [128, 2, 512], F32)
    npc = sb("C_npc", [128, 2, 4], F32)
    if env.get("dbg"):
        env["dbg"]["Ct"] = {"new_mem": new_mem}

    L = {k: cnt[k] for k in cnt}

    @block.sync
    def _(sync):
        sync.wait_ge(sem["sCC"], L["sCC"])
        for t in range(24):
            g, fc = t // 6, t % 6
            b = t % 2
            if t >= 2:
                sync.wait_ge(sem["sVm"], L["sVm"] + t - 1)
            sync.dma_start(out=mpc[:, b, :, :],
                           in_=memcat_h[4 * g:4 * (g + 1), 128 * fc:128 * (fc + 1), :].rearrange("h p c -> p h c")
                           ).then_inc(sem["sXL0" if b == 0 else "sXL1"], 16)
            sync.dma_start(out=dpc[:, b, :],
                           in_=dcc_out[:, (g * 6 + fc) * 512:(g * 6 + fc + 1) * 512]).then_inc(sem["sXL0" if b == 0 else "sXL1"], 16)
            sync.dma_start(out=npc[:, b, :],
                           in_=dcc_out[:, 12288 + g * 24 + 4 * fc:12288 + g * 24 + 4 * (fc + 1)]).then_inc(sem["sXL0" if b == 0 else "sXL1"], 16)
    cnt["sXL0"] += 16 * 36
    cnt["sXL1"] += 16 * 36

    @block.vector
    def _(vector):
        for t in range(24):
            g, fc = t // 6, t % 6
            b = t % 2
            s_n = "sXL0" if b == 0 else "sXL1"
            vector.wait_ge(sem[s_n], L[s_n] + 48 * (t // 2 + 1))
            vector.tensor_tensor(new_mem[:, 4 * g:4 * (g + 1), fc, 0:128],
                                 mpc[:, b, :, 0:128],
                                 dpc[:, b, :].rearrange("p (h d) -> p h d", d=128), ALU.add)
            vector.tensor_tensor(new_mem[:, 4 * g:4 * (g + 1), fc, 128:129],
                                 mpc[:, b, :, 128:129],
                                 npc[:, b, :].rearrange("p (h one) -> p h one", one=1), ALU.add).then_inc(sem["sVm"], 1)
    cnt["sVm"] += 24

    _emit_C1(nc, block, sem, cnt, env, locals())
    _emit_C2(nc, block, sem, cnt, env, locals())

    st.close()


def _emit_C1(nc, block, sem, cnt, env, C):
    xnT_hbm, mh_hbm = env["xnT_hbm"], env["mh_hbm"]
    wqT_h, bq_h = env["wqT_h"], env["bq_h"]
    identb, onesb = env["identb"], env["onesb"]
    new_mem = C["new_mem"]

    st = ExitStack()
    sb = lambda n, s, d: st.enter_context(nc.sbuf_tensor(n, s, d))
    ps = lambda n, s, d: st.enter_context(nc.psum_tensor(n, s, d))
    wq = sb("C1_wq", [128, KT, DM], BF16)
    bqb = sb("C1_bq", [1, DM], BF16)
    xnt = sb("C1_xnt", [128, 2, KT, 128], BF16)
    qf = sb("C1_qf", [128, 2, DM], F32)
    rqC = sb("C1_rq", [128, 8, 256], BF16)
    fq8 = sb("C1_fq", [128, 8, 768], BF16)
    fqT = sb("C1_fqT", [128, 2, 6, 128], BF16)
    dumpC = sb("C1_dump", [128, F], BF16)
    sq8 = sb("C1_sq", [128, 8], F32)
    sqr8 = sb("C1_sqr", [128, 8], F32)
    ne8 = sb("C1_ne", [128, 8], F32)
    den8 = sb("C1_den", [128, 8], F32)
    d8 = sb("C1_d8", [128, 8], F32)
    rc8 = sb("C1_rc", [128, 8], F32)
    mh = sb("C1_mh", [128, 2, DM], F32)
    psQ = [ps(f"C1_q{j}", [128, 512], F32) for j in range(2)]
    psT = [ps(f"C1_t{j}", [128, 128], BF16) for j in range(2)]
    psN2 = [ps(f"C1_n{j}", [128, 129], F32) for j in range(2)]

    L = {k: cnt[k] for k in cnt}

    @block.gpsimd
    def _(gpsimd):
        gpsimd.wait_ge(sem["sPg"], L["sPg"])
        gpsimd.wait_ge(sem["sPy"], L["sPy"])
        for q in range(4):
            gpsimd.dma_start(out=wq[:, 4 * q:4 * (q + 1), :],
                             in_=wqT_h[4 * q:4 * (q + 1)].rearrange("k p m -> p k m")).then_inc(sem["sGD"], 16)
        gpsimd.dma_start(out=bqb[:], in_=bq_h).then_inc(sem["sGD"], 16)
    cnt["sGD"] += 16 * 5
    nGD = cnt["sGD"]

    @block.sync
    def _(sync):
        for i in range(NCH):
            b = i % 2
            if i == 0:
                sync.wait_ge(sem["sPg"], L["sPg"])
                sync.wait_ge(sem["sPy"], L["sPy"])
            if i >= 2:
                sync.wait_ge(sem["sPq"], L["sPq"] + 4 * (i - 1))
            sync.dma_start(out=xnt[:, b, :, :],
                           in_=xnT_hbm[:, :, i * 128:(i + 1) * 128].rearrange("k p t -> p k t")
                           ).then_inc(sem["sXL0" if b == 0 else "sXL1"], 16)
    cnt["sXL0"] += 16 * (NCH // 2)
    cnt["sXL1"] += 16 * (NCH // 2)

    @block.tensor
    def _(tensor):
        tensor.wait_ge(sem["sGD"], nGD)
        tensor.wait_ge(sem["sVy"], L["sVy"])
        tensor.wait_ge(sem["sAs"], L["sAs"])
        tensor.wait_ge(sem["sVi"], L["sVi"])
        tensor.wait_ge(sem["sVm"], L["sVm"])
        for i in range(NCH):
            b = i % 2
            s_n = "sXL0" if b == 0 else "sXL1"
            tensor.wait_ge(sem[s_n], L[s_n] + 16 * (i // 2 + 1))
            for nq in range(4):
                if 4 * i + nq >= 2:
                    tensor.wait_ge(sem["sAq"], L["sAq"] + 4 * i + nq - 1)
                for kc in range(KT):
                    tensor.matmul(psQ[nq % 2][:], xnt[:, b, kc, :], wq[:, kc, 512 * nq:512 * (nq + 1)],
                                  start=(kc == 0), stop=False)
                tensor.matmul(psQ[nq % 2][:], onesb[:, :], bqb[:, 512 * nq:512 * (nq + 1)],
                              start=False, stop=True).then_inc(sem["sPq"], 1)
            for hg in range(2):
                hgg = 2 * i + hg
                tensor.wait_ge(sem["sVf"], L["sVf"] + hgg + 1)
                for s in range(9):
                    if s < 8:
                        hh = s
                        for j in range(6):
                            jt = hgg * 48 + 6 * hh + j
                            if jt >= 2:
                                tensor.wait_ge(sem["sAt"], L["sAt"] + jt - 1)
                            tensor.transpose(psT[jt % 2][:], fq8[:, hh, 128 * j:128 * (j + 1)], identb[:]).then_inc(sem["sPt"], 1)
                    if s >= 1:
                        hh = s - 1
                        h = 8 * hg + hh
                        hb = hh % 2
                        tensor.wait_ge(sem["sAt"], L["sAt"] + hgg * 48 + 6 * (hh + 1))
                        if 16 * i + 8 * hg + hh >= 2:
                            tensor.wait_ge(sem["sVn"], L["sVn"] + 16 * i + 8 * hg + hh - 1)
                        for fc in range(6):
                            mm = tensor.matmul(psN2[hh % 2][:], fqT[:, hb, fc, :], new_mem[:, h, fc, :],
                                               start=(fc == 0), stop=(fc == 5))
                        mm.then_inc(sem["sPn"], 1)
    cnt["sPq"] += 4 * NCH
    cnt["sPt"] += 96 * NCH
    cnt["sPn"] += 16 * NCH

    @block.scalar
    def _(scalar):
        for i in range(NCH):
            b = i % 2
            for nq in range(4):
                scalar.wait_ge(sem["sPq"], L["sPq"] + 4 * i + nq + 1)
                if i >= 2 and nq == 0:
                    scalar.wait_ge(sem["sVf"], L["sVf"] + 2 * i - 2)
                scalar.copy(qf[:, b, 512 * nq:512 * (nq + 1)], psQ[nq % 2][:]).then_inc(sem["sAq"], 1)
            if i >= 1:
                bp = (i - 1) % 2
                scalar.wait_ge(sem["sVw"], L["sVw"] + 2 * i)
                scalar.dma_start(out=mh_hbm[128 * (i - 1):128 * i, :], in_=mh[:, bp, :]).then_inc(sem["sDD"], 16)
            for hg in range(2):
                hgg = 2 * i + hg
                if hgg >= 1:
                    scalar.wait_ge(sem["sVf"], L["sVf"] + hgg)
                scalar.activation(rqC[:, :, 0:128],
                                  qf[:, b, 1024 * hg:1024 * (hg + 1)].rearrange("p (h d) -> p h d", d=128), AF.Relu)
                scalar.activation(rqC[:, :, 128:256],
                                  qf[:, b, 1024 * hg:1024 * (hg + 1)].rearrange("p (h d) -> p h d", d=128),
                                  AF.Relu, bias=0.0, scale=-1.0).then_inc(sem["sAg"], 1)
                scalar.wait_ge(sem["sVq"], L["sVq"] + hgg + 1)
                scalar.activation(sqr8[:], sq8[:], AF.Sqrt).then_inc(sem["sAr"], 1)
                for hh in range(8):
                    for j in range(6):
                        jt = hgg * 48 + 6 * hh + j
                        scalar.wait_ge(sem["sPt"], L["sPt"] + jt + 1)
                        if jt >= 12:
                            scalar.wait_ge(sem["sPn"], L["sPn"] + (jt - 6) // 6)
                        scalar.copy(fqT[:, hh % 2, j, :], psT[jt % 2][:]).then_inc(sem["sAt"], 1)
    cnt["sAq"] += 4 * NCH
    cnt["sAg"] += 2 * NCH
    cnt["sAr"] += 2 * NCH
    cnt["sAt"] += 96 * NCH

    @block.scalar
    def _(scalar):
        bp = (NCH - 1) % 2
        scalar.wait_ge(sem["sVw"], L["sVw"] + 2 * NCH)
        scalar.dma_start(out=mh_hbm[128 * (NCH - 1):128 * NCH, :], in_=mh[:, bp, :]).then_inc(sem["sDD"], 16)
    cnt["sDD"] += 16 * NCH

    vs = [L["sVs"]]

    @block.vector
    def _(vector):
        def vedge(res):
            vs[0] += 1
            res.then_inc(sem["sVs"], 1)
            vector.wait_ge(sem["sVs"], vs[0])

        for i in range(NCH):
            b = i % 2
            if i >= 2:
                vector.wait_ge(sem["sDD"], L["sDD"] + 16 * (i - 1))
            for hg in range(2):
                hgg = 2 * i + hg
                vector.wait_ge(sem["sAg"], L["sAg"] + hgg + 1)
                if hgg >= 1:
                    vector.wait_ge(sem["sPt"], L["sPt"] + 48 * hgg)
                for j in (1, 2, 3):
                    base = 256 * (j - 1)
                    vector.tensor_tensor(fq8[:, :, base:base + j], rqC[:, :, 0:j], rqC[:, :, 256 - j:256], ALU.mult)
                    res = vector.tensor_tensor(fq8[:, :, base + j:base + 256], rqC[:, :, j:256], rqC[:, :, 0:256 - j], ALU.mult)
                res.then_inc(sem["sVf"], 1)
                vector.wait_ge(sem["sVf"], L["sVf"] + hgg + 1)
                for hh in range(8):
                    res = vector.scalar_tensor_tensor(dumpC[:], fq8[:, hh, :], 1.0, fq8[:, hh, :],
                                                      ALU.mult, ALU.mult, accum_out=sq8[:, hh:hh + 1])
                vedge(res)
                vector.nop().then_inc(sem["sVq"], 1)
                vector.wait_ge(sem["sAr"], L["sAr"] + hgg + 1)
                vedge(vector.tensor_scalar(ne8[:], sqr8[:], 1e-12, 1e-8, ALU.max, ALU.mult))
                for hh in range(8):
                    vector.wait_ge(sem["sPn"], L["sPn"] + 16 * i + 8 * hg + hh + 1)
                    vector.tensor_copy(mh[:, b, 1024 * hg + 128 * hh:1024 * hg + 128 * (hh + 1)],
                                       psN2[hh % 2][:, 0:128])
                    res = vector.tensor_copy(den8[:, hh:hh + 1], psN2[hh % 2][:, 128:129])
                    res.then_inc(sem["sVn"], 1)
                vector.wait_ge(sem["sVn"], L["sVn"] + 16 * i + 8 * hg + 8)
                vedge(vector.tensor_tensor(d8[:], den8[:], ne8[:], ALU.add))
                vedge(vector.reciprocal(rc8[:], d8[:]))
                for hh in range(8):
                    res = vector.tensor_scalar(mh[:, b, 1024 * hg + 128 * hh:1024 * hg + 128 * (hh + 1)],
                                               mh[:, b, 1024 * hg + 128 * hh:1024 * hg + 128 * (hh + 1)],
                                               rc8[:, hh:hh + 1], None, ALU.mult)
                res.then_inc(sem["sVw"], 1)
    cnt["sVf"] += 2 * NCH
    cnt["sVq"] += 2 * NCH
    cnt["sVn"] += 16 * NCH
    cnt["sVw"] += 2 * NCH
    cnt["sVs"] = vs[0]

    st.close()


def _emit_C2(nc, block, sem, cnt, env, C):
    xnT_hbm, out_h, mh_hbm = env["xnT_hbm"], env["out_h"], env["mh_hbm"]
    wgT_h, bg_h = env["wgT_h"], env["bg_h"]
    onesb = env["onesb"]

    st = ExitStack()
    sb = lambda n, s, d: st.enter_context(nc.sbuf_tensor(n, s, d))
    ps = lambda n, s, d: st.enter_context(nc.psum_tensor(n, s, d))
    wg = sb("C2_wg", [128, KT, DM], BF16)
    bgb = sb("C2_bg", [1, DM], BF16)
    xnt = sb("C2_xnt", [128, 2, KT, 128], BF16)
    mhs = sb("C2_mh", [128, 2, DM], F32)
    gatef = sb("C2_gate", [128, 2, DM], F32)
    ctb = sb("C2_ctb", [128, 2, DM], F32)
    psG2 = [ps(f"C2_g{j}", [128, 512], F32) for j in range(2)]

    L = {k: cnt[k] for k in cnt}

    @block.gpsimd
    def _(gpsimd):
        gpsimd.wait_ge(sem["sPq"], L["sPq"])
        gpsimd.wait_ge(sem["sPn"], L["sPn"])
        for q in range(4):
            gpsimd.dma_start(out=wg[:, 4 * q:4 * (q + 1), :],
                             in_=wgT_h[4 * q:4 * (q + 1)].rearrange("k p m -> p k m")).then_inc(sem["sGD"], 16)
        gpsimd.dma_start(out=bgb[:], in_=bg_h).then_inc(sem["sGD"], 16)
        for i in range(NCH):
            b = i % 2
            gpsimd.wait_ge(sem["sVi"], L["sVi"] + i + 1)
            gpsimd.dma_start(out=out_h[128 * i:128 * (i + 1), :], in_=ctb[:, b, :],
                             accum_op=ALU.add).then_inc(sem["sOUT0" if b == 0 else "sOUT1"], 16)
    cnt["sGD"] += 16 * 5
    nGD = cnt["sGD"]
    cnt["sOUT0"] += 16 * (NCH // 2)
    cnt["sOUT1"] += 16 * (NCH // 2)

    @block.sync
    def _(sync):
        for i in range(NCH):
            b = i % 2
            if i == 0:
                sync.wait_ge(sem["sDD"], L["sDD"])
            if i >= 2:
                sync.wait_ge(sem["sPq"], L["sPq"] + 4 * (i - 1))
            sync.dma_start(out=xnt[:, b, :, :],
                           in_=xnT_hbm[:, :, i * 128:(i + 1) * 128].rearrange("k p t -> p k t")
                           ).then_inc(sem["sXL0" if b == 0 else "sXL1"], 16)
            if i >= 2:
                sync.wait_ge(sem["sVi"], L["sVi"] + i - 1)
            sync.dma_start(out=mhs[:, b, :], in_=mh_hbm[128 * i:128 * (i + 1), :]).then_inc(sem["sXL0" if b == 0 else "sXL1"], 16)
    cnt["sXL0"] += 32 * (NCH // 2)
    cnt["sXL1"] += 32 * (NCH // 2)

    @block.tensor
    def _(tensor):
        tensor.wait_ge(sem["sGD"], nGD)
        tensor.wait_ge(sem["sVn"], L["sVn"])
        tensor.wait_ge(sem["sAq"], L["sAq"])
        tensor.wait_ge(sem["sAt"], L["sAt"])
        for i in range(NCH):
            b = i % 2
            s_n = "sXL0" if b == 0 else "sXL1"
            tensor.wait_ge(sem[s_n], L[s_n] + 32 * (i // 2 + 1))
            for nq in range(4):
                if 4 * i + nq >= 2:
                    tensor.wait_ge(sem["sAq"], L["sAq"] + 4 * i + nq - 1)
                for kc in range(KT):
                    tensor.matmul(psG2[nq % 2][:], xnt[:, b, kc, :], wg[:, kc, 512 * nq:512 * (nq + 1)],
                                  start=(kc == 0), stop=False)
                tensor.matmul(psG2[nq % 2][:], onesb[:, :], bgb[:, 512 * nq:512 * (nq + 1)],
                              start=False, stop=True).then_inc(sem["sPq"], 1)
    cnt["sPq"] += 4 * NCH

    @block.scalar
    def _(scalar):
        for i in range(NCH):
            b = i % 2
            for nq in range(4):
                scalar.wait_ge(sem["sPq"], L["sPq"] + 4 * i + nq + 1)
                if i >= 1 and nq == 0:
                    scalar.wait_ge(sem["sVi"], L["sVi"] + i)
                scalar.activation(gatef[:, b, 512 * nq:512 * (nq + 1)], psG2[nq % 2][:], AF.Sigmoid).then_inc(sem["sAq"], 1)
    cnt["sAq"] += 4 * NCH

    @block.vector
    def _(vector):
        for i in range(NCH):
            b = i % 2
            s_n = "sXL0" if b == 0 else "sXL1"
            vector.wait_ge(sem["sAq"], L["sAq"] + 4 * (i + 1))
            vector.wait_ge(sem[s_n], L[s_n] + 32 * (i // 2 + 1))
            if i >= 2:
                vector.wait_ge(sem["sOUT0" if b == 0 else "sOUT1"],
                               L["sOUT0" if b == 0 else "sOUT1"] + 16 * (i // 2))
            vector.tensor_tensor(ctb[:, b, :], gatef[:, b, :], mhs[:, b, :], ALU.mult).then_inc(sem["sVi"], 1)
    cnt["sVi"] += NCH

    st.close()


def emit_dumps(nc, block, sem, cnt, env, dbg, phases):
    xnT_hbm = env["xnT_hbm"]
    combine = env["combine"]

    @block.sync
    def _(sync):
        base = cnt["sLD"]
        n = 0
        sync.wait_ge(sem["sXW0"], cnt["sXW0"])
        sync.wait_ge(sem["sXW1"], cnt["sXW1"])
        if "xnT" in dbg:
            for k in range(KT):
                sync.dma_start(out=dbg["xnT"][k], in_=xnT_hbm[k]).then_inc(sem["sLD"], 16)
                n += 16
        if "dccout" in dbg and "B" in phases:
            sync.wait_ge(sem["sCC"], cnt["sCC"])
            sync.dma_start(out=dbg["dccout"], in_=env["dcc_out"][:, :]).then_inc(sem["sLD"], 16)
            sync.dma_start(out=dbg["dccin"], in_=env["dcc_in"][:, :]).then_inc(sem["sLD"], 16)
            n += 32
            if "bb" in dbg:
                sync.dma_start(out=dbg["bb"], in_=dbg["bb_sb"][:]).then_inc(sem["sLD"], 16)
                n += 16
            if "fTm" in dbg and dbg.get("Bt") is not None:
                bt = dbg["Bt"]
                sync.dma_start(out=dbg["fTm"], in_=bt["fTm"][:]).then_inc(sem["sLD"], 16)
                sync.dma_start(out=dbg["vmb"][:, :, 0:128], in_=bt["v_sb"][:]).then_inc(sem["sLD"], 16)
                sync.dma_start(out=dbg["vmb"][:, :, 128:640], in_=bt["mb_sb"][:]).then_inc(sem["sLD"], 16)
                n += 48
        if "combine" in dbg:
            sync.dma_start(out=dbg["combine"], in_=combine[:]).then_inc(sem["sLD"], 16)
            n += 16
        sync.wait_ge(sem["sLD"], base + n)
        cnt["sLD"] += n


def _prep_inputs(inputs):
    f32 = np.float32
    c = np.ascontiguousarray
    d = {}
    d["wqT"] = c(inputs["L0_attn_Wq"].T.reshape(KT, 128, DM).astype(f32, copy=False))
    d["wgT"] = c(inputs["L0_attn_Wg"].T.reshape(KT, 128, DM).astype(f32, copy=False))
    d["wmbT"] = c(inputs["L0_attn_Wmb"].T.reshape(KT, 128, DM).astype(f32, copy=False))
    d["wkT"] = c(inputs["L0_attn_Wk"].T.reshape(KT, 128, 512).astype(f32, copy=False))
    d["wvT"] = c(inputs["L0_attn_Wv"].T.reshape(KT, 128, 512).astype(f32, copy=False))
    d["bq"] = c(inputs["L0_attn_bq"].reshape(1, DM).astype(f32, copy=False))
    d["bg"] = c(inputs["L0_attn_bg"].reshape(1, DM).astype(f32, copy=False))
    d["bmb"] = c(inputs["L0_attn_bmb"].reshape(1, DM).astype(f32, copy=False))
    d["bk"] = c(inputs["L0_attn_bk"].reshape(1, 512).astype(f32, copy=False))
    d["bv"] = c(inputs["L0_attn_bv"].reshape(1, 512).astype(f32, copy=False))
    mem = np.asarray(inputs["L0_attn_memory"], f32)
    nrm = np.asarray(inputs["L0_attn_norm"], f32)
    d["memcat"] = c(np.concatenate([mem, nrm[:, :, None]], axis=2))
    d["normr"] = c(nrm)
    d["gwT"] = c(inputs["L0_ffn_gate_w"].T.reshape(KT, 128, NE).astype(f32, copy=False))
    gup = np.asarray(inputs["L0_ffn_gate_up_proj"], f32)
    perm = np.empty(I2, np.int64)
    for j in range(II // 128):
        perm[256 * j:256 * j + 128] = np.arange(128 * j, 128 * (j + 1))
        perm[256 * j + 128:256 * j + 256] = II + np.arange(128 * j, 128 * (j + 1))
    d["wgu"] = c(gup[:, :, perm])
    d["wdn"] = c(np.asarray(inputs["L0_ffn_down_proj"], f32))
    d["identf"] = np.eye(128, dtype=f32)
    d["identb"] = np.eye(128, dtype=ml_dtypes.bfloat16)
    d["onesb"] = np.ones((1, 128), dtype=ml_dtypes.bfloat16)
    return d


def kernel(**inputs):
    shared = _prep_inputs(inputs)
    x = np.ascontiguousarray(np.asarray(inputs["L0_attn_hidden_states"], np.float32).reshape(B * S, DM))

    nc = build_program()
    in_maps = []
    for c in range(NCORES):
        m = dict(shared)
        m["x"] = x[c * T:(c + 1) * T]
        in_maps.append(m)
    res = run_bass_kernel_spmd(nc, in_maps, core_ids=list(range(NCORES)))
    out = np.concatenate([np.asarray(r["out"]) for r in res.results], axis=0)
    return out.reshape(B, S, DM)


# version 13
# speedup vs baseline: 1.1283x; 1.1283x over previous
import math
from contextlib import ExitStack

import numpy as np
import ml_dtypes

import concourse.bass as bass
import concourse.bass_utils as _bass_utils
import concourse.mybir as mybir
from concourse.bass_utils import run_bass_kernel_spmd

import os as _os
if _os.environ.get("KB_NO_LDWOPT", "") == "":
    _LDW_ENABLE = True
else:
    _LDW_ENABLE = False
if _LDW_ENABLE and not getattr(_bass_utils, "_ldw_opt_patched", False):
    _orig_run_command = _bass_utils.run_command

    def _run_command_ldw(argv, **kw):
        argv = ["--enable-ldw-opt=true" if a == "--enable-ldw-opt=false" else a for a in argv]
        return _orig_run_command(argv, **kw)

    _bass_utils.run_command = _run_command_ldw
    _bass_utils._ldw_opt_patched = True

F32 = mybir.dt.float32
BF16 = mybir.dt.bfloat16
AF = mybir.ActivationFunctionType
ALU = mybir.AluOpType
AX = mybir.AxisListType

NCORES = 8
B, S, DM = 16, 2048, 2048
T = (B // NCORES) * S
NCH = T // 128
KT = DM // 128
H, HKV, HD, F = 16, 4, 128, 768
NE, I2, II = 16, 2048, 1024
EPS = 1e-8

DEV_PHASES = ("A", "B", "D", "C")
DEV_DUMP = False


def _cdma(q, out, in_):
    return q.dma_start(out=out, in_=in_)


def build_program(phases=DEV_PHASES, dump=DEV_DUMP):
    nc = bass.Bass("TRN2", target_bir_lowering=False, debug=False)

    x_h = nc.dram_tensor("x", [T, DM], F32, kind="ExternalInput").ap()
    out_h = nc.dram_tensor("out", [T, DM], F32, kind="ExternalOutput").ap()

    wqT_h = nc.dram_tensor("wqT", [KT, 128, DM], F32, kind="ExternalInput").ap()
    wgT_h = nc.dram_tensor("wgT", [KT, 128, DM], F32, kind="ExternalInput").ap()
    wmbT_h = nc.dram_tensor("wmbT", [KT, 128, DM], F32, kind="ExternalInput").ap()
    wkT_h = nc.dram_tensor("wkT", [KT, 128, 512], F32, kind="ExternalInput").ap()
    wvT_h = nc.dram_tensor("wvT", [KT, 128, 512], F32, kind="ExternalInput").ap()
    bq_h = nc.dram_tensor("bq", [1, DM], F32, kind="ExternalInput").ap()
    bg_h = nc.dram_tensor("bg", [1, DM], F32, kind="ExternalInput").ap()
    bmb_h = nc.dram_tensor("bmb", [1, DM], F32, kind="ExternalInput").ap()
    bk_h = nc.dram_tensor("bk", [1, 512], F32, kind="ExternalInput").ap()
    bv_h = nc.dram_tensor("bv", [1, 512], F32, kind="ExternalInput").ap()
    memcat_h = nc.dram_tensor("memcat", [H, F, 129], F32, kind="ExternalInput").ap()
    normr_h = nc.dram_tensor("normr", [H, F], F32, kind="ExternalInput").ap()
    gwT_h = nc.dram_tensor("gwT", [KT, 128, NE], F32, kind="ExternalInput").ap()
    wgu_h = nc.dram_tensor("wgu", [NE, DM, I2], F32, kind="ExternalInput").ap()
    wdn_h = nc.dram_tensor("wdn", [NE, II, DM], F32, kind="ExternalInput").ap()
    identf_h = nc.dram_tensor("identf", [128, 128], F32, kind="ExternalInput").ap()
    identb_h = nc.dram_tensor("identb", [128, 128], BF16, kind="ExternalInput").ap()
    onesb_h = nc.dram_tensor("onesb", [1, 128], BF16, kind="ExternalInput").ap()

    xnT_hbm = nc.dram_tensor("xnT_hbm", [KT, 128, T], BF16).ap()
    mh_hbm = nc.dram_tensor("mh_hbm", [T, DM], F32).ap()
    dcc_in = nc.dram_tensor("dcc_in", [128, 12384], F32)
    dcc_out = nc.dram_tensor("dcc_out", [128, 12384], F32)

    dbg = {}
    if dump:
        dbg["xnT"] = nc.dram_tensor("dbg_xnT", [KT, 128, T], BF16, kind="ExternalOutput").ap()
        dbg["combine"] = nc.dram_tensor("dbg_combine", [128, NCH, NE], F32, kind="ExternalOutput").ap()
        dbg["newmem"] = nc.dram_tensor("dbg_newmem", [128, H, 6, 129], BF16, kind="ExternalOutput").ap()
        dbg["dccout"] = nc.dram_tensor("dbg_dccout", [128, 12384], F32, kind="ExternalOutput").ap()
        dbg["dccin"] = nc.dram_tensor("dbg_dccin", [128, 12384], F32, kind="ExternalOutput").ap()
        dbg["bb"] = nc.dram_tensor("dbg_bb", [128, 1043], F32, kind="ExternalOutput").ap()
        dbg["bb_sb"] = nc.alloc_sbuf_tensor("dbg_bb_s", [128, 1043], F32)
        dbg["fTm"] = nc.dram_tensor("dbg_fTm", [128, NCH, F], BF16, kind="ExternalOutput").ap()
        dbg["vmb"] = nc.dram_tensor("dbg_vmb", [128, NCH, 640], BF16, kind="ExternalOutput").ap()

    identf = nc.alloc_sbuf_tensor("identf_s", [128, 128], F32)
    identb = nc.alloc_sbuf_tensor("identb_s", [128, 128], BF16)
    onesb = nc.alloc_sbuf_tensor("onesb_s", [1, 128], BF16)
    combine = nc.alloc_sbuf_tensor("combine", [128, NCH, NE], F32)

    sem_names = [
        "sLD", "sGD", "sOUT", "sXW", "sXR",
        "sXL0", "sXL1", "sGW", "sXW0", "sXW1", "sOUT0", "sOUT1",
        "sVs", "sVq", "sAr", "sVxa", "sVxn", "sVxt", "sVro",
        "sPt", "sPr", "sAe", "sAp",
        "sB", "sCC", "sDD",
        "sPk", "sPv", "sPm", "sPn", "sPd", "sVm", "sVv", "sVf", "sVw", "sAq", "sAg",
        "sPq", "sPg", "sPu", "sPy", "sVi", "sVy", "sAs", "sPB", "sAB", "sVN", "sPN", "sAt", "sVn", "sGu", "sGw",
    ]
    sem = {n: nc.alloc_semaphore(n) for n in sem_names}
    cnt = {n: 0 for n in sem_names}

    ENGS = ["sync", "gpsimd", "vector", "scalar", "tensor"]

    def barrier(block, nb):
        target = 5 * nb
        for name in ENGS:
            eng = getattr(block, name)

            @eng
            def _(e, target=target):
                e.nop().then_inc(sem["sB"], 1)
                e.wait_ge(sem["sB"], target)

    with nc.Block() as block:
        @block.sync
        def _(sync):
            sync.dma_start(out=identf[:], in_=identf_h).then_inc(sem["sLD"], 16)
            sync.dma_start(out=identb[:], in_=identb_h).then_inc(sem["sLD"], 16)
            sync.dma_start(out=onesb[:], in_=onesb_h).then_inc(sem["sLD"], 16)
        cnt["sLD"] += 48

        nbar = 0
        if "A" in phases:
            emit_phase_A(nc, block, sem, cnt, locals())
        nbar += 1
        barrier(block, nbar)
        if "B" in phases:
            emit_phase_B(nc, block, sem, cnt, locals())
        nbar += 1
        barrier(block, nbar)
        if "D" in phases:
            emit_phase_D(nc, block, sem, cnt, locals())
        nbar += 1
        barrier(block, nbar)
        if "C" in phases:
            emit_phase_C(nc, block, sem, cnt, locals())
        nbar += 1
        barrier(block, nbar)

        n_out = cnt["sOUT"]
        n_out0 = cnt["sOUT0"]
        n_out1 = cnt["sOUT1"]

        @block.gpsimd
        def _(gpsimd):
            gpsimd.wait_ge(sem["sOUT"], n_out)
            gpsimd.wait_ge(sem["sOUT0"], n_out0)
            gpsimd.wait_ge(sem["sOUT1"], n_out1)

        if dump:
            emit_dumps(nc, block, sem, cnt, locals(), dbg, phases)

    return nc


def emit_phase_A(nc, block, sem, cnt, env):
    dbg = env.get("dbg", {})
    x_h, out_h, xnT_hbm = env["x_h"], env["out_h"], env["xnT_hbm"]
    gwT_h = env["gwT_h"]
    identf, combine = env["identf"], env["combine"]

    st = ExitStack()
    sb = lambda n, s, d: st.enter_context(nc.sbuf_tensor(n, s, d))
    ps = lambda n, s, d: st.enter_context(nc.psum_tensor(n, s, d))

    xch = sb("A_xch", [128, 2, DM], F32)
    xnf = sb("A_xnf", [128, 2, DM], F32)
    xa = sb("A_xa", [128, 2, DM], F32)
    dumpA = sb("A_dump", [128, DM], F32)
    s1 = sb("A_s1", [128, 1], F32)
    s2 = sb("A_s2", [128, 1], F32)
    mcol = sb("A_m", [128, 1], F32)
    msq = sb("A_msq", [128, 1], F32)
    s2n = sb("A_s2n", [128, 1], F32)
    ve = sb("A_ve", [128, 1], F32)
    rv = sb("A_rv", [128, 1], F32)
    rstd = sb("A_rstd", [128, 1], F32)
    xnTf = sb("A_xnTf", [128, KT, 128], F32)
    xnT16 = sb("A_xnT16", [128, 2, KT, 128], BF16)
    gwT = sb("A_gwT", [128, KT, NE], F32)
    pr = sb("A_pr", [128, NE], F32)
    m1 = sb("A_m1", [128, 1], F32)
    m2 = sb("A_m2", [128, 1], F32)
    msk = sb("A_msk", [128, NE], F32)
    pm = sb("A_pm", [128, NE], F32)
    wsum = sb("A_wsum", [128, 1], F32)
    winv = sb("A_winv", [128, 1], F32)
    c1t = sb("A_c1t", [128, NE], F32)

    trp = [ps(f"A_tr{j}", [128, 128], F32) for j in range(4)]
    rps = [ps(f"A_rt{j}", [128, NE], F32) for j in range(2)]

    L = {k: cnt[k] for k in cnt}

    @block.sync
    def _(sync):
        sync.dma_start(out=gwT[:], in_=gwT_h.rearrange("k p e -> p k e")).then_inc(sem["sGW"], 16)
        for i in range(NCH):
            b = i % 2
            if i >= 2:
                sync.wait_ge(sem["sVxa"], L["sVxa"] + i - 1)
            sync.dma_start(out=xch[:, b, :], in_=x_h[i * 128:(i + 1) * 128, :]).then_inc(sem["sXL0" if b == 0 else "sXL1"], 16)
    cnt["sGW"] += 16
    cnt["sXL0"] += 16 * (NCH // 2)
    cnt["sXL1"] += 16 * (NCH // 2)

    vs = [L["sVs"]]

    @block.vector
    def _(vector):
        def vedge(res):
            vs[0] += 1
            res.then_inc(sem["sVs"], 1)
            vector.wait_ge(sem["sVs"], vs[0])

        for i in range(NCH):
            b = i % 2
            vector.wait_ge(sem["sXL0" if i % 2 == 0 else "sXL1"], L["sXL0" if i % 2 == 0 else "sXL1"] + 16 * (i // 2 + 1))
            vector.reduce_sum(s1[:], xch[:, b, :], AX.X)
            vedge(vector.scalar_tensor_tensor(dumpA[:], xch[:, b, :], 1.0, xch[:, b, :],
                                              ALU.mult, ALU.mult, accum_out=s2[:]))
            vector.tensor_scalar(mcol[:], s1[:], 1.0 / DM, None, ALU.mult)
            vedge(vector.tensor_scalar(s2n[:], s2[:], 1.0 / DM, 1e-5, ALU.mult, ALU.add))
            vedge(vector.scalar_tensor_tensor(ve[:], mcol[:], mcol[:], s2n[:], ALU.mult, ALU.subtract))
            vector.reciprocal(rv[:], ve[:]).then_inc(sem["sVq"], 1)
            vector.wait_ge(sem["sAr"], L["sAr"] + i + 1)
            if i >= 2:
                vector.wait_ge(sem["sPt"], L["sPt"] + 16 * (i - 1))
            vector.tensor_scalar(xnf[:, b, :], xch[:, b, :], mcol[:], rstd[:], ALU.subtract, ALU.mult).then_inc(sem["sVxn"], 1)
            if i >= 2:
                vector.wait_ge(sem["sOUT0" if i % 2 == 0 else "sOUT1"],
                               L["sOUT0" if i % 2 == 0 else "sOUT1"] + 16 * (i // 2))
            vector.tensor_tensor(xa[:, b, :], xch[:, b, :], xnf[:, b, :], ALU.add).then_inc(sem["sVxa"], 1)
            vector.wait_ge(sem["sAe"], L["sAe"] + 16 * (i + 1))
            if i >= 2:
                vector.wait_ge(sem["sXW0" if i % 2 == 0 else "sXW1"],
                               L["sXW0" if i % 2 == 0 else "sXW1"] + 16 * (i // 2))
            vector.tensor_copy(xnT16[:, b, :, :], xnTf[:, :, :]).then_inc(sem["sVxt"], 1)
            vector.wait_ge(sem["sAp"], L["sAp"] + i + 1)
            vedge(vector.reduce_max(m1[:], pr[:], AX.X))
            vedge(vector.tensor_scalar(msk[:], pr[:], m1[:], None, ALU.is_ge))
            vedge(vector.scalar_tensor_tensor(pm[:], msk[:], -1e30, pr[:], ALU.mult, ALU.add))
            vedge(vector.reduce_max(m2[:], pm[:], AX.X))
            vedge(vector.tensor_tensor(wsum[:], m1[:], m2[:], ALU.add))
            vedge(vector.reciprocal(winv[:], wsum[:]))
            vedge(vector.tensor_scalar(msk[:], pr[:], m2[:], None, ALU.is_ge))
            vedge(vector.tensor_tensor(c1t[:], msk[:], pr[:], ALU.mult))
            vector.tensor_scalar(combine[:, i, :], c1t[:], winv[:], None, ALU.mult).then_inc(sem["sVro"], 1)
    cnt["sVs"] = vs[0]
    cnt["sVq"] += NCH
    cnt["sVxa"] += NCH
    cnt["sVxn"] += NCH
    cnt["sVxt"] += NCH
    cnt["sVro"] += NCH

    @block.scalar
    def _(scalar):
        for i in range(NCH):
            scalar.wait_ge(sem["sVq"], L["sVq"] + i + 1)
            scalar.activation(rstd[:], rv[:], AF.Sqrt, bias=0.0, scale=-1.0).then_inc(sem["sAr"], 1)
            for k in range(KT):
                scalar.wait_ge(sem["sPt"], L["sPt"] + 16 * i + k + 1)
                if i > 0 and k == 0:
                    scalar.wait_ge(sem["sPr"], L["sPr"] + i)
                    scalar.wait_ge(sem["sVxt"], L["sVxt"] + i)
                scalar.copy(xnTf[:, k, :], trp[k % 4][:]).then_inc(sem["sAe"], 1)
            scalar.wait_ge(sem["sPr"], L["sPr"] + i + 1)
            if i >= 1:
                scalar.wait_ge(sem["sVro"], L["sVro"] + i)
            scalar.activation(pr[:], rps[i % 2][:], AF.Sigmoid).then_inc(sem["sAp"], 1)
    cnt["sAr"] += NCH
    cnt["sAe"] += 16 * NCH
    cnt["sAp"] += NCH

    @block.tensor
    def _(tensor):
        tensor.wait_ge(sem["sLD"], L["sLD"])
        tensor.wait_ge(sem["sGW"], L["sGW"] + 16)
        for i in range(NCH):
            b = i % 2
            tensor.wait_ge(sem["sVxn"], L["sVxn"] + i + 1)
            for k in range(KT):
                t = 16 * i + k
                if t >= 4:
                    tensor.wait_ge(sem["sAe"], L["sAe"] + t - 3)
                tensor.transpose(trp[k % 4][:], xnf[:, b, 128 * k:128 * (k + 1)], identf[:]).then_inc(sem["sPt"], 1)
            tensor.wait_ge(sem["sAe"], L["sAe"] + 16 * (i + 1))
            if i >= 2:
                tensor.wait_ge(sem["sAp"], L["sAp"] + i - 1)
            for k in range(KT):
                mm = tensor.matmul(rps[i % 2][:], xnTf[:, k, :], gwT[:, k, :],
                                   start=(k == 0), stop=(k == KT - 1))
            mm.then_inc(sem["sPr"], 1)
    cnt["sPt"] += 16 * NCH
    cnt["sPr"] += NCH

    @block.gpsimd
    def _(gpsimd):
        for i in range(NCH):
            b = i % 2
            gpsimd.wait_ge(sem["sVxa"], L["sVxa"] + i + 1)
            gpsimd.dma_start(out=out_h[i * 128:(i + 1) * 128, :], in_=xa[:, b, :]).then_inc(sem["sOUT0" if b == 0 else "sOUT1"], 16)
            gpsimd.wait_ge(sem["sVxt"], L["sVxt"] + i + 1)
            gpsimd.dma_start(out=xnT_hbm[:, :, i * 128:(i + 1) * 128].rearrange("k p t -> p k t"),
                             in_=xnT16[:, b, :, :]).then_inc(sem["sXW0" if b == 0 else "sXW1"], 16)
    cnt["sOUT0"] += 16 * (NCH // 2)
    cnt["sOUT1"] += 16 * (NCH // 2)
    cnt["sXW0"] += 16 * (NCH // 2)
    cnt["sXW1"] += 16 * (NCH // 2)

    st.close()


def emit_phase_B(nc, block, sem, cnt, env):
    xnT_hbm = env["xnT_hbm"]
    wkT_h, wvT_h, wmbT_h = env["wkT_h"], env["wvT_h"], env["wmbT_h"]
    bk_h, bv_h, bmb_h = env["bk_h"], env["bv_h"], env["bmb_h"]
    memcat_h, normr_h = env["memcat_h"], env["normr_h"]
    dcc_in, dcc_out = env["dcc_in"], env["dcc_out"]
    identb, onesb = env["identb"], env["onesb"]

    st = ExitStack()
    sb = lambda n, s, d: st.enter_context(nc.sbuf_tensor(n, s, d))

    fTm = sb("B_fTm", [128, NCH, F], BF16)
    fT = sb("B_fT", [128, 6, T], BF16)
    v_sb = sb("B_v", [128, NCH, HD], BF16)
    mb_sb = sb("B_mb", [128, NCH, 512], BF16)
    s_sb = sb("B_s", [128, NCH], F32)
    nmax_sb = sb("B_nmax", [128, NCH], F32)
    invn_sb = sb("B_invn", [128, NCH], F32)
    coefi_sb = sb("B_coefi", [128, NCH, 4], BF16)
    wk_g = sb("B_wk", [128, KT, 128], BF16)
    wv_g = sb("B_wv", [128, KT, 128], BF16)
    wmb_g = sb("B_wmb", [128, KT, 512], BF16)
    bk_g = sb("B_bk", [1, 128], BF16)
    bv_g = sb("B_bv", [1, 128], BF16)
    bmb_g = sb("B_bmb", [1, 512], BF16)
    mem_rhs = sb("B_memrhs", [128, 4, 6, 129], BF16)
    normb = sb("B_normb", [1, 4, F], BF16)
    norm_bc = sb("B_normbc", [128, 4, F], BF16)
    xnt = sb("B_xnt", [128, 2, KT, 128], BF16)
    rq = sb("B_rq", [128, 2, 256], BF16)
    dumpB = sb("B_dump", [128, F], BF16)
    sqn = sb("B_sqn", [128, 1], F32)
    den4 = sb("B_den4", [128, 4], F32)
    ne1 = sb("B_ne1", [128, 1], F32)
    d4 = sb("B_d4", [128, 4], F32)
    rec4 = sb("B_rec4", [128, 4], F32)
    nri4 = sb("B_nri4", [128, 4], F32)
    dr4 = sb("B_dr4", [128, 4], F32)
    msq1 = sb("B_msq1", [128, 1], F32)
    rmsq = sb("B_rmsq", [128, 1], F32)
    co4 = sb("B_co4", [128, 4], F32)
    co4b = sb("B_co4b", [128, 4], F32)
    mvs = sb("B_mvs", [128, 512], F32)
    wmv4 = sb("B_wmv4", [128, 2, 512], BF16)
    dbounce = sb("B_dbounce", [128, 2, 512], F32)
    dnbounce = sb("B_dnb", [128, 24], F32)

    import os
    glist = range(int(os.environ.get('KB_GONLY', HKV)))
    for g in glist:
        _emit_Ba(nc, block, sem, cnt, env, locals(), g)
        _emit_Bb(nc, block, sem, cnt, env, locals(), g)

    if env.get("dbg"):
        env["dbg"]["Bt"] = {"fTm": fTm, "v_sb": v_sb, "mb_sb": mb_sb}

    nDD = cnt["sDD"]

    @block.gpsimd
    def _(gpsimd):
        gpsimd.wait_ge(sem["sDD"], nDD)
        gpsimd.collective_compute(
            "AllReduce", ALU.add,
            replica_groups=[list(range(NCORES))],
            ins=[dcc_in.ap().opt()],
            outs=[dcc_out.ap().opt()],
        ).then_inc(sem["sCC"], 1)
    cnt["sCC"] += 1

    if not env.get("dbg"):
        st.close()


def _emit_Ba(nc, block, sem, cnt, env, B, g):
    wkT_h, wvT_h, wmbT_h = env["wkT_h"], env["wvT_h"], env["wmbT_h"]
    bk_h, bv_h, bmb_h = env["bk_h"], env["bv_h"], env["bmb_h"]
    memcat_h, normr_h = env["memcat_h"], env["normr_h"]
    xnT_hbm = env["xnT_hbm"]
    identb, onesb = env["identb"], env["onesb"]
    (fTm, fT, v_sb, mb_sb, s_sb, nmax_sb, invn_sb, wk_g, wv_g, wmb_g, bk_g, bv_g,
     bmb_g, mem_rhs, normb, norm_bc, xnt, rq, dumpB, sqn) = (
        B["fTm"], B["fT"], B["v_sb"], B["mb_sb"], B["s_sb"], B["nmax_sb"], B["invn_sb"],
        B["wk_g"], B["wv_g"], B["wmb_g"], B["bk_g"], B["bv_g"], B["bmb_g"],
        B["mem_rhs"], B["normb"], B["norm_bc"], B["xnt"], B["rq"], B["dumpB"], B["sqn"])
    del B

    st = ExitStack()
    ps = lambda n, s, d: st.enter_context(nc.psum_tensor(n, s, d))
    psK = [ps(f"Ba{g}_k{j}", [128, 128], F32) for j in range(2)]
    psV = [ps(f"Ba{g}_v{j}", [128, 128], F32) for j in range(2)]
    psM = [ps(f"Ba{g}_m{j}", [128, 512], F32) for j in range(2)]
    psT = [ps(f"Ba{g}_t{j}", [128, 768], BF16) for j in range(2)]

    L = {k: cnt[k] for k in cnt}

    @block.gpsimd
    def _(gpsimd):
        if g == 0:
            gpsimd.wait_ge(sem["sVro"], L["sVro"])
        gpsimd.dma_start(out=wk_g[:], in_=wkT_h[:, :, g * 128:(g + 1) * 128].rearrange("k p n -> p k n")).then_inc(sem["sGD"], 16)
        gpsimd.dma_start(out=wv_g[:], in_=wvT_h[:, :, g * 128:(g + 1) * 128].rearrange("k p n -> p k n")).then_inc(sem["sGD"], 16)
        gpsimd.dma_start(out=wmb_g[:], in_=wmbT_h[:, :, g * 512:(g + 1) * 512].rearrange("k p n -> p k n")).then_inc(sem["sGD"], 16)
        gpsimd.dma_start(out=bk_g[:], in_=bk_h[:, g * 128:(g + 1) * 128]).then_inc(sem["sGD"], 16)
        gpsimd.dma_start(out=bv_g[:], in_=bv_h[:, g * 128:(g + 1) * 128]).then_inc(sem["sGD"], 16)
        gpsimd.dma_start(out=bmb_g[:], in_=bmb_h[:, g * 512:(g + 1) * 512]).then_inc(sem["sGD"], 16)
        gpsimd.dma_start(out=mem_rhs[:], in_=memcat_h[4 * g:4 * (g + 1)].rearrange("h (fc p) c -> p h fc c", p=128)).then_inc(sem["sGD"], 16)
        for h in range(4):
            gpsimd.dma_start(out=normb[:, h, :], in_=normr_h[4 * g + h]).then_inc(sem["sGD"], 16)
    cnt["sGD"] += 16 * 11
    nGD = cnt["sGD"]

    @block.sync
    def _(sync):
        for i in range(NCH):
            b = i % 2
            if i >= 2:
                sync.wait_ge(sem["sPB"], L["sPB"] + 4 * (i - 2) + 3)
            else:
                sync.wait_ge(sem["sPB"], L["sPB"])
            if g == 0 and i == 0:
                sync.wait_ge(sem["sXW0"], L["sXW0"])
                sync.wait_ge(sem["sXW1"], L["sXW1"])
            sync.dma_start(out=xnt[:, b, :, :],
                           in_=xnT_hbm[:, :, i * 128:(i + 1) * 128].rearrange("k p t -> p k t")
                           ).then_inc(sem["sXL0" if b == 0 else "sXL1"], 16)
    cnt["sXL0"] += 16 * (NCH // 2)
    cnt["sXL1"] += 16 * (NCH // 2)

    @block.tensor
    def _(tensor):
        tensor.wait_ge(sem["sGD"], nGD)
        tensor.wait_ge(sem["sVy"], L["sVy"])
        tensor.wait_ge(sem["sAp"], L["sAp"])
        tensor.wait_ge(sem["sAe"], L["sAe"])
        for h in range(4):
            tensor.matmul(psM[0][:, 0:512], onesb[:, :], normb[:, h, 0:512], start=True, stop=True)
            tensor.matmul(psM[1][:, 0:256], onesb[:, :], normb[:, h, 512:768], start=True, stop=True).then_inc(sem["sPn"], 1)
            tensor.wait_ge(sem["sAg"], L["sAg"] + h + 1)
        for i in range(NCH):
            b = i % 2
            tensor.wait_ge(sem["sXL0" if b == 0 else "sXL1"], L["sXL0" if b == 0 else "sXL1"] + 16 * (i // 2 + 1))
            if i >= 2:
                tensor.wait_ge(sem["sAB"], L["sAB"] + i - 1)
                tensor.wait_ge(sem["sVv"], L["sVv"] + i - 1)
                tensor.wait_ge(sem["sVm"], L["sVm"] + i - 1)
            for k in range(KT):
                tensor.matmul(psK[b][:], xnt[:, b, k, :], wk_g[:, k, :], start=(k == 0), stop=False)
            tensor.matmul(psK[b][:], onesb[:, :], bk_g[:], start=False, stop=True).then_inc(sem["sPB"], 1)
            for k in range(KT):
                tensor.matmul(psV[b][:], xnt[:, b, k, :], wv_g[:, k, :], start=(k == 0), stop=False)
            tensor.matmul(psV[b][:], onesb[:, :], bv_g[:], start=False, stop=True).then_inc(sem["sPB"], 1)
            for k in range(KT):
                tensor.matmul(psM[b][:], xnt[:, b, k, :], wmb_g[:, k, :], start=(k == 0), stop=False)
            tensor.matmul(psM[b][:], onesb[:, :], bmb_g[:], start=False, stop=True).then_inc(sem["sPB"], 1)
            tensor.wait_ge(sem["sVf"], L["sVf"] + i + 1)
            if i >= 2:
                tensor.wait_ge(sem["sAq"], L["sAq"] + i - 1)
            for j in range(6):
                mm = tensor.transpose(psT[i % 2][:, 128 * j:128 * (j + 1)],
                                      fTm[:, i, 128 * j:128 * (j + 1)], identb[:])
            mm.then_inc(sem["sPB"], 1)
    cnt["sPn"] += 4
    cnt["sPB"] += 4 * NCH

    @block.scalar
    def _(scalar):
        for h in range(4):
            scalar.wait_ge(sem["sPn"], L["sPn"] + h + 1)
            scalar.copy(norm_bc[:, h, 0:512], psM[0][:, 0:512])
            scalar.copy(norm_bc[:, h, 512:768], psM[1][:, 0:256]).then_inc(sem["sAg"], 1)
        for i in range(NCH):
            b = i % 2
            scalar.wait_ge(sem["sPB"], L["sPB"] + 4 * i + 1)
            if i >= 2:
                scalar.wait_ge(sem["sVf"], L["sVf"] + i - 1)
            scalar.activation(rq[:, b, 0:128], psK[b][:], AF.Relu)
            scalar.activation(rq[:, b, 128:256], psK[b][:], AF.Relu, bias=0.0, scale=-1.0).then_inc(sem["sAB"], 1)
            scalar.wait_ge(sem["sVq"], L["sVq"] + i + 1)
            scalar.activation(sqn[:], s_sb[:, i:i + 1], AF.Sqrt).then_inc(sem["sAr"], 1)
            scalar.wait_ge(sem["sPB"], L["sPB"] + 4 * i + 2)
            scalar.copy(v_sb[:, i, :], psV[b][:]).then_inc(sem["sVv"], 1)
            scalar.wait_ge(sem["sPB"], L["sPB"] + 4 * i + 3)
            scalar.activation(mb_sb[:, i, :], psM[b][:], AF.Sigmoid).then_inc(sem["sVm"], 1)
            scalar.wait_ge(sem["sPB"], L["sPB"] + 4 * i + 4)
            scalar.copy(fT[:, :, i * 128:(i + 1) * 128],
                        psT[i % 2][:].rearrange("p (j d) -> p j d", d=128)).then_inc(sem["sAq"], 1)
    cnt["sAg"] += 4
    cnt["sAB"] += NCH
    cnt["sAr"] += NCH
    cnt["sAq"] += NCH
    cnt["sVm"] += NCH
    cnt["sVv"] += NCH

    vs = [L["sVs"]]

    @block.vector
    def _(vector):
        def vedge(res):
            vs[0] += 1
            res.then_inc(sem["sVs"], 1)
            vector.wait_ge(sem["sVs"], vs[0])

        for i in range(NCH):
            b = i % 2
            vector.wait_ge(sem["sAB"], L["sAB"] + i + 1)
            for j in (1, 2, 3):
                base = 256 * (j - 1)
                vector.tensor_tensor(fTm[:, i, base:base + j], rq[:, b, 0:j], rq[:, b, 256 - j:256], ALU.mult)
                res = vector.tensor_tensor(fTm[:, i, base + j:base + 256], rq[:, b, j:256], rq[:, b, 0:256 - j], ALU.mult)
            res.then_inc(sem["sVf"], 1)
            vector.wait_ge(sem["sVf"], L["sVf"] + i + 1)
            vedge(vector.scalar_tensor_tensor(dumpB[:], fTm[:, i, :], 1.0, fTm[:, i, :],
                                              ALU.mult, ALU.mult, accum_out=s_sb[:, i:i + 1]))
            vector.nop().then_inc(sem["sVq"], 1)
            vector.wait_ge(sem["sAr"], L["sAr"] + i + 1)
            vedge(vector.tensor_scalar(nmax_sb[:, i:i + 1], sqn[:], 1e-12, None, ALU.max))
            vedge(vector.reciprocal(invn_sb[:, i:i + 1], nmax_sb[:, i:i + 1]))

    cnt["sVs"] = vs[0]
    cnt["sVf"] += NCH
    cnt["sVq"] += NCH

    st.close()


def _emit_Bb(nc, block, sem, cnt, env, B, g):
    dbg_bb = env.get("dbg", {}).get("bb_sb")
    dcc_in = env["dcc_in"]
    (fTm, fT, v_sb, mb_sb, s_sb, nmax_sb, invn_sb, coefi_sb, mem_rhs, norm_bc,
     den4, ne1, d4, rec4, nri4, dr4, msq1, rmsq, co4, co4b, mvs, wmv4, dbounce,
     dnbounce, dumpB) = (
        B["fTm"], B["fT"], B["v_sb"], B["mb_sb"], B["s_sb"], B["nmax_sb"], B["invn_sb"],
        B["coefi_sb"], B["mem_rhs"], B["norm_bc"], B["den4"], B["ne1"], B["d4"],
        B["rec4"], B["nri4"], B["dr4"], B["msq1"], B["rmsq"], B["co4"], B["co4b"],
        B["mvs"], B["wmv4"], B["dbounce"], B["dnbounce"], B["dumpB"])

    st = ExitStack()
    ps = lambda n, s, d: st.enter_context(nc.psum_tensor(n, s, d))
    psD = [ps(f"Bb{g}_d{j}", [128, 512], F32) for j in range(6)]
    psN = [ps(f"Bb{g}_n{j}", [128, 512], F32) for j in range(2)]

    L = {k: cnt[k] for k in cnt}

    @block.tensor
    def _(tensor):
        tensor.wait_ge(sem["sAq"], L["sAq"])
        tensor.wait_ge(sem["sVv"], L["sVv"])
        tensor.wait_ge(sem["sVm"], L["sVm"])
        tensor.wait_ge(sem["sAB"], L["sAB"])
        for i in range(NCH):
            b = i % 2
            if i >= 2:
                tensor.wait_ge(sem["sVw"], L["sVw"] + i - 1)
            for fc in range(6):
                mm = tensor.matmul(psN[b][:, 0:512],
                                   fT[:, fc, i * 128:(i + 1) * 128],
                                   mem_rhs[:, 0:4, fc, 0:128],
                                   start=(fc == 0), stop=(fc == 5))
            mm.then_inc(sem["sPN"], 1)
            tensor.wait_ge(sem["sVw"], L["sVw"] + i + 1)
            for fc in range(6):
                mm = tensor.matmul(psD[fc][:], fTm[:, i, 128 * fc:128 * (fc + 1)], wmv4[:, b, :],
                                   start=(i == 0), stop=(i == NCH - 1))
            mm.then_inc(sem["sPd"], 1)
        tensor.wait_ge(sem["sVw"], L["sVw"] + NCH)
        tensor.wait_ge(sem["sVy"], L["sVy"] + 6)
        for fc in range(6):
            for i in range(NCH):
                mm = tensor.matmul(psN[0][:, 4 * fc:4 * (fc + 1)], fTm[:, i, 128 * fc:128 * (fc + 1)],
                                   coefi_sb[:, i, :], start=(i == 0), stop=(i == NCH - 1))
        mm.then_inc(sem["sPd"], 1)
    cnt["sPN"] += NCH
    cnt["sPd"] += NCH + 1

    vs = [L["sVs"]]

    @block.vector
    def _(vector):
        def vedge(res):
            vs[0] += 1
            res.then_inc(sem["sVs"], 1)
            vector.wait_ge(sem["sVs"], vs[0])

        for i in range(NCH):
            b = i % 2
            for h in range(4):
                res = vector.scalar_tensor_tensor(dumpB[:], fTm[:, i, :], 1.0, norm_bc[:, h, :],
                                                  ALU.mult, ALU.mult, accum_out=den4[:, h:h + 1])
            vedge(res)
            vedge(vector.tensor_scalar(ne1[:], nmax_sb[:, i:i + 1], EPS, None, ALU.mult))
            vedge(vector.tensor_scalar(d4[:], den4[:], ne1[:], None, ALU.add))
            vedge(vector.reciprocal(rec4[:], d4[:]))
            vedge(vector.tensor_scalar(nri4[:], rec4[:], -1.0, None, ALU.mult))
            vedge(vector.tensor_scalar(dr4[:], den4[:], invn_sb[:, i:i + 1], EPS, ALU.mult, ALU.add))
            vedge(vector.tensor_scalar(msq1[:], s_sb[:, i:i + 1], invn_sb[:, i:i + 1], invn_sb[:, i:i + 1], ALU.mult, ALU.mult))
            vedge(vector.reciprocal(rmsq[:], msq1[:]))
            vedge(vector.tensor_scalar(co4[:], dr4[:], rmsq[:], -1.0, ALU.mult, ALU.mult))
            vedge(vector.tensor_scalar(co4b[:], co4[:], 1.0, 0.0, ALU.add, ALU.max))
            vedge(vector.tensor_scalar(coefi_sb[:, i, :], co4b[:], 1.0, invn_sb[:, i:i + 1], ALU.min, ALU.mult))
            vector.wait_ge(sem["sPN"], L["sPN"] + i + 1)
            if i >= 2:
                vector.wait_ge(sem["sPd"], L["sPd"] + i - 1)
            for h in range(4):
                vector.scalar_tensor_tensor(mvs[:, 128 * h:128 * (h + 1)], psN[b][:, 128 * h:128 * (h + 1)],
                                            nri4[:, h:h + 1], v_sb[:, i, :], ALU.mult, ALU.add)
            for h in range(4):
                res = vector.scalar_tensor_tensor(wmv4[:, b, 128 * h:128 * (h + 1)], mvs[:, 128 * h:128 * (h + 1)],
                                                  invn_sb[:, i:i + 1], mb_sb[:, i, 128 * h:128 * (h + 1)],
                                                  ALU.mult, ALU.mult)
            if dbg_bb is not None and g == 3 and i == 5:
                vector.tensor_copy(dbg_bb[:, 0:512], psN[b][:, 0:512])
                vector.tensor_copy(dbg_bb[:, 512:516], den4[:])
                vector.tensor_copy(dbg_bb[:, 516:520], d4[:])
                vector.tensor_copy(dbg_bb[:, 520:524], rec4[:])
                vector.tensor_copy(dbg_bb[:, 524:528], nri4[:])
                vector.tensor_copy(dbg_bb[:, 528:1040], wmv4[:, b, :])
                vector.tensor_copy(dbg_bb[:, 1040:1041], invn_sb[:, i:i + 1])
                vector.tensor_copy(dbg_bb[:, 1041:1042], nmax_sb[:, i:i + 1])
                vector.tensor_copy(dbg_bb[:, 1042:1043], s_sb[:, i:i + 1])
            res.then_inc(sem["sVw"], 1)
        vector.wait_ge(sem["sPd"], L["sPd"] + NCH)
        for fc in range(6):
            if fc >= 2:
                vector.wait_ge(sem["sDD"], L["sDD"] + 16 * (fc - 1))
            else:
                vector.wait_ge(sem["sDD"], L["sDD"])
            vector.tensor_copy(dbounce[:, fc % 2, :], psD[fc][:]).then_inc(sem["sVy"], 1)
        vector.wait_ge(sem["sPd"], L["sPd"] + NCH + 1)
        vector.tensor_copy(dnbounce[:], psN[0][:, 0:24]).then_inc(sem["sVy"], 1)
    cnt["sVs"] = vs[0]
    cnt["sVw"] += NCH
    cnt["sVy"] += 7

    @block.gpsimd
    def _(gpsimd):
        for fc in range(6):
            gpsimd.wait_ge(sem["sVy"], L["sVy"] + fc + 1)
            gpsimd.dma_start(out=dcc_in[:, (g * 6 + fc) * 512:(g * 6 + fc + 1) * 512],
                             in_=dbounce[:, fc % 2, :]).then_inc(sem["sDD"], 16)
        gpsimd.wait_ge(sem["sVy"], L["sVy"] + 7)
        gpsimd.dma_start(out=dcc_in[:, 12288 + g * 24:12288 + (g + 1) * 24],
                         in_=dnbounce[:]).then_inc(sem["sDD"], 16)
    cnt["sDD"] += 16 * 7

    st.close()


def emit_phase_D(nc, block, sem, cnt, env):
    xnT_hbm, out_h = env["xnT_hbm"], env["out_h"]
    wgu_h, wdn_h = env["wgu_h"], env["wdn_h"]
    combine = env["combine"]

    st = ExitStack()
    sb = lambda n, s, d: st.enter_context(nc.sbuf_tensor(n, s, d))
    ps = lambda n, s, d: st.enter_context(nc.psum_tensor(n, s, d))

    wgu_sb = sb("D_wgu", [128, KT, I2], BF16)
    wdn_sb = sb("D_wdn", [128, 2, 8, DM], BF16)
    xnt = sb("D_xnt", [128, 2, KT, 512], BF16)
    inter = sb("D_inter", [128, 2, 8, 512], BF16)
    silu_b = sb("D_silu", [128, 2, 512], F32)
    ystage = sb("D_ys", [128, 2, DM], F32)
    psG = [ps(f"D_g{j}", [128, 512], F32) for j in range(2)]
    psU = [ps(f"D_u{j}", [128, 512], F32) for j in range(2)]
    psY = [ps(f"D_y{j}", [128, 512], F32) for j in range(2)]

    import os
    NE_D = int(os.environ.get("KD_EONLY", NE))
    SER = os.environ.get("KD_SERIAL", "")
    L = {k: cnt[k] for k in cnt}
    NTB = T // 512
    NJ = 8
    nTB = NE_D * NTB

    def load_wgu(gpsimd, e):
        for q in range(4):
            gpsimd.dma_start(out=wgu_sb[:, 4 * q:4 * (q + 1), :],
                             in_=wgu_h[e, 512 * q:512 * (q + 1), :].rearrange("(k p) m -> p k m", p=128)).then_inc(sem["sGu"], 16)

    def load_wdn(gpsimd, e):
        for q in range(2):
            gpsimd.dma_start(out=wdn_sb[:, e % 2, 4 * q:4 * (q + 1), :],
                             in_=wdn_h[e, 512 * q:512 * (q + 1), :].rearrange("(k p) m -> p k m", p=128)).then_inc(sem["sGw"], 16)

    @block.gpsimd
    def _(gpsimd):
        gpsimd.wait_ge(sem["sPd"], L["sPd"])
        load_wgu(gpsimd, 0)
        load_wdn(gpsimd, 0)
        for e in range(NE_D):
            for tb in range(NTB):
                for tc in range(4):
                    tcg = e * 32 + tb * 4 + tc
                    ch = tb * 4 + tc
                    b2 = tcg % 2
                    gpsimd.wait_ge(sem["sVy"], L["sVy"] + 4 * (tcg + 1))
                    gpsimd.dma_start(out=out_h[ch * 128:(ch + 1) * 128, :], in_=ystage[:, b2, :],
                                     accum_op=ALU.add).then_inc(sem["sOUT0" if b2 == 0 else "sOUT1"], 16)
                    if tb * 4 + tc == 23 and e + 1 < NE_D:
                        gpsimd.wait_ge(sem["sPg"], L["sPg"] + 16 * NTB * (e + 1))
                        load_wgu(gpsimd, e + 1)
                        gpsimd.wait_ge(sem["sPy"], L["sPy"] + 16 * NTB * e)
                        load_wdn(gpsimd, e + 1)
    cnt["sGu"] += 16 * 4 * NE_D
    cnt["sGw"] += 16 * 2 * NE_D
    n_out_d = NE_D * 32
    cnt["sOUT0"] += 16 * (n_out_d // 2)
    cnt["sOUT1"] += 16 * (n_out_d // 2)

    @block.sync
    def _(sync):
        for e in range(NE_D):
            for tb in range(NTB):
                tbg = e * NTB + tb
                b = tbg % 2
                if tbg >= 2:
                    sync.wait_ge(sem["sPg"], L["sPg"] + 16 * (tbg - 1))
                if tbg == 0:
                    sync.wait_ge(sem["sXW0"], L["sXW0"])
                    sync.wait_ge(sem["sXW1"], L["sXW1"])
                sync.dma_start(out=xnt[:, b, :, :],
                               in_=xnT_hbm[:, :, tb * 512:(tb + 1) * 512].rearrange("k p t -> p k t")
                               ).then_inc(sem["sXL0" if b == 0 else "sXL1"], 16)
    cnt["sXL0"] += 16 * (nTB // 2)
    cnt["sXL1"] += 16 * (nTB // 2)

    @block.tensor
    def _(tensor):
        def emit_down(tbg):
            ib = tbg % 2
            for tc in range(4):
                tcg = tbg * 4 + tc
                for nb in range(4):
                    yt = tcg * 4 + nb
                    yb = yt % 2
                    if yt >= 2:
                        tensor.wait_ge(sem["sVy"], L["sVy"] + yt - 1)
                    for kc in range(8):
                        if tc == 0 and nb == 0:
                            tensor.wait_ge(sem["sVi"], L["sVi"] + tbg * NJ + kc + 1)
                        mm = tensor.matmul(psY[yb][:], inter[:, ib, kc, 128 * tc:128 * (tc + 1)],
                                           wdn_sb[:, (tbg // NTB) % 2, kc, 512 * nb:512 * (nb + 1)],
                                           start=(kc == 0), stop=(kc == 7))
                    mm.then_inc(sem["sPy"], 1)

        tensor.wait_ge(sem["sVy"], L["sVy"])
        for e in range(NE_D):
            tensor.wait_ge(sem["sGu"], L["sGu"] + 64 * (e + 1))
            for tb in range(NTB):
                tbg = e * NTB + tb
                b = tbg % 2
                tensor.wait_ge(sem["sXL0" if b == 0 else "sXL1"],
                               L["sXL0" if b == 0 else "sXL1"] + 16 * (tbg // 2 + 1))
                for j in range(NJ):
                    jg = tbg * NJ + j
                    jb = jg % 2
                    if jg >= 2:
                        tensor.wait_ge(sem["sVi"], L["sVi"] + jg - 1)
                    for kc in range(KT):
                        mm = tensor.matmul(psG[jb][:], wgu_sb[:, kc, 256 * j:256 * j + 128], xnt[:, b, kc, :],
                                           start=(kc == 0), stop=(kc == KT - 1))
                    mm.then_inc(sem["sPg"], 1)
                    for kc in range(KT):
                        mm = tensor.matmul(psU[jb][:], wgu_sb[:, kc, 256 * j + 128:256 * (j + 1)], xnt[:, b, kc, :],
                                           start=(kc == 0), stop=(kc == KT - 1))
                    mm.then_inc(sem["sPg"], 1)
                if tbg >= 1:
                    if (tbg - 1) % NTB == 0:
                        tensor.wait_ge(sem["sGw"], L["sGw"] + 32 * ((tbg - 1) // NTB + 1))
                    emit_down(tbg - 1)
        tensor.wait_ge(sem["sGw"], L["sGw"] + 32 * NE_D)
        emit_down(NE_D * NTB - 1)
    cnt["sPg"] += 2 * NJ * nTB
    cnt["sPy"] += 16 * nTB

    @block.scalar
    def _(scalar):
        for e in range(NE_D):
            for tb in range(NTB):
                for j in range(NJ):
                    jg = (e * NTB + tb) * NJ + j
                    jb = jg % 2
                    scalar.wait_ge(sem["sPg"], L["sPg"] + 2 * jg + 1)
                    if jg >= 2:
                        scalar.wait_ge(sem["sVi"], L["sVi"] + jg - 1)
                    scalar.activation(silu_b[:, jb, :], psG[jb][:], AF.Silu).then_inc(sem["sAs"], 1)
    cnt["sAs"] += NJ * nTB

    @block.vector
    def _(vector):
        def emit_ystage(tbg):
            e2, tb2 = tbg // NTB, tbg % NTB
            for tc in range(4):
                tcg = tbg * 4 + tc
                ch = tb2 * 4 + tc
                b2 = tcg % 2
                if tcg >= 2:
                    vector.wait_ge(sem["sOUT0" if b2 == 0 else "sOUT1"],
                                   L["sOUT0" if b2 == 0 else "sOUT1"] + 16 * (tcg // 2))
                for nb in range(4):
                    yt = tcg * 4 + nb
                    yb = yt % 2
                    vector.wait_ge(sem["sPy"], L["sPy"] + yt + 1)
                    vector.tensor_scalar(ystage[:, b2, 512 * nb:512 * (nb + 1)], psY[yb][:],
                                         combine[:, ch, e2:e2 + 1], None, ALU.mult).then_inc(sem["sVy"], 1)

        for e in range(NE_D):
            for tb in range(NTB):
                tbg = e * NTB + tb
                for j in range(NJ):
                    jg = tbg * NJ + j
                    jb = jg % 2
                    vector.wait_ge(sem["sAs"], L["sAs"] + jg + 1)
                    vector.wait_ge(sem["sPg"], L["sPg"] + 2 * jg + 2)
                    if tbg >= 2 and j == 0:
                        vector.wait_ge(sem["sPy"], L["sPy"] + 16 * (tbg - 1))
                    vector.tensor_tensor(inter[:, tbg % 2, j, :], silu_b[:, jb, :], psU[jb][:], ALU.mult).then_inc(sem["sVi"], 1)
                if tbg >= 1:
                    emit_ystage(tbg - 1)
        emit_ystage(NE_D * NTB - 1)
    cnt["sVi"] += NJ * nTB
    cnt["sVy"] += 16 * nTB

    st.close()


def emit_phase_C(nc, block, sem, cnt, env):
    xnT_hbm, out_h, mh_hbm = env["xnT_hbm"], env["out_h"], env["mh_hbm"]
    wqT_h, wgT_h, bq_h, bg_h = env["wqT_h"], env["wgT_h"], env["bq_h"], env["bg_h"]
    memcat_h, dcc_out = env["memcat_h"], env["dcc_out"]
    identb, onesb = env["identb"], env["onesb"]

    st = ExitStack()
    sb = lambda n, s, d: st.enter_context(nc.sbuf_tensor(n, s, d))

    new_mem = sb("C_newmem", [128, H, 6, 129], BF16)
    mpc = sb("C_mpc", [128, 2, 4, 129], F32)
    dpc = sb("C_dpc", [128, 2, 512], F32)
    npc = sb("C_npc", [128, 2, 4], F32)
    if env.get("dbg"):
        env["dbg"]["Ct"] = {"new_mem": new_mem}

    L = {k: cnt[k] for k in cnt}

    @block.sync
    def _(sync):
        sync.wait_ge(sem["sCC"], L["sCC"])
        for t in range(24):
            g, fc = t // 6, t % 6
            b = t % 2
            if t >= 2:
                sync.wait_ge(sem["sVm"], L["sVm"] + t - 1)
            sync.dma_start(out=mpc[:, b, :, :],
                           in_=memcat_h[4 * g:4 * (g + 1), 128 * fc:128 * (fc + 1), :].rearrange("h p c -> p h c")
                           ).then_inc(sem["sXL0" if b == 0 else "sXL1"], 16)
            sync.dma_start(out=dpc[:, b, :],
                           in_=dcc_out[:, (g * 6 + fc) * 512:(g * 6 + fc + 1) * 512]).then_inc(sem["sXL0" if b == 0 else "sXL1"], 16)
            sync.dma_start(out=npc[:, b, :],
                           in_=dcc_out[:, 12288 + g * 24 + 4 * fc:12288 + g * 24 + 4 * (fc + 1)]).then_inc(sem["sXL0" if b == 0 else "sXL1"], 16)
    cnt["sXL0"] += 16 * 36
    cnt["sXL1"] += 16 * 36

    @block.vector
    def _(vector):
        for t in range(24):
            g, fc = t // 6, t % 6
            b = t % 2
            s_n = "sXL0" if b == 0 else "sXL1"
            vector.wait_ge(sem[s_n], L[s_n] + 48 * (t // 2 + 1))
            vector.tensor_tensor(new_mem[:, 4 * g:4 * (g + 1), fc, 0:128],
                                 mpc[:, b, :, 0:128],
                                 dpc[:, b, :].rearrange("p (h d) -> p h d", d=128), ALU.add)
            vector.tensor_tensor(new_mem[:, 4 * g:4 * (g + 1), fc, 128:129],
                                 mpc[:, b, :, 128:129],
                                 npc[:, b, :].rearrange("p (h one) -> p h one", one=1), ALU.add).then_inc(sem["sVm"], 1)
    cnt["sVm"] += 24

    _emit_C1(nc, block, sem, cnt, env, locals())
    _emit_C2(nc, block, sem, cnt, env, locals())

    st.close()


def _emit_C1(nc, block, sem, cnt, env, C):
    xnT_hbm, mh_hbm = env["xnT_hbm"], env["mh_hbm"]
    wqT_h, bq_h = env["wqT_h"], env["bq_h"]
    identb, onesb = env["identb"], env["onesb"]
    new_mem = C["new_mem"]

    st = ExitStack()
    sb = lambda n, s, d: st.enter_context(nc.sbuf_tensor(n, s, d))
    ps = lambda n, s, d: st.enter_context(nc.psum_tensor(n, s, d))
    wq = sb("C1_wq", [128, KT, DM], BF16)
    bqb = sb("C1_bq", [1, DM], BF16)
    xnt = sb("C1_xnt", [128, 2, KT, 128], BF16)
    qf = sb("C1_qf", [128, 2, DM], F32)
    rqC = sb("C1_rq", [128, 8, 256], BF16)
    fq8 = sb("C1_fq", [128, 8, 768], BF16)
    fqT = sb("C1_fqT", [128, 2, 768], BF16)
    dumpC = sb("C1_dump", [128, F], BF16)
    sq8 = sb("C1_sq", [128, 8], F32)
    sqr8 = sb("C1_sqr", [128, 8], F32)
    ne8 = sb("C1_ne", [128, 8], F32)
    den8 = sb("C1_den", [128, 8], F32)
    d8 = sb("C1_d8", [128, 8], F32)
    rc8 = sb("C1_rc", [128, 8], F32)
    mh = sb("C1_mh", [128, 2, DM], F32)
    psQ = [ps(f"C1_q{j}", [128, 512], F32) for j in range(2)]
    psT = [ps(f"C1_t{j}", [128, 768], BF16) for j in range(2)]
    psN2 = [ps(f"C1_n{j}", [128, 129], F32) for j in range(2)]

    L = {k: cnt[k] for k in cnt}

    @block.gpsimd
    def _(gpsimd):
        gpsimd.wait_ge(sem["sPg"], L["sPg"])
        gpsimd.wait_ge(sem["sPy"], L["sPy"])
        for q in range(4):
            gpsimd.dma_start(out=wq[:, 4 * q:4 * (q + 1), :],
                             in_=wqT_h[4 * q:4 * (q + 1)].rearrange("k p m -> p k m")).then_inc(sem["sGD"], 16)
        gpsimd.dma_start(out=bqb[:], in_=bq_h).then_inc(sem["sGD"], 16)
    cnt["sGD"] += 16 * 5
    nGD = cnt["sGD"]

    @block.sync
    def _(sync):
        for i in range(NCH):
            b = i % 2
            if i == 0:
                sync.wait_ge(sem["sPg"], L["sPg"])
                sync.wait_ge(sem["sPy"], L["sPy"])
            if i >= 2:
                sync.wait_ge(sem["sPq"], L["sPq"] + 4 * (i - 1))
            sync.dma_start(out=xnt[:, b, :, :],
                           in_=xnT_hbm[:, :, i * 128:(i + 1) * 128].rearrange("k p t -> p k t")
                           ).then_inc(sem["sXL0" if b == 0 else "sXL1"], 16)
    cnt["sXL0"] += 16 * (NCH // 2)
    cnt["sXL1"] += 16 * (NCH // 2)

    @block.tensor
    def _(tensor):
        tensor.wait_ge(sem["sGD"], nGD)
        tensor.wait_ge(sem["sVy"], L["sVy"])
        tensor.wait_ge(sem["sAs"], L["sAs"])
        tensor.wait_ge(sem["sVi"], L["sVi"])
        tensor.wait_ge(sem["sVm"], L["sVm"])
        for i in range(NCH):
            b = i % 2
            s_n = "sXL0" if b == 0 else "sXL1"
            tensor.wait_ge(sem[s_n], L[s_n] + 16 * (i // 2 + 1))
            for nq in range(4):
                if 4 * i + nq >= 2:
                    tensor.wait_ge(sem["sAq"], L["sAq"] + 4 * i + nq - 1)
                for kc in range(KT):
                    tensor.matmul(psQ[nq % 2][:], xnt[:, b, kc, :], wq[:, kc, 512 * nq:512 * (nq + 1)],
                                  start=(kc == 0), stop=False)
                tensor.matmul(psQ[nq % 2][:], onesb[:, :], bqb[:, 512 * nq:512 * (nq + 1)],
                              start=False, stop=True).then_inc(sem["sPq"], 1)
            for hg in range(2):
                hgg = 2 * i + hg
                tensor.wait_ge(sem["sVf"], L["sVf"] + hgg + 1)
                for s in range(9):
                    if s < 8:
                        hh = s
                        ht = hgg * 8 + hh
                        if ht >= 2:
                            tensor.wait_ge(sem["sAt"], L["sAt"] + ht - 1)
                        for j in range(6):
                            mm = tensor.transpose(psT[ht % 2][:, 128 * j:128 * (j + 1)],
                                                  fq8[:, hh, 128 * j:128 * (j + 1)], identb[:])
                        mm.then_inc(sem["sPt"], 1)
                    if s >= 1:
                        hh = s - 1
                        h = 8 * hg + hh
                        hb = hh % 2
                        ht = hgg * 8 + hh
                        tensor.wait_ge(sem["sAt"], L["sAt"] + ht + 1)
                        if ht >= 2:
                            tensor.wait_ge(sem["sVn"], L["sVn"] + ht - 1)
                        for fc in range(6):
                            mm = tensor.matmul(psN2[hh % 2][:], fqT[:, hb, 128 * fc:128 * (fc + 1)], new_mem[:, h, fc, :],
                                               start=(fc == 0), stop=(fc == 5))
                        mm.then_inc(sem["sPn"], 1)
    cnt["sPq"] += 4 * NCH
    cnt["sPt"] += 16 * NCH
    cnt["sPn"] += 16 * NCH

    @block.scalar
    def _(scalar):
        for i in range(NCH):
            b = i % 2
            for nq in range(4):
                scalar.wait_ge(sem["sPq"], L["sPq"] + 4 * i + nq + 1)
                if i >= 2 and nq == 0:
                    scalar.wait_ge(sem["sVf"], L["sVf"] + 2 * i - 2)
                scalar.copy(qf[:, b, 512 * nq:512 * (nq + 1)], psQ[nq % 2][:]).then_inc(sem["sAq"], 1)
            if i >= 1:
                bp = (i - 1) % 2
                scalar.wait_ge(sem["sVw"], L["sVw"] + 2 * i)
                scalar.dma_start(out=mh_hbm[128 * (i - 1):128 * i, :], in_=mh[:, bp, :]).then_inc(sem["sDD"], 16)
            for hg in range(2):
                hgg = 2 * i + hg
                if hgg >= 1:
                    scalar.wait_ge(sem["sVf"], L["sVf"] + hgg)
                scalar.activation(rqC[:, :, 0:128],
                                  qf[:, b, 1024 * hg:1024 * (hg + 1)].rearrange("p (h d) -> p h d", d=128), AF.Relu)
                scalar.activation(rqC[:, :, 128:256],
                                  qf[:, b, 1024 * hg:1024 * (hg + 1)].rearrange("p (h d) -> p h d", d=128),
                                  AF.Relu, bias=0.0, scale=-1.0).then_inc(sem["sAg"], 1)
                scalar.wait_ge(sem["sVq"], L["sVq"] + hgg + 1)
                scalar.activation(sqr8[:], sq8[:], AF.Sqrt).then_inc(sem["sAr"], 1)
                for hh in range(8):
                    ht = hgg * 8 + hh
                    scalar.wait_ge(sem["sPt"], L["sPt"] + ht + 1)
                    if ht >= 2:
                        scalar.wait_ge(sem["sPn"], L["sPn"] + ht - 1)
                    scalar.copy(fqT[:, hh % 2, :], psT[ht % 2][:]).then_inc(sem["sAt"], 1)
    cnt["sAq"] += 4 * NCH
    cnt["sAg"] += 2 * NCH
    cnt["sAr"] += 2 * NCH
    cnt["sAt"] += 16 * NCH

    @block.scalar
    def _(scalar):
        bp = (NCH - 1) % 2
        scalar.wait_ge(sem["sVw"], L["sVw"] + 2 * NCH)
        scalar.dma_start(out=mh_hbm[128 * (NCH - 1):128 * NCH, :], in_=mh[:, bp, :]).then_inc(sem["sDD"], 16)
    cnt["sDD"] += 16 * NCH

    vs = [L["sVs"]]

    @block.vector
    def _(vector):
        def vedge(res):
            vs[0] += 1
            res.then_inc(sem["sVs"], 1)
            vector.wait_ge(sem["sVs"], vs[0])

        for i in range(NCH):
            b = i % 2
            if i >= 2:
                vector.wait_ge(sem["sDD"], L["sDD"] + 16 * (i - 1))
            for hg in range(2):
                hgg = 2 * i + hg
                vector.wait_ge(sem["sAg"], L["sAg"] + hgg + 1)
                if hgg >= 1:
                    vector.wait_ge(sem["sPt"], L["sPt"] + 8 * hgg)
                for j in (1, 2, 3):
                    base = 256 * (j - 1)
                    vector.tensor_tensor(fq8[:, :, base:base + j], rqC[:, :, 0:j], rqC[:, :, 256 - j:256], ALU.mult)
                    res = vector.tensor_tensor(fq8[:, :, base + j:base + 256], rqC[:, :, j:256], rqC[:, :, 0:256 - j], ALU.mult)
                res.then_inc(sem["sVf"], 1)
                vector.wait_ge(sem["sVf"], L["sVf"] + hgg + 1)
                for hh in range(8):
                    res = vector.scalar_tensor_tensor(dumpC[:], fq8[:, hh, :], 1.0, fq8[:, hh, :],
                                                      ALU.mult, ALU.mult, accum_out=sq8[:, hh:hh + 1])
                vedge(res)
                vector.nop().then_inc(sem["sVq"], 1)
                vector.wait_ge(sem["sAr"], L["sAr"] + hgg + 1)
                vedge(vector.tensor_scalar(ne8[:], sqr8[:], 1e-12, 1e-8, ALU.max, ALU.mult))
                for hh in range(8):
                    vector.wait_ge(sem["sPn"], L["sPn"] + 16 * i + 8 * hg + hh + 1)
                    vector.tensor_copy(mh[:, b, 1024 * hg + 128 * hh:1024 * hg + 128 * (hh + 1)],
                                       psN2[hh % 2][:, 0:128])
                    res = vector.tensor_copy(den8[:, hh:hh + 1], psN2[hh % 2][:, 128:129])
                    res.then_inc(sem["sVn"], 1)
                vector.wait_ge(sem["sVn"], L["sVn"] + 16 * i + 8 * hg + 8)
                vedge(vector.tensor_tensor(d8[:], den8[:], ne8[:], ALU.add))
                vedge(vector.reciprocal(rc8[:], d8[:]))
                for hh in range(8):
                    res = vector.tensor_scalar(mh[:, b, 1024 * hg + 128 * hh:1024 * hg + 128 * (hh + 1)],
                                               mh[:, b, 1024 * hg + 128 * hh:1024 * hg + 128 * (hh + 1)],
                                               rc8[:, hh:hh + 1], None, ALU.mult)
                res.then_inc(sem["sVw"], 1)
    cnt["sVf"] += 2 * NCH
    cnt["sVq"] += 2 * NCH
    cnt["sVn"] += 16 * NCH
    cnt["sVw"] += 2 * NCH
    cnt["sVs"] = vs[0]

    st.close()


def _emit_C2(nc, block, sem, cnt, env, C):
    xnT_hbm, out_h, mh_hbm = env["xnT_hbm"], env["out_h"], env["mh_hbm"]
    wgT_h, bg_h = env["wgT_h"], env["bg_h"]
    onesb = env["onesb"]

    st = ExitStack()
    sb = lambda n, s, d: st.enter_context(nc.sbuf_tensor(n, s, d))
    ps = lambda n, s, d: st.enter_context(nc.psum_tensor(n, s, d))
    wg = sb("C2_wg", [128, KT, DM], BF16)
    bgb = sb("C2_bg", [1, DM], BF16)
    xnt = sb("C2_xnt", [128, 2, KT, 128], BF16)
    mhs = sb("C2_mh", [128, 2, DM], F32)
    gatef = sb("C2_gate", [128, 2, DM], F32)
    ctb = sb("C2_ctb", [128, 2, DM], F32)
    psG2 = [ps(f"C2_g{j}", [128, 512], F32) for j in range(2)]

    L = {k: cnt[k] for k in cnt}

    @block.gpsimd
    def _(gpsimd):
        gpsimd.wait_ge(sem["sPq"], L["sPq"])
        gpsimd.wait_ge(sem["sPn"], L["sPn"])
        for q in range(4):
            gpsimd.dma_start(out=wg[:, 4 * q:4 * (q + 1), :],
                             in_=wgT_h[4 * q:4 * (q + 1)].rearrange("k p m -> p k m")).then_inc(sem["sGD"], 16)
        gpsimd.dma_start(out=bgb[:], in_=bg_h).then_inc(sem["sGD"], 16)
        for i in range(NCH):
            b = i % 2
            gpsimd.wait_ge(sem["sVi"], L["sVi"] + i + 1)
            gpsimd.dma_start(out=out_h[128 * i:128 * (i + 1), :], in_=ctb[:, b, :],
                             accum_op=ALU.add).then_inc(sem["sOUT0" if b == 0 else "sOUT1"], 16)
    cnt["sGD"] += 16 * 5
    nGD = cnt["sGD"]
    cnt["sOUT0"] += 16 * (NCH // 2)
    cnt["sOUT1"] += 16 * (NCH // 2)

    @block.sync
    def _(sync):
        for i in range(NCH):
            b = i % 2
            if i == 0:
                sync.wait_ge(sem["sDD"], L["sDD"])
            if i >= 2:
                sync.wait_ge(sem["sPq"], L["sPq"] + 4 * (i - 1))
            sync.dma_start(out=xnt[:, b, :, :],
                           in_=xnT_hbm[:, :, i * 128:(i + 1) * 128].rearrange("k p t -> p k t")
                           ).then_inc(sem["sXL0" if b == 0 else "sXL1"], 16)
            if i >= 2:
                sync.wait_ge(sem["sVi"], L["sVi"] + i - 1)
            sync.dma_start(out=mhs[:, b, :], in_=mh_hbm[128 * i:128 * (i + 1), :]).then_inc(sem["sXL0" if b == 0 else "sXL1"], 16)
    cnt["sXL0"] += 32 * (NCH // 2)
    cnt["sXL1"] += 32 * (NCH // 2)

    @block.tensor
    def _(tensor):
        tensor.wait_ge(sem["sGD"], nGD)
        tensor.wait_ge(sem["sVn"], L["sVn"])
        tensor.wait_ge(sem["sAq"], L["sAq"])
        tensor.wait_ge(sem["sAt"], L["sAt"])
        for i in range(NCH):
            b = i % 2
            s_n = "sXL0" if b == 0 else "sXL1"
            tensor.wait_ge(sem[s_n], L[s_n] + 32 * (i // 2 + 1))
            for nq in range(4):
                if 4 * i + nq >= 2:
                    tensor.wait_ge(sem["sAq"], L["sAq"] + 4 * i + nq - 1)
                for kc in range(KT):
                    tensor.matmul(psG2[nq % 2][:], xnt[:, b, kc, :], wg[:, kc, 512 * nq:512 * (nq + 1)],
                                  start=(kc == 0), stop=False)
                tensor.matmul(psG2[nq % 2][:], onesb[:, :], bgb[:, 512 * nq:512 * (nq + 1)],
                              start=False, stop=True).then_inc(sem["sPq"], 1)
    cnt["sPq"] += 4 * NCH

    @block.scalar
    def _(scalar):
        for i in range(NCH):
            b = i % 2
            for nq in range(4):
                scalar.wait_ge(sem["sPq"], L["sPq"] + 4 * i + nq + 1)
                if i >= 1 and nq == 0:
                    scalar.wait_ge(sem["sVi"], L["sVi"] + i)
                scalar.activation(gatef[:, b, 512 * nq:512 * (nq + 1)], psG2[nq % 2][:], AF.Sigmoid).then_inc(sem["sAq"], 1)
    cnt["sAq"] += 4 * NCH

    @block.vector
    def _(vector):
        for i in range(NCH):
            b = i % 2
            s_n = "sXL0" if b == 0 else "sXL1"
            vector.wait_ge(sem["sAq"], L["sAq"] + 4 * (i + 1))
            vector.wait_ge(sem[s_n], L[s_n] + 32 * (i // 2 + 1))
            if i >= 2:
                vector.wait_ge(sem["sOUT0" if b == 0 else "sOUT1"],
                               L["sOUT0" if b == 0 else "sOUT1"] + 16 * (i // 2))
            vector.tensor_tensor(ctb[:, b, :], gatef[:, b, :], mhs[:, b, :], ALU.mult).then_inc(sem["sVi"], 1)
    cnt["sVi"] += NCH

    st.close()


def emit_dumps(nc, block, sem, cnt, env, dbg, phases):
    xnT_hbm = env["xnT_hbm"]
    combine = env["combine"]

    @block.sync
    def _(sync):
        base = cnt["sLD"]
        n = 0
        sync.wait_ge(sem["sXW0"], cnt["sXW0"])
        sync.wait_ge(sem["sXW1"], cnt["sXW1"])
        if "xnT" in dbg:
            for k in range(KT):
                sync.dma_start(out=dbg["xnT"][k], in_=xnT_hbm[k]).then_inc(sem["sLD"], 16)
                n += 16
        if "dccout" in dbg and "B" in phases:
            sync.wait_ge(sem["sCC"], cnt["sCC"])
            sync.dma_start(out=dbg["dccout"], in_=env["dcc_out"][:, :]).then_inc(sem["sLD"], 16)
            sync.dma_start(out=dbg["dccin"], in_=env["dcc_in"][:, :]).then_inc(sem["sLD"], 16)
            n += 32
            if "bb" in dbg:
                sync.dma_start(out=dbg["bb"], in_=dbg["bb_sb"][:]).then_inc(sem["sLD"], 16)
                n += 16
            if "fTm" in dbg and dbg.get("Bt") is not None:
                bt = dbg["Bt"]
                sync.dma_start(out=dbg["fTm"], in_=bt["fTm"][:]).then_inc(sem["sLD"], 16)
                sync.dma_start(out=dbg["vmb"][:, :, 0:128], in_=bt["v_sb"][:]).then_inc(sem["sLD"], 16)
                sync.dma_start(out=dbg["vmb"][:, :, 128:640], in_=bt["mb_sb"][:]).then_inc(sem["sLD"], 16)
                n += 48
        if "combine" in dbg:
            sync.dma_start(out=dbg["combine"], in_=combine[:]).then_inc(sem["sLD"], 16)
            n += 16
        sync.wait_ge(sem["sLD"], base + n)
        cnt["sLD"] += n


def _prep_inputs(inputs):
    f32 = np.float32
    c = np.ascontiguousarray
    d = {}
    d["wqT"] = c(inputs["L0_attn_Wq"].T.reshape(KT, 128, DM).astype(f32, copy=False))
    d["wgT"] = c(inputs["L0_attn_Wg"].T.reshape(KT, 128, DM).astype(f32, copy=False))
    d["wmbT"] = c(inputs["L0_attn_Wmb"].T.reshape(KT, 128, DM).astype(f32, copy=False))
    d["wkT"] = c(inputs["L0_attn_Wk"].T.reshape(KT, 128, 512).astype(f32, copy=False))
    d["wvT"] = c(inputs["L0_attn_Wv"].T.reshape(KT, 128, 512).astype(f32, copy=False))
    d["bq"] = c(inputs["L0_attn_bq"].reshape(1, DM).astype(f32, copy=False))
    d["bg"] = c(inputs["L0_attn_bg"].reshape(1, DM).astype(f32, copy=False))
    d["bmb"] = c(inputs["L0_attn_bmb"].reshape(1, DM).astype(f32, copy=False))
    d["bk"] = c(inputs["L0_attn_bk"].reshape(1, 512).astype(f32, copy=False))
    d["bv"] = c(inputs["L0_attn_bv"].reshape(1, 512).astype(f32, copy=False))
    mem = np.asarray(inputs["L0_attn_memory"], f32)
    nrm = np.asarray(inputs["L0_attn_norm"], f32)
    d["memcat"] = c(np.concatenate([mem, nrm[:, :, None]], axis=2))
    d["normr"] = c(nrm)
    d["gwT"] = c(inputs["L0_ffn_gate_w"].T.reshape(KT, 128, NE).astype(f32, copy=False))
    gup = np.asarray(inputs["L0_ffn_gate_up_proj"], f32)
    perm = np.empty(I2, np.int64)
    for j in range(II // 128):
        perm[256 * j:256 * j + 128] = np.arange(128 * j, 128 * (j + 1))
        perm[256 * j + 128:256 * j + 256] = II + np.arange(128 * j, 128 * (j + 1))
    d["wgu"] = c(gup[:, :, perm])
    d["wdn"] = c(np.asarray(inputs["L0_ffn_down_proj"], f32))
    d["identf"] = np.eye(128, dtype=f32)
    d["identb"] = np.eye(128, dtype=ml_dtypes.bfloat16)
    d["onesb"] = np.ones((1, 128), dtype=ml_dtypes.bfloat16)
    return d


def kernel(**inputs):
    shared = _prep_inputs(inputs)
    x = np.ascontiguousarray(np.asarray(inputs["L0_attn_hidden_states"], np.float32).reshape(B * S, DM))

    nc = build_program()
    in_maps = []
    for c in range(NCORES):
        m = dict(shared)
        m["x"] = x[c * T:(c + 1) * T]
        in_maps.append(m)
    res = run_bass_kernel_spmd(nc, in_maps, core_ids=list(range(NCORES)))
    out = np.concatenate([np.asarray(r["out"]) for r in res.results], axis=0)
    return out.reshape(B, S, DM)


# version 14 (final)
# speedup vs baseline: 1.1360x; 1.0068x over previous
import math
from contextlib import ExitStack

import numpy as np
import ml_dtypes

import concourse.bass as bass
import concourse.bass_utils as _bass_utils
import concourse.mybir as mybir
from concourse.bass_utils import run_bass_kernel_spmd

import os as _os
if _os.environ.get("KB_NO_LDWOPT", "") == "":
    _LDW_ENABLE = True
else:
    _LDW_ENABLE = False
if _LDW_ENABLE and not getattr(_bass_utils, "_ldw_opt_patched", False):
    _orig_run_command = _bass_utils.run_command

    def _run_command_ldw(argv, **kw):
        argv = ["--enable-ldw-opt=true" if a == "--enable-ldw-opt=false" else a for a in argv]
        return _orig_run_command(argv, **kw)

    _bass_utils.run_command = _run_command_ldw
    _bass_utils._ldw_opt_patched = True

F32 = mybir.dt.float32
BF16 = mybir.dt.bfloat16
AF = mybir.ActivationFunctionType
ALU = mybir.AluOpType
AX = mybir.AxisListType

NCORES = 8
B, S, DM = 16, 2048, 2048
T = (B // NCORES) * S
NCH = T // 128
KT = DM // 128
H, HKV, HD, F = 16, 4, 128, 768
NE, I2, II = 16, 2048, 1024
EPS = 1e-8

DEV_PHASES = ("A", "B", "D", "C")
DEV_DUMP = False


def _cdma(q, out, in_):
    return q.dma_start(out=out, in_=in_)


def build_program(phases=DEV_PHASES, dump=DEV_DUMP):
    nc = bass.Bass("TRN2", target_bir_lowering=False, debug=False)

    x_h = nc.dram_tensor("x", [T, DM], F32, kind="ExternalInput").ap()
    out_h = nc.dram_tensor("out", [T, DM], F32, kind="ExternalOutput").ap()

    wqT_h = nc.dram_tensor("wqT", [KT, 128, DM], F32, kind="ExternalInput").ap()
    wgT_h = nc.dram_tensor("wgT", [KT, 128, DM], F32, kind="ExternalInput").ap()
    wmbT_h = nc.dram_tensor("wmbT", [KT, 128, DM], F32, kind="ExternalInput").ap()
    wkT_h = nc.dram_tensor("wkT", [KT, 128, 512], F32, kind="ExternalInput").ap()
    wvT_h = nc.dram_tensor("wvT", [KT, 128, 512], F32, kind="ExternalInput").ap()
    bq_h = nc.dram_tensor("bq", [1, DM], F32, kind="ExternalInput").ap()
    bg_h = nc.dram_tensor("bg", [1, DM], F32, kind="ExternalInput").ap()
    bmb_h = nc.dram_tensor("bmb", [1, DM], F32, kind="ExternalInput").ap()
    bk_h = nc.dram_tensor("bk", [1, 512], F32, kind="ExternalInput").ap()
    bv_h = nc.dram_tensor("bv", [1, 512], F32, kind="ExternalInput").ap()
    memcat_h = nc.dram_tensor("memcat", [H, F, 129], F32, kind="ExternalInput").ap()
    normr_h = nc.dram_tensor("normr", [H, F], F32, kind="ExternalInput").ap()
    gwT_h = nc.dram_tensor("gwT", [KT, 128, NE], F32, kind="ExternalInput").ap()
    wgu_h = nc.dram_tensor("wgu", [NE, DM, I2], F32, kind="ExternalInput").ap()
    wdn_h = nc.dram_tensor("wdn", [NE, II, DM], F32, kind="ExternalInput").ap()
    identf_h = nc.dram_tensor("identf", [128, 128], F32, kind="ExternalInput").ap()
    identb_h = nc.dram_tensor("identb", [128, 128], BF16, kind="ExternalInput").ap()
    onesb_h = nc.dram_tensor("onesb", [1, 128], BF16, kind="ExternalInput").ap()

    xnT_hbm = nc.dram_tensor("xnT_hbm", [KT, 128, T], BF16).ap()
    mh_hbm = nc.dram_tensor("mh_hbm", [T, DM], F32).ap()
    dcc_in = nc.dram_tensor("dcc_in", [128, 12384], F32)
    dcc_out = nc.dram_tensor("dcc_out", [128, 12384], F32)

    dbg = {}
    if dump:
        dbg["xnT"] = nc.dram_tensor("dbg_xnT", [KT, 128, T], BF16, kind="ExternalOutput").ap()
        dbg["combine"] = nc.dram_tensor("dbg_combine", [128, NCH, NE], F32, kind="ExternalOutput").ap()
        dbg["newmem"] = nc.dram_tensor("dbg_newmem", [128, H, 6, 129], BF16, kind="ExternalOutput").ap()
        dbg["dccout"] = nc.dram_tensor("dbg_dccout", [128, 12384], F32, kind="ExternalOutput").ap()
        dbg["dccin"] = nc.dram_tensor("dbg_dccin", [128, 12384], F32, kind="ExternalOutput").ap()
        dbg["bb"] = nc.dram_tensor("dbg_bb", [128, 1043], F32, kind="ExternalOutput").ap()
        dbg["bb_sb"] = nc.alloc_sbuf_tensor("dbg_bb_s", [128, 1043], F32)
        dbg["fTm"] = nc.dram_tensor("dbg_fTm", [128, NCH, F], BF16, kind="ExternalOutput").ap()
        dbg["vmb"] = nc.dram_tensor("dbg_vmb", [128, NCH, 640], BF16, kind="ExternalOutput").ap()

    identf = nc.alloc_sbuf_tensor("identf_s", [128, 128], F32)
    identb = nc.alloc_sbuf_tensor("identb_s", [128, 128], BF16)
    onesb = nc.alloc_sbuf_tensor("onesb_s", [1, 128], BF16)
    combine = nc.alloc_sbuf_tensor("combine", [128, NCH, NE], F32)

    sem_names = [
        "sLD", "sGD", "sOUT", "sXW", "sXR",
        "sXL0", "sXL1", "sGW", "sXW0", "sXW1", "sOUT0", "sOUT1",
        "sVs", "sVq", "sAr", "sVxa", "sVxn", "sVxt", "sVro",
        "sPt", "sPr", "sAe", "sAp",
        "sB", "sCC", "sDD",
        "sPk", "sPv", "sPm", "sPn", "sPd", "sVm", "sVv", "sVf", "sVw", "sAq", "sAg",
        "sPq", "sPg", "sPu", "sPy", "sVi", "sVy", "sAs", "sPB", "sAB", "sVN", "sPN", "sAt", "sVn", "sGu", "sGw",
    ]
    sem = {n: nc.alloc_semaphore(n) for n in sem_names}
    cnt = {n: 0 for n in sem_names}

    ENGS = ["sync", "gpsimd", "vector", "scalar", "tensor"]

    def barrier(block, nb):
        target = 5 * nb
        for name in ENGS:
            eng = getattr(block, name)

            @eng
            def _(e, target=target):
                e.nop().then_inc(sem["sB"], 1)
                e.wait_ge(sem["sB"], target)

    with nc.Block() as block:
        @block.sync
        def _(sync):
            sync.dma_start(out=identf[:], in_=identf_h).then_inc(sem["sLD"], 16)
            sync.dma_start(out=identb[:], in_=identb_h).then_inc(sem["sLD"], 16)
            sync.dma_start(out=onesb[:], in_=onesb_h).then_inc(sem["sLD"], 16)
        cnt["sLD"] += 48

        nbar = 0
        if "A" in phases:
            emit_phase_A(nc, block, sem, cnt, locals())
        nbar += 1
        barrier(block, nbar)
        if "B" in phases:
            emit_phase_B(nc, block, sem, cnt, locals())
        nbar += 1
        barrier(block, nbar)
        if "D" in phases:
            emit_phase_D(nc, block, sem, cnt, locals())
        nbar += 1
        barrier(block, nbar)
        if "C" in phases:
            emit_phase_C(nc, block, sem, cnt, locals())
        nbar += 1
        barrier(block, nbar)

        n_out = cnt["sOUT"]
        n_out0 = cnt["sOUT0"]
        n_out1 = cnt["sOUT1"]

        @block.gpsimd
        def _(gpsimd):
            gpsimd.wait_ge(sem["sOUT"], n_out)
            gpsimd.wait_ge(sem["sOUT0"], n_out0)
            gpsimd.wait_ge(sem["sOUT1"], n_out1)

        if dump:
            emit_dumps(nc, block, sem, cnt, locals(), dbg, phases)

    return nc


def emit_phase_A(nc, block, sem, cnt, env):
    dbg = env.get("dbg", {})
    x_h, out_h, xnT_hbm = env["x_h"], env["out_h"], env["xnT_hbm"]
    gwT_h = env["gwT_h"]
    identf, combine = env["identf"], env["combine"]

    st = ExitStack()
    sb = lambda n, s, d: st.enter_context(nc.sbuf_tensor(n, s, d))
    ps = lambda n, s, d: st.enter_context(nc.psum_tensor(n, s, d))

    xch = sb("A_xch", [128, 2, DM], F32)
    xnf = sb("A_xnf", [128, 2, DM], F32)
    xa = sb("A_xa", [128, 2, DM], F32)
    dumpA = sb("A_dump", [128, DM], F32)
    s1 = sb("A_s1", [128, 1], F32)
    s2 = sb("A_s2", [128, 1], F32)
    mcol = sb("A_m", [128, 1], F32)
    msq = sb("A_msq", [128, 1], F32)
    s2n = sb("A_s2n", [128, 1], F32)
    ve = sb("A_ve", [128, 1], F32)
    rv = sb("A_rv", [128, 1], F32)
    rstd = sb("A_rstd", [128, 1], F32)
    xnTf = sb("A_xnTf", [128, KT, 128], F32)
    xnT16 = sb("A_xnT16", [128, 2, KT, 128], BF16)
    gwT = sb("A_gwT", [128, KT, NE], F32)
    pr = sb("A_pr", [128, NE], F32)
    m1 = sb("A_m1", [128, 1], F32)
    m2 = sb("A_m2", [128, 1], F32)
    msk = sb("A_msk", [128, NE], F32)
    pm = sb("A_pm", [128, NE], F32)
    wsum = sb("A_wsum", [128, 1], F32)
    winv = sb("A_winv", [128, 1], F32)
    c1t = sb("A_c1t", [128, NE], F32)

    trp = [ps(f"A_tr{j}", [128, 128], F32) for j in range(4)]
    rps = [ps(f"A_rt{j}", [128, NE], F32) for j in range(2)]

    L = {k: cnt[k] for k in cnt}

    @block.sync
    def _(sync):
        sync.dma_start(out=gwT[:], in_=gwT_h.rearrange("k p e -> p k e")).then_inc(sem["sGW"], 16)
        for i in range(NCH):
            b = i % 2
            if i >= 2:
                sync.wait_ge(sem["sVxa"], L["sVxa"] + i - 1)
            sync.dma_start(out=xch[:, b, :], in_=x_h[i * 128:(i + 1) * 128, :]).then_inc(sem["sXL0" if b == 0 else "sXL1"], 16)
    cnt["sGW"] += 16
    cnt["sXL0"] += 16 * (NCH // 2)
    cnt["sXL1"] += 16 * (NCH // 2)

    vs = [L["sVs"]]

    @block.vector
    def _(vector):
        def vedge(res):
            vs[0] += 1
            res.then_inc(sem["sVs"], 1)
            vector.wait_ge(sem["sVs"], vs[0])

        for i in range(NCH):
            b = i % 2
            vector.wait_ge(sem["sXL0" if i % 2 == 0 else "sXL1"], L["sXL0" if i % 2 == 0 else "sXL1"] + 16 * (i // 2 + 1))
            vector.reduce_sum(s1[:], xch[:, b, :], AX.X)
            vedge(vector.scalar_tensor_tensor(dumpA[:], xch[:, b, :], 1.0, xch[:, b, :],
                                              ALU.mult, ALU.mult, accum_out=s2[:]))
            vector.tensor_scalar(mcol[:], s1[:], 1.0 / DM, None, ALU.mult)
            vedge(vector.tensor_scalar(s2n[:], s2[:], 1.0 / DM, 1e-5, ALU.mult, ALU.add))
            vedge(vector.scalar_tensor_tensor(ve[:], mcol[:], mcol[:], s2n[:], ALU.mult, ALU.subtract))
            vector.reciprocal(rv[:], ve[:]).then_inc(sem["sVq"], 1)
            vector.wait_ge(sem["sAr"], L["sAr"] + i + 1)
            if i >= 2:
                vector.wait_ge(sem["sPt"], L["sPt"] + 16 * (i - 1))
            vector.tensor_scalar(xnf[:, b, :], xch[:, b, :], mcol[:], rstd[:], ALU.subtract, ALU.mult).then_inc(sem["sVxn"], 1)
            if i >= 2:
                vector.wait_ge(sem["sOUT0" if i % 2 == 0 else "sOUT1"],
                               L["sOUT0" if i % 2 == 0 else "sOUT1"] + 16 * (i // 2))
            vector.tensor_tensor(xa[:, b, :], xch[:, b, :], xnf[:, b, :], ALU.add).then_inc(sem["sVxa"], 1)
            vector.wait_ge(sem["sAe"], L["sAe"] + 16 * (i + 1))
            if i >= 2:
                vector.wait_ge(sem["sXW0" if i % 2 == 0 else "sXW1"],
                               L["sXW0" if i % 2 == 0 else "sXW1"] + 16 * (i // 2))
            vector.tensor_copy(xnT16[:, b, :, :], xnTf[:, :, :]).then_inc(sem["sVxt"], 1)
            vector.wait_ge(sem["sAp"], L["sAp"] + i + 1)
            vedge(vector.reduce_max(m1[:], pr[:], AX.X))
            vedge(vector.tensor_scalar(msk[:], pr[:], m1[:], None, ALU.is_ge))
            vedge(vector.scalar_tensor_tensor(pm[:], msk[:], -1e30, pr[:], ALU.mult, ALU.add))
            vedge(vector.reduce_max(m2[:], pm[:], AX.X))
            vedge(vector.tensor_tensor(wsum[:], m1[:], m2[:], ALU.add))
            vedge(vector.reciprocal(winv[:], wsum[:]))
            vedge(vector.tensor_scalar(msk[:], pr[:], m2[:], None, ALU.is_ge))
            vedge(vector.tensor_tensor(c1t[:], msk[:], pr[:], ALU.mult))
            vector.tensor_scalar(combine[:, i, :], c1t[:], winv[:], None, ALU.mult).then_inc(sem["sVro"], 1)
    cnt["sVs"] = vs[0]
    cnt["sVq"] += NCH
    cnt["sVxa"] += NCH
    cnt["sVxn"] += NCH
    cnt["sVxt"] += NCH
    cnt["sVro"] += NCH

    @block.scalar
    def _(scalar):
        for i in range(NCH):
            scalar.wait_ge(sem["sVq"], L["sVq"] + i + 1)
            scalar.activation(rstd[:], rv[:], AF.Sqrt, bias=0.0, scale=-1.0).then_inc(sem["sAr"], 1)
            for k in range(KT):
                scalar.wait_ge(sem["sPt"], L["sPt"] + 16 * i + k + 1)
                if i > 0 and k == 0:
                    scalar.wait_ge(sem["sPr"], L["sPr"] + i)
                    scalar.wait_ge(sem["sVxt"], L["sVxt"] + i)
                scalar.copy(xnTf[:, k, :], trp[k % 4][:]).then_inc(sem["sAe"], 1)
            scalar.wait_ge(sem["sPr"], L["sPr"] + i + 1)
            if i >= 1:
                scalar.wait_ge(sem["sVro"], L["sVro"] + i)
            scalar.activation(pr[:], rps[i % 2][:], AF.Sigmoid).then_inc(sem["sAp"], 1)
    cnt["sAr"] += NCH
    cnt["sAe"] += 16 * NCH
    cnt["sAp"] += NCH

    @block.tensor
    def _(tensor):
        tensor.wait_ge(sem["sLD"], L["sLD"])
        tensor.wait_ge(sem["sGW"], L["sGW"] + 16)
        for i in range(NCH):
            b = i % 2
            tensor.wait_ge(sem["sVxn"], L["sVxn"] + i + 1)
            for k in range(KT):
                t = 16 * i + k
                if t >= 4:
                    tensor.wait_ge(sem["sAe"], L["sAe"] + t - 3)
                tensor.transpose(trp[k % 4][:], xnf[:, b, 128 * k:128 * (k + 1)], identf[:]).then_inc(sem["sPt"], 1)
            tensor.wait_ge(sem["sAe"], L["sAe"] + 16 * (i + 1))
            if i >= 2:
                tensor.wait_ge(sem["sAp"], L["sAp"] + i - 1)
            for k in range(KT):
                mm = tensor.matmul(rps[i % 2][:], xnTf[:, k, :], gwT[:, k, :],
                                   start=(k == 0), stop=(k == KT - 1))
            mm.then_inc(sem["sPr"], 1)
    cnt["sPt"] += 16 * NCH
    cnt["sPr"] += NCH

    @block.gpsimd
    def _(gpsimd):
        for i in range(NCH):
            b = i % 2
            gpsimd.wait_ge(sem["sVxa"], L["sVxa"] + i + 1)
            gpsimd.dma_start(out=out_h[i * 128:(i + 1) * 128, :], in_=xa[:, b, :]).then_inc(sem["sOUT0" if b == 0 else "sOUT1"], 16)
            gpsimd.wait_ge(sem["sVxt"], L["sVxt"] + i + 1)
            gpsimd.dma_start(out=xnT_hbm[:, :, i * 128:(i + 1) * 128].rearrange("k p t -> p k t"),
                             in_=xnT16[:, b, :, :]).then_inc(sem["sXW0" if b == 0 else "sXW1"], 16)
    cnt["sOUT0"] += 16 * (NCH // 2)
    cnt["sOUT1"] += 16 * (NCH // 2)
    cnt["sXW0"] += 16 * (NCH // 2)
    cnt["sXW1"] += 16 * (NCH // 2)

    st.close()


def emit_phase_B(nc, block, sem, cnt, env):
    xnT_hbm = env["xnT_hbm"]
    wkT_h, wvT_h, wmbT_h = env["wkT_h"], env["wvT_h"], env["wmbT_h"]
    bk_h, bv_h, bmb_h = env["bk_h"], env["bv_h"], env["bmb_h"]
    memcat_h, normr_h = env["memcat_h"], env["normr_h"]
    dcc_in, dcc_out = env["dcc_in"], env["dcc_out"]
    identb, onesb = env["identb"], env["onesb"]

    st = ExitStack()
    sb = lambda n, s, d: st.enter_context(nc.sbuf_tensor(n, s, d))

    fTm = sb("B_fTm", [128, NCH, F], BF16)
    fT = sb("B_fT", [128, 6, T], BF16)
    v_sb = sb("B_v", [128, NCH, HD], BF16)
    mb_sb = sb("B_mb", [128, NCH, 512], BF16)
    s_sb = sb("B_s", [128, NCH], F32)
    nmax_sb = sb("B_nmax", [128, NCH], F32)
    invn_sb = sb("B_invn", [128, NCH], F32)
    coefi_sb = sb("B_coefi", [128, NCH, 4], BF16)
    wk_g = sb("B_wk", [128, KT, 128], BF16)
    wv_g = sb("B_wv", [128, KT, 128], BF16)
    wmb_g = sb("B_wmb", [128, KT, 512], BF16)
    bk_g = sb("B_bk", [1, 128], BF16)
    bv_g = sb("B_bv", [1, 128], BF16)
    bmb_g = sb("B_bmb", [1, 512], BF16)
    mem_rhs = sb("B_memrhs", [128, 4, 6, 129], BF16)
    normb = sb("B_normb", [1, 4, F], BF16)
    norm_bc = sb("B_normbc", [128, 4, F], BF16)
    xnt = sb("B_xnt", [128, 2, KT, 128], BF16)
    rq = sb("B_rq", [128, 2, 256], BF16)
    dumpB = sb("B_dump", [128, F], BF16)
    sqn = sb("B_sqn", [128, 1], F32)
    den4 = sb("B_den4", [128, 4], F32)
    ne1 = sb("B_ne1", [128, 1], F32)
    d4 = sb("B_d4", [128, 4], F32)
    rec4 = sb("B_rec4", [128, 4], F32)
    nri4 = sb("B_nri4", [128, 4], F32)
    dr4 = sb("B_dr4", [128, 4], F32)
    msq1 = sb("B_msq1", [128, 1], F32)
    rmsq = sb("B_rmsq", [128, 1], F32)
    co4 = sb("B_co4", [128, 4], F32)
    co4b = sb("B_co4b", [128, 4], F32)
    mvs = sb("B_mvs", [128, 512], F32)
    wmv4 = sb("B_wmv4", [128, 2, 512], BF16)
    dbounce = sb("B_dbounce", [128, 2, 512], F32)
    dnbounce = sb("B_dnb", [128, 24], F32)

    import os
    glist = range(int(os.environ.get('KB_GONLY', HKV)))
    for g in glist:
        _emit_Ba(nc, block, sem, cnt, env, locals(), g)
        _emit_Bb(nc, block, sem, cnt, env, locals(), g)

    if env.get("dbg"):
        env["dbg"]["Bt"] = {"fTm": fTm, "v_sb": v_sb, "mb_sb": mb_sb}

    nDD = cnt["sDD"]

    @block.gpsimd
    def _(gpsimd):
        gpsimd.wait_ge(sem["sDD"], nDD)
        gpsimd.collective_compute(
            "AllReduce", ALU.add,
            replica_groups=[list(range(NCORES))],
            ins=[dcc_in.ap().opt()],
            outs=[dcc_out.ap().opt()],
        ).then_inc(sem["sCC"], 1)
    cnt["sCC"] += 1

    if not env.get("dbg"):
        st.close()


def _emit_Ba(nc, block, sem, cnt, env, B, g):
    wkT_h, wvT_h, wmbT_h = env["wkT_h"], env["wvT_h"], env["wmbT_h"]
    bk_h, bv_h, bmb_h = env["bk_h"], env["bv_h"], env["bmb_h"]
    memcat_h, normr_h = env["memcat_h"], env["normr_h"]
    xnT_hbm = env["xnT_hbm"]
    identb, onesb = env["identb"], env["onesb"]
    (fTm, fT, v_sb, mb_sb, s_sb, nmax_sb, invn_sb, wk_g, wv_g, wmb_g, bk_g, bv_g,
     bmb_g, mem_rhs, normb, norm_bc, xnt, rq, dumpB, sqn) = (
        B["fTm"], B["fT"], B["v_sb"], B["mb_sb"], B["s_sb"], B["nmax_sb"], B["invn_sb"],
        B["wk_g"], B["wv_g"], B["wmb_g"], B["bk_g"], B["bv_g"], B["bmb_g"],
        B["mem_rhs"], B["normb"], B["norm_bc"], B["xnt"], B["rq"], B["dumpB"], B["sqn"])
    del B

    st = ExitStack()
    ps = lambda n, s, d: st.enter_context(nc.psum_tensor(n, s, d))
    psK = [ps(f"Ba{g}_k{j}", [128, 128], F32) for j in range(2)]
    psV = [ps(f"Ba{g}_v{j}", [128, 128], F32) for j in range(2)]
    psM = [ps(f"Ba{g}_m{j}", [128, 512], F32) for j in range(2)]
    psT = [ps(f"Ba{g}_t{j}", [128, 768], BF16) for j in range(2)]

    L = {k: cnt[k] for k in cnt}

    @block.gpsimd
    def _(gpsimd):
        if g == 0:
            gpsimd.wait_ge(sem["sVro"], L["sVro"])
        gpsimd.dma_start(out=wk_g[:], in_=wkT_h[:, :, g * 128:(g + 1) * 128].rearrange("k p n -> p k n")).then_inc(sem["sGD"], 16)
        gpsimd.dma_start(out=wv_g[:], in_=wvT_h[:, :, g * 128:(g + 1) * 128].rearrange("k p n -> p k n")).then_inc(sem["sGD"], 16)
        gpsimd.dma_start(out=wmb_g[:], in_=wmbT_h[:, :, g * 512:(g + 1) * 512].rearrange("k p n -> p k n")).then_inc(sem["sGD"], 16)
        gpsimd.dma_start(out=bk_g[:], in_=bk_h[:, g * 128:(g + 1) * 128]).then_inc(sem["sGD"], 16)
        gpsimd.dma_start(out=bv_g[:], in_=bv_h[:, g * 128:(g + 1) * 128]).then_inc(sem["sGD"], 16)
        gpsimd.dma_start(out=bmb_g[:], in_=bmb_h[:, g * 512:(g + 1) * 512]).then_inc(sem["sGD"], 16)
        gpsimd.dma_start(out=mem_rhs[:], in_=memcat_h[4 * g:4 * (g + 1)].rearrange("h (fc p) c -> p h fc c", p=128)).then_inc(sem["sGD"], 16)
        for h in range(4):
            gpsimd.dma_start(out=normb[:, h, :], in_=normr_h[4 * g + h]).then_inc(sem["sGD"], 16)
    cnt["sGD"] += 16 * 11
    nGD = cnt["sGD"]

    @block.sync
    def _(sync):
        for i in range(NCH):
            b = i % 2
            if i >= 2:
                sync.wait_ge(sem["sPB"], L["sPB"] + 4 * (i - 2) + 3)
            else:
                sync.wait_ge(sem["sPB"], L["sPB"])
            if g == 0 and i == 0:
                sync.wait_ge(sem["sXW0"], L["sXW0"])
                sync.wait_ge(sem["sXW1"], L["sXW1"])
            sync.dma_start(out=xnt[:, b, :, :],
                           in_=xnT_hbm[:, :, i * 128:(i + 1) * 128].rearrange("k p t -> p k t")
                           ).then_inc(sem["sXL0" if b == 0 else "sXL1"], 16)
    cnt["sXL0"] += 16 * (NCH // 2)
    cnt["sXL1"] += 16 * (NCH // 2)

    @block.tensor
    def _(tensor):
        tensor.wait_ge(sem["sGD"], nGD)
        tensor.wait_ge(sem["sVy"], L["sVy"])
        tensor.wait_ge(sem["sAp"], L["sAp"])
        tensor.wait_ge(sem["sAe"], L["sAe"])
        for h in range(4):
            tensor.matmul(psM[0][:, 0:512], onesb[:, :], normb[:, h, 0:512], start=True, stop=True)
            tensor.matmul(psM[1][:, 0:256], onesb[:, :], normb[:, h, 512:768], start=True, stop=True).then_inc(sem["sPn"], 1)
            tensor.wait_ge(sem["sAg"], L["sAg"] + h + 1)
        for i in range(NCH):
            b = i % 2
            tensor.wait_ge(sem["sXL0" if b == 0 else "sXL1"], L["sXL0" if b == 0 else "sXL1"] + 16 * (i // 2 + 1))
            if i >= 2:
                tensor.wait_ge(sem["sAB"], L["sAB"] + i - 1)
                tensor.wait_ge(sem["sVv"], L["sVv"] + i - 1)
                tensor.wait_ge(sem["sVm"], L["sVm"] + i - 1)
            for k in range(KT):
                tensor.matmul(psK[b][:], xnt[:, b, k, :], wk_g[:, k, :], start=(k == 0), stop=False)
            tensor.matmul(psK[b][:], onesb[:, :], bk_g[:], start=False, stop=True).then_inc(sem["sPB"], 1)
            for k in range(KT):
                tensor.matmul(psV[b][:], xnt[:, b, k, :], wv_g[:, k, :], start=(k == 0), stop=False)
            tensor.matmul(psV[b][:], onesb[:, :], bv_g[:], start=False, stop=True).then_inc(sem["sPB"], 1)
            for k in range(KT):
                tensor.matmul(psM[b][:], xnt[:, b, k, :], wmb_g[:, k, :], start=(k == 0), stop=False)
            tensor.matmul(psM[b][:], onesb[:, :], bmb_g[:], start=False, stop=True).then_inc(sem["sPB"], 1)
            tensor.wait_ge(sem["sVf"], L["sVf"] + i + 1)
            if i >= 2:
                tensor.wait_ge(sem["sAq"], L["sAq"] + i - 1)
            for j in range(6):
                mm = tensor.transpose(psT[i % 2][:, 128 * j:128 * (j + 1)],
                                      fTm[:, i, 128 * j:128 * (j + 1)], identb[:])
            mm.then_inc(sem["sPB"], 1)
    cnt["sPn"] += 4
    cnt["sPB"] += 4 * NCH

    @block.scalar
    def _(scalar):
        for h in range(4):
            scalar.wait_ge(sem["sPn"], L["sPn"] + h + 1)
            scalar.copy(norm_bc[:, h, 0:512], psM[0][:, 0:512])
            scalar.copy(norm_bc[:, h, 512:768], psM[1][:, 0:256]).then_inc(sem["sAg"], 1)
        for i in range(NCH):
            b = i % 2
            scalar.wait_ge(sem["sPB"], L["sPB"] + 4 * i + 1)
            if i >= 2:
                scalar.wait_ge(sem["sVf"], L["sVf"] + i - 1)
            scalar.activation(rq[:, b, 0:128], psK[b][:], AF.Relu)
            scalar.activation(rq[:, b, 128:256], psK[b][:], AF.Relu, bias=0.0, scale=-1.0).then_inc(sem["sAB"], 1)
            scalar.wait_ge(sem["sVq"], L["sVq"] + i + 1)
            scalar.activation(sqn[:], s_sb[:, i:i + 1], AF.Sqrt).then_inc(sem["sAr"], 1)
            scalar.wait_ge(sem["sPB"], L["sPB"] + 4 * i + 2)
            scalar.copy(v_sb[:, i, :], psV[b][:]).then_inc(sem["sVv"], 1)
            scalar.wait_ge(sem["sPB"], L["sPB"] + 4 * i + 3)
            scalar.activation(mb_sb[:, i, :], psM[b][:], AF.Sigmoid).then_inc(sem["sVm"], 1)
            scalar.wait_ge(sem["sPB"], L["sPB"] + 4 * i + 4)
            scalar.copy(fT[:, :, i * 128:(i + 1) * 128],
                        psT[i % 2][:].rearrange("p (j d) -> p j d", d=128)).then_inc(sem["sAq"], 1)
    cnt["sAg"] += 4
    cnt["sAB"] += NCH
    cnt["sAr"] += NCH
    cnt["sAq"] += NCH
    cnt["sVm"] += NCH
    cnt["sVv"] += NCH

    vs = [L["sVs"]]

    @block.vector
    def _(vector):
        def vedge(res):
            vs[0] += 1
            res.then_inc(sem["sVs"], 1)
            vector.wait_ge(sem["sVs"], vs[0])

        for i in range(NCH):
            b = i % 2
            vector.wait_ge(sem["sAB"], L["sAB"] + i + 1)
            for j in (1, 2, 3):
                base = 256 * (j - 1)
                vector.tensor_tensor(fTm[:, i, base:base + j], rq[:, b, 0:j], rq[:, b, 256 - j:256], ALU.mult)
                res = vector.tensor_tensor(fTm[:, i, base + j:base + 256], rq[:, b, j:256], rq[:, b, 0:256 - j], ALU.mult)
            res.then_inc(sem["sVf"], 1)
            vector.wait_ge(sem["sVf"], L["sVf"] + i + 1)
            vedge(vector.scalar_tensor_tensor(dumpB[:], fTm[:, i, :], 1.0, fTm[:, i, :],
                                              ALU.mult, ALU.mult, accum_out=s_sb[:, i:i + 1]))
            vector.nop().then_inc(sem["sVq"], 1)
            vector.wait_ge(sem["sAr"], L["sAr"] + i + 1)
            vedge(vector.tensor_scalar(nmax_sb[:, i:i + 1], sqn[:], 1e-12, None, ALU.max))
            vedge(vector.reciprocal(invn_sb[:, i:i + 1], nmax_sb[:, i:i + 1]))

    cnt["sVs"] = vs[0]
    cnt["sVf"] += NCH
    cnt["sVq"] += NCH

    st.close()


def _emit_Bb(nc, block, sem, cnt, env, B, g):
    dbg_bb = env.get("dbg", {}).get("bb_sb")
    dcc_in = env["dcc_in"]
    (fTm, fT, v_sb, mb_sb, s_sb, nmax_sb, invn_sb, coefi_sb, mem_rhs, norm_bc,
     den4, ne1, d4, rec4, nri4, dr4, msq1, rmsq, co4, co4b, mvs, wmv4, dbounce,
     dnbounce, dumpB) = (
        B["fTm"], B["fT"], B["v_sb"], B["mb_sb"], B["s_sb"], B["nmax_sb"], B["invn_sb"],
        B["coefi_sb"], B["mem_rhs"], B["norm_bc"], B["den4"], B["ne1"], B["d4"],
        B["rec4"], B["nri4"], B["dr4"], B["msq1"], B["rmsq"], B["co4"], B["co4b"],
        B["mvs"], B["wmv4"], B["dbounce"], B["dnbounce"], B["dumpB"])

    st = ExitStack()
    ps = lambda n, s, d: st.enter_context(nc.psum_tensor(n, s, d))
    psD = [ps(f"Bb{g}_d{j}", [128, 512], F32) for j in range(6)]
    psN = [ps(f"Bb{g}_n{j}", [128, 512], F32) for j in range(2)]

    L = {k: cnt[k] for k in cnt}

    @block.tensor
    def _(tensor):
        tensor.wait_ge(sem["sAq"], L["sAq"])
        tensor.wait_ge(sem["sVv"], L["sVv"])
        tensor.wait_ge(sem["sVm"], L["sVm"])
        tensor.wait_ge(sem["sAB"], L["sAB"])
        for i in range(NCH):
            b = i % 2
            if i >= 2:
                tensor.wait_ge(sem["sVw"], L["sVw"] + i - 1)
            for fc in range(6):
                mm = tensor.matmul(psN[b][:, 0:512],
                                   fT[:, fc, i * 128:(i + 1) * 128],
                                   mem_rhs[:, 0:4, fc, 0:128],
                                   start=(fc == 0), stop=(fc == 5))
            mm.then_inc(sem["sPN"], 1)
            tensor.wait_ge(sem["sVw"], L["sVw"] + i + 1)
            for fc in range(6):
                mm = tensor.matmul(psD[fc][:], fTm[:, i, 128 * fc:128 * (fc + 1)], wmv4[:, b, :],
                                   start=(i == 0), stop=(i == NCH - 1))
            mm.then_inc(sem["sPd"], 1)
        tensor.wait_ge(sem["sVw"], L["sVw"] + NCH)
        tensor.wait_ge(sem["sVy"], L["sVy"] + 6)
        for fc in range(6):
            for i in range(NCH):
                mm = tensor.matmul(psN[0][:, 4 * fc:4 * (fc + 1)], fTm[:, i, 128 * fc:128 * (fc + 1)],
                                   coefi_sb[:, i, :], start=(i == 0), stop=(i == NCH - 1))
        mm.then_inc(sem["sPd"], 1)
    cnt["sPN"] += NCH
    cnt["sPd"] += NCH + 1

    vs = [L["sVs"]]

    @block.vector
    def _(vector):
        def vedge(res):
            vs[0] += 1
            res.then_inc(sem["sVs"], 1)
            vector.wait_ge(sem["sVs"], vs[0])

        for i in range(NCH):
            b = i % 2
            for h in range(4):
                res = vector.scalar_tensor_tensor(dumpB[:], fTm[:, i, :], 1.0, norm_bc[:, h, :],
                                                  ALU.mult, ALU.mult, accum_out=den4[:, h:h + 1])
            vedge(res)
            vedge(vector.tensor_scalar(ne1[:], nmax_sb[:, i:i + 1], EPS, None, ALU.mult))
            vedge(vector.tensor_scalar(d4[:], den4[:], ne1[:], None, ALU.add))
            vedge(vector.reciprocal(rec4[:], d4[:]))
            vedge(vector.tensor_scalar(nri4[:], rec4[:], -1.0, None, ALU.mult))
            vedge(vector.tensor_scalar(dr4[:], den4[:], invn_sb[:, i:i + 1], EPS, ALU.mult, ALU.add))
            vedge(vector.tensor_scalar(msq1[:], s_sb[:, i:i + 1], invn_sb[:, i:i + 1], invn_sb[:, i:i + 1], ALU.mult, ALU.mult))
            vedge(vector.reciprocal(rmsq[:], msq1[:]))
            vedge(vector.tensor_scalar(co4[:], dr4[:], rmsq[:], -1.0, ALU.mult, ALU.mult))
            vedge(vector.tensor_scalar(co4b[:], co4[:], 1.0, 0.0, ALU.add, ALU.max))
            vedge(vector.tensor_scalar(coefi_sb[:, i, :], co4b[:], 1.0, invn_sb[:, i:i + 1], ALU.min, ALU.mult))
            vector.wait_ge(sem["sPN"], L["sPN"] + i + 1)
            if i >= 2:
                vector.wait_ge(sem["sPd"], L["sPd"] + i - 1)
            for h in range(4):
                vector.scalar_tensor_tensor(mvs[:, 128 * h:128 * (h + 1)], psN[b][:, 128 * h:128 * (h + 1)],
                                            nri4[:, h:h + 1], v_sb[:, i, :], ALU.mult, ALU.add)
            for h in range(4):
                res = vector.scalar_tensor_tensor(wmv4[:, b, 128 * h:128 * (h + 1)], mvs[:, 128 * h:128 * (h + 1)],
                                                  invn_sb[:, i:i + 1], mb_sb[:, i, 128 * h:128 * (h + 1)],
                                                  ALU.mult, ALU.mult)
            if dbg_bb is not None and g == 3 and i == 5:
                vector.tensor_copy(dbg_bb[:, 0:512], psN[b][:, 0:512])
                vector.tensor_copy(dbg_bb[:, 512:516], den4[:])
                vector.tensor_copy(dbg_bb[:, 516:520], d4[:])
                vector.tensor_copy(dbg_bb[:, 520:524], rec4[:])
                vector.tensor_copy(dbg_bb[:, 524:528], nri4[:])
                vector.tensor_copy(dbg_bb[:, 528:1040], wmv4[:, b, :])
                vector.tensor_copy(dbg_bb[:, 1040:1041], invn_sb[:, i:i + 1])
                vector.tensor_copy(dbg_bb[:, 1041:1042], nmax_sb[:, i:i + 1])
                vector.tensor_copy(dbg_bb[:, 1042:1043], s_sb[:, i:i + 1])
            res.then_inc(sem["sVw"], 1)
        vector.wait_ge(sem["sPd"], L["sPd"] + NCH)
        for fc in range(6):
            if fc >= 2:
                vector.wait_ge(sem["sDD"], L["sDD"] + 16 * (fc - 1))
            else:
                vector.wait_ge(sem["sDD"], L["sDD"])
            vector.tensor_copy(dbounce[:, fc % 2, :], psD[fc][:]).then_inc(sem["sVy"], 1)
        vector.wait_ge(sem["sPd"], L["sPd"] + NCH + 1)
        vector.tensor_copy(dnbounce[:], psN[0][:, 0:24]).then_inc(sem["sVy"], 1)
    cnt["sVs"] = vs[0]
    cnt["sVw"] += NCH
    cnt["sVy"] += 7

    @block.gpsimd
    def _(gpsimd):
        for fc in range(6):
            gpsimd.wait_ge(sem["sVy"], L["sVy"] + fc + 1)
            gpsimd.dma_start(out=dcc_in[:, (g * 6 + fc) * 512:(g * 6 + fc + 1) * 512],
                             in_=dbounce[:, fc % 2, :]).then_inc(sem["sDD"], 16)
        gpsimd.wait_ge(sem["sVy"], L["sVy"] + 7)
        gpsimd.dma_start(out=dcc_in[:, 12288 + g * 24:12288 + (g + 1) * 24],
                         in_=dnbounce[:]).then_inc(sem["sDD"], 16)
    cnt["sDD"] += 16 * 7

    st.close()


def emit_phase_D(nc, block, sem, cnt, env):
    xnT_hbm, out_h = env["xnT_hbm"], env["out_h"]
    wgu_h, wdn_h = env["wgu_h"], env["wdn_h"]
    combine = env["combine"]

    st = ExitStack()
    sb = lambda n, s, d: st.enter_context(nc.sbuf_tensor(n, s, d))
    ps = lambda n, s, d: st.enter_context(nc.psum_tensor(n, s, d))

    wgu_sb = sb("D_wgu", [128, KT, I2], BF16)
    wdn_sb = sb("D_wdn", [128, 2, 8, DM], BF16)
    xnt = sb("D_xnt", [128, 2, KT, 512], BF16)
    inter = sb("D_inter", [128, 2, 8, 512], BF16)
    silu_b = sb("D_silu", [128, 2, 512], F32)
    ystage = sb("D_ys", [128, 2, DM], F32)
    psG = [ps(f"D_g{j}", [128, 512], F32) for j in range(2)]
    psU = [ps(f"D_u{j}", [128, 512], F32) for j in range(2)]
    psY = [ps(f"D_y{j}", [128, 512], F32) for j in range(2)]

    import os
    NE_D = int(os.environ.get("KD_EONLY", NE))
    SER = os.environ.get("KD_SERIAL", "")
    L = {k: cnt[k] for k in cnt}
    NTB = T // 512
    NJ = 8
    nTB = NE_D * NTB

    def load_wgu(gpsimd, e):
        for q in range(4):
            gpsimd.dma_start(out=wgu_sb[:, 4 * q:4 * (q + 1), :],
                             in_=wgu_h[e, 512 * q:512 * (q + 1), :].rearrange("(k p) m -> p k m", p=128)).then_inc(sem["sGu"], 16)

    def load_wdn(gpsimd, e):
        for q in range(2):
            gpsimd.dma_start(out=wdn_sb[:, e % 2, 4 * q:4 * (q + 1), :],
                             in_=wdn_h[e, 512 * q:512 * (q + 1), :].rearrange("(k p) m -> p k m", p=128)).then_inc(sem["sGw"], 16)

    @block.gpsimd
    def _(gpsimd):
        gpsimd.wait_ge(sem["sPd"], L["sPd"])
        load_wgu(gpsimd, 0)
        load_wdn(gpsimd, 0)
        for e in range(NE_D):
            for tb in range(NTB):
                for tc in range(4):
                    tcg = e * 32 + tb * 4 + tc
                    ch = tb * 4 + tc
                    b2 = tcg % 2
                    gpsimd.wait_ge(sem["sVy"], L["sVy"] + 4 * (tcg + 1))
                    gpsimd.dma_start(out=out_h[ch * 128:(ch + 1) * 128, :], in_=ystage[:, b2, :],
                                     accum_op=ALU.add).then_inc(sem["sOUT0" if b2 == 0 else "sOUT1"], 16)
                    if tb * 4 + tc == 23 and e + 1 < NE_D:
                        gpsimd.wait_ge(sem["sPg"], L["sPg"] + 16 * NTB * (e + 1))
                        load_wgu(gpsimd, e + 1)
                        gpsimd.wait_ge(sem["sPy"], L["sPy"] + 16 * NTB * e)
                        load_wdn(gpsimd, e + 1)
    cnt["sGu"] += 16 * 4 * NE_D
    cnt["sGw"] += 16 * 2 * NE_D
    n_out_d = NE_D * 32
    cnt["sOUT0"] += 16 * (n_out_d // 2)
    cnt["sOUT1"] += 16 * (n_out_d // 2)

    @block.sync
    def _(sync):
        for e in range(NE_D):
            for tb in range(NTB):
                tbg = e * NTB + tb
                b = tbg % 2
                if tbg >= 2:
                    sync.wait_ge(sem["sPg"], L["sPg"] + 16 * (tbg - 1))
                if tbg == 0:
                    sync.wait_ge(sem["sXW0"], L["sXW0"])
                    sync.wait_ge(sem["sXW1"], L["sXW1"])
                sync.dma_start(out=xnt[:, b, :, :],
                               in_=xnT_hbm[:, :, tb * 512:(tb + 1) * 512].rearrange("k p t -> p k t")
                               ).then_inc(sem["sXL0" if b == 0 else "sXL1"], 16)
    cnt["sXL0"] += 16 * (nTB // 2)
    cnt["sXL1"] += 16 * (nTB // 2)

    @block.tensor
    def _(tensor):
        def emit_down(tbg):
            ib = tbg % 2
            for tc in range(4):
                tcg = tbg * 4 + tc
                for nb in range(4):
                    yt = tcg * 4 + nb
                    yb = yt % 2
                    if yt >= 2:
                        tensor.wait_ge(sem["sVy"], L["sVy"] + yt - 1)
                    for kc in range(8):
                        if tc == 0 and nb == 0:
                            tensor.wait_ge(sem["sVi"], L["sVi"] + tbg * NJ + kc + 1)
                        mm = tensor.matmul(psY[yb][:], inter[:, ib, kc, 128 * tc:128 * (tc + 1)],
                                           wdn_sb[:, (tbg // NTB) % 2, kc, 512 * nb:512 * (nb + 1)],
                                           start=(kc == 0), stop=(kc == 7))
                    mm.then_inc(sem["sPy"], 1)

        tensor.wait_ge(sem["sVy"], L["sVy"])
        for e in range(NE_D):
            tensor.wait_ge(sem["sGu"], L["sGu"] + 64 * (e + 1))
            for tb in range(NTB):
                tbg = e * NTB + tb
                b = tbg % 2
                tensor.wait_ge(sem["sXL0" if b == 0 else "sXL1"],
                               L["sXL0" if b == 0 else "sXL1"] + 16 * (tbg // 2 + 1))
                for j in range(NJ):
                    jg = tbg * NJ + j
                    jb = jg % 2
                    if jg >= 2:
                        tensor.wait_ge(sem["sVi"], L["sVi"] + jg - 1)
                    for kc in range(KT):
                        mm = tensor.matmul(psG[jb][:], wgu_sb[:, kc, 256 * j:256 * j + 128], xnt[:, b, kc, :],
                                           start=(kc == 0), stop=(kc == KT - 1))
                    mm.then_inc(sem["sPg"], 1)
                    for kc in range(KT):
                        mm = tensor.matmul(psU[jb][:], wgu_sb[:, kc, 256 * j + 128:256 * (j + 1)], xnt[:, b, kc, :],
                                           start=(kc == 0), stop=(kc == KT - 1))
                    mm.then_inc(sem["sPg"], 1)
                if tb >= 1:
                    if tb == 1:
                        tensor.wait_ge(sem["sGw"], L["sGw"] + 32 * (e + 1))
                    emit_down(tbg - 1)
            emit_down(e * NTB + NTB - 1)
    cnt["sPg"] += 2 * NJ * nTB
    cnt["sPy"] += 16 * nTB

    @block.scalar
    def _(scalar):
        for e in range(NE_D):
            for tb in range(NTB):
                for j in range(NJ):
                    jg = (e * NTB + tb) * NJ + j
                    jb = jg % 2
                    scalar.wait_ge(sem["sPg"], L["sPg"] + 2 * jg + 1)
                    if jg >= 2:
                        scalar.wait_ge(sem["sVi"], L["sVi"] + jg - 1)
                    scalar.activation(silu_b[:, jb, :], psG[jb][:], AF.Silu).then_inc(sem["sAs"], 1)
    cnt["sAs"] += NJ * nTB

    @block.vector
    def _(vector):
        def emit_ystage(tbg):
            e2, tb2 = tbg // NTB, tbg % NTB
            for tc in range(4):
                tcg = tbg * 4 + tc
                ch = tb2 * 4 + tc
                b2 = tcg % 2
                if tcg >= 2:
                    vector.wait_ge(sem["sOUT0" if b2 == 0 else "sOUT1"],
                                   L["sOUT0" if b2 == 0 else "sOUT1"] + 16 * (tcg // 2))
                for nb in range(4):
                    yt = tcg * 4 + nb
                    yb = yt % 2
                    vector.wait_ge(sem["sPy"], L["sPy"] + yt + 1)
                    vector.tensor_scalar(ystage[:, b2, 512 * nb:512 * (nb + 1)], psY[yb][:],
                                         combine[:, ch, e2:e2 + 1], None, ALU.mult).then_inc(sem["sVy"], 1)

        for e in range(NE_D):
            for tb in range(NTB):
                tbg = e * NTB + tb
                for j in range(NJ):
                    jg = tbg * NJ + j
                    jb = jg % 2
                    vector.wait_ge(sem["sAs"], L["sAs"] + jg + 1)
                    vector.wait_ge(sem["sPg"], L["sPg"] + 2 * jg + 2)
                    if tbg >= 2 and j == 0:
                        vector.wait_ge(sem["sPy"], L["sPy"] + 16 * (tbg - 1))
                    vector.tensor_tensor(inter[:, tbg % 2, j, :], silu_b[:, jb, :], psU[jb][:], ALU.mult).then_inc(sem["sVi"], 1)
                if tb >= 1:
                    emit_ystage(tbg - 1)
            emit_ystage(e * NTB + NTB - 1)
    cnt["sVi"] += NJ * nTB
    cnt["sVy"] += 16 * nTB

    st.close()


def emit_phase_C(nc, block, sem, cnt, env):
    xnT_hbm, out_h, mh_hbm = env["xnT_hbm"], env["out_h"], env["mh_hbm"]
    wqT_h, wgT_h, bq_h, bg_h = env["wqT_h"], env["wgT_h"], env["bq_h"], env["bg_h"]
    memcat_h, dcc_out = env["memcat_h"], env["dcc_out"]
    identb, onesb = env["identb"], env["onesb"]

    st = ExitStack()
    sb = lambda n, s, d: st.enter_context(nc.sbuf_tensor(n, s, d))

    new_mem = sb("C_newmem", [128, H, 6, 129], BF16)
    mpc = sb("C_mpc", [128, 2, 4, 129], F32)
    dpc = sb("C_dpc", [128, 2, 512], F32)
    npc = sb("C_npc", [128, 2, 4], F32)
    if env.get("dbg"):
        env["dbg"]["Ct"] = {"new_mem": new_mem}

    L = {k: cnt[k] for k in cnt}

    @block.sync
    def _(sync):
        sync.wait_ge(sem["sCC"], L["sCC"])
        for t in range(24):
            g, fc = t // 6, t % 6
            b = t % 2
            if t >= 2:
                sync.wait_ge(sem["sVm"], L["sVm"] + t - 1)
            sync.dma_start(out=mpc[:, b, :, :],
                           in_=memcat_h[4 * g:4 * (g + 1), 128 * fc:128 * (fc + 1), :].rearrange("h p c -> p h c")
                           ).then_inc(sem["sXL0" if b == 0 else "sXL1"], 16)
            sync.dma_start(out=dpc[:, b, :],
                           in_=dcc_out[:, (g * 6 + fc) * 512:(g * 6 + fc + 1) * 512]).then_inc(sem["sXL0" if b == 0 else "sXL1"], 16)
            sync.dma_start(out=npc[:, b, :],
                           in_=dcc_out[:, 12288 + g * 24 + 4 * fc:12288 + g * 24 + 4 * (fc + 1)]).then_inc(sem["sXL0" if b == 0 else "sXL1"], 16)
    cnt["sXL0"] += 16 * 36
    cnt["sXL1"] += 16 * 36

    @block.vector
    def _(vector):
        for t in range(24):
            g, fc = t // 6, t % 6
            b = t % 2
            s_n = "sXL0" if b == 0 else "sXL1"
            vector.wait_ge(sem[s_n], L[s_n] + 48 * (t // 2 + 1))
            vector.tensor_tensor(new_mem[:, 4 * g:4 * (g + 1), fc, 0:128],
                                 mpc[:, b, :, 0:128],
                                 dpc[:, b, :].rearrange("p (h d) -> p h d", d=128), ALU.add)
            vector.tensor_tensor(new_mem[:, 4 * g:4 * (g + 1), fc, 128:129],
                                 mpc[:, b, :, 128:129],
                                 npc[:, b, :].rearrange("p (h one) -> p h one", one=1), ALU.add).then_inc(sem["sVm"], 1)
    cnt["sVm"] += 24

    _emit_C1(nc, block, sem, cnt, env, locals())
    _emit_C2(nc, block, sem, cnt, env, locals())

    st.close()


def _emit_C1(nc, block, sem, cnt, env, C):
    xnT_hbm, mh_hbm = env["xnT_hbm"], env["mh_hbm"]
    wqT_h, bq_h = env["wqT_h"], env["bq_h"]
    identb, onesb = env["identb"], env["onesb"]
    new_mem = C["new_mem"]

    st = ExitStack()
    sb = lambda n, s, d: st.enter_context(nc.sbuf_tensor(n, s, d))
    ps = lambda n, s, d: st.enter_context(nc.psum_tensor(n, s, d))
    wq = sb("C1_wq", [128, KT, DM], BF16)
    bqb = sb("C1_bq", [1, DM], BF16)
    xnt = sb("C1_xnt", [128, 2, KT, 128], BF16)
    qf = sb("C1_qf", [128, 2, DM], F32)
    rqC = sb("C1_rq", [128, 8, 256], BF16)
    fq8 = sb("C1_fq", [128, 8, 768], BF16)
    fqT = sb("C1_fqT", [128, 2, 768], BF16)
    dumpC = sb("C1_dump", [128, F], BF16)
    sq8 = sb("C1_sq", [128, 8], F32)
    sqr8 = sb("C1_sqr", [128, 8], F32)
    ne8 = sb("C1_ne", [128, 8], F32)
    den8 = sb("C1_den", [128, 8], F32)
    d8 = sb("C1_d8", [128, 8], F32)
    rc8 = sb("C1_rc", [128, 8], F32)
    mh = sb("C1_mh", [128, 2, DM], F32)
    psQ = [ps(f"C1_q{j}", [128, 512], F32) for j in range(2)]
    psT = [ps(f"C1_t{j}", [128, 768], BF16) for j in range(2)]
    psN2 = [ps(f"C1_n{j}", [128, 129], F32) for j in range(2)]

    L = {k: cnt[k] for k in cnt}

    @block.gpsimd
    def _(gpsimd):
        gpsimd.wait_ge(sem["sPg"], L["sPg"])
        gpsimd.wait_ge(sem["sPy"], L["sPy"])
        for q in range(4):
            gpsimd.dma_start(out=wq[:, 4 * q:4 * (q + 1), :],
                             in_=wqT_h[4 * q:4 * (q + 1)].rearrange("k p m -> p k m")).then_inc(sem["sGD"], 16)
        gpsimd.dma_start(out=bqb[:], in_=bq_h).then_inc(sem["sGD"], 16)
    cnt["sGD"] += 16 * 5
    nGD = cnt["sGD"]

    @block.sync
    def _(sync):
        for i in range(NCH):
            b = i % 2
            if i == 0:
                sync.wait_ge(sem["sPg"], L["sPg"])
                sync.wait_ge(sem["sPy"], L["sPy"])
            if i >= 2:
                sync.wait_ge(sem["sPq"], L["sPq"] + 4 * (i - 1))
            sync.dma_start(out=xnt[:, b, :, :],
                           in_=xnT_hbm[:, :, i * 128:(i + 1) * 128].rearrange("k p t -> p k t")
                           ).then_inc(sem["sXL0" if b == 0 else "sXL1"], 16)
    cnt["sXL0"] += 16 * (NCH // 2)
    cnt["sXL1"] += 16 * (NCH // 2)

    @block.tensor
    def _(tensor):
        tensor.wait_ge(sem["sGD"], nGD)
        tensor.wait_ge(sem["sVy"], L["sVy"])
        tensor.wait_ge(sem["sAs"], L["sAs"])
        tensor.wait_ge(sem["sVi"], L["sVi"])
        tensor.wait_ge(sem["sVm"], L["sVm"])
        for i in range(NCH):
            b = i % 2
            s_n = "sXL0" if b == 0 else "sXL1"
            tensor.wait_ge(sem[s_n], L[s_n] + 16 * (i // 2 + 1))
            for nq in range(4):
                if 4 * i + nq >= 2:
                    tensor.wait_ge(sem["sAq"], L["sAq"] + 4 * i + nq - 1)
                for kc in range(KT):
                    tensor.matmul(psQ[nq % 2][:], xnt[:, b, kc, :], wq[:, kc, 512 * nq:512 * (nq + 1)],
                                  start=(kc == 0), stop=False)
                tensor.matmul(psQ[nq % 2][:], onesb[:, :], bqb[:, 512 * nq:512 * (nq + 1)],
                              start=False, stop=True).then_inc(sem["sPq"], 1)
            for hg in range(2):
                hgg = 2 * i + hg
                tensor.wait_ge(sem["sVf"], L["sVf"] + hgg + 1)
                for s in range(9):
                    if s < 8:
                        hh = s
                        ht = hgg * 8 + hh
                        if ht >= 2:
                            tensor.wait_ge(sem["sAt"], L["sAt"] + ht - 1)
                        for j in range(6):
                            mm = tensor.transpose(psT[ht % 2][:, 128 * j:128 * (j + 1)],
                                                  fq8[:, hh, 128 * j:128 * (j + 1)], identb[:])
                        mm.then_inc(sem["sPt"], 1)
                    if s >= 1:
                        hh = s - 1
                        h = 8 * hg + hh
                        hb = hh % 2
                        ht = hgg * 8 + hh
                        tensor.wait_ge(sem["sAt"], L["sAt"] + ht + 1)
                        if ht >= 2:
                            tensor.wait_ge(sem["sVn"], L["sVn"] + ht - 1)
                        for fc in range(6):
                            mm = tensor.matmul(psN2[hh % 2][:], fqT[:, hb, 128 * fc:128 * (fc + 1)], new_mem[:, h, fc, :],
                                               start=(fc == 0), stop=(fc == 5))
                        mm.then_inc(sem["sPn"], 1)
    cnt["sPq"] += 4 * NCH
    cnt["sPt"] += 16 * NCH
    cnt["sPn"] += 16 * NCH

    @block.scalar
    def _(scalar):
        for i in range(NCH):
            b = i % 2
            for nq in range(4):
                scalar.wait_ge(sem["sPq"], L["sPq"] + 4 * i + nq + 1)
                if i >= 2 and nq == 0:
                    scalar.wait_ge(sem["sVf"], L["sVf"] + 2 * i - 2)
                scalar.copy(qf[:, b, 512 * nq:512 * (nq + 1)], psQ[nq % 2][:]).then_inc(sem["sAq"], 1)
            if i >= 1:
                bp = (i - 1) % 2
                scalar.wait_ge(sem["sVw"], L["sVw"] + 2 * i)
                scalar.dma_start(out=mh_hbm[128 * (i - 1):128 * i, :], in_=mh[:, bp, :]).then_inc(sem["sDD"], 16)
            for hg in range(2):
                hgg = 2 * i + hg
                if hgg >= 1:
                    scalar.wait_ge(sem["sVf"], L["sVf"] + hgg)
                scalar.activation(rqC[:, :, 0:128],
                                  qf[:, b, 1024 * hg:1024 * (hg + 1)].rearrange("p (h d) -> p h d", d=128), AF.Relu)
                scalar.activation(rqC[:, :, 128:256],
                                  qf[:, b, 1024 * hg:1024 * (hg + 1)].rearrange("p (h d) -> p h d", d=128),
                                  AF.Relu, bias=0.0, scale=-1.0).then_inc(sem["sAg"], 1)
                scalar.wait_ge(sem["sVq"], L["sVq"] + hgg + 1)
                scalar.activation(sqr8[:], sq8[:], AF.Sqrt).then_inc(sem["sAr"], 1)
                for hh in range(8):
                    ht = hgg * 8 + hh
                    scalar.wait_ge(sem["sPt"], L["sPt"] + ht + 1)
                    if ht >= 2:
                        scalar.wait_ge(sem["sPn"], L["sPn"] + ht - 1)
                    scalar.copy(fqT[:, hh % 2, :], psT[ht % 2][:]).then_inc(sem["sAt"], 1)
    cnt["sAq"] += 4 * NCH
    cnt["sAg"] += 2 * NCH
    cnt["sAr"] += 2 * NCH
    cnt["sAt"] += 16 * NCH

    @block.scalar
    def _(scalar):
        bp = (NCH - 1) % 2
        scalar.wait_ge(sem["sVw"], L["sVw"] + 2 * NCH)
        scalar.dma_start(out=mh_hbm[128 * (NCH - 1):128 * NCH, :], in_=mh[:, bp, :]).then_inc(sem["sDD"], 16)
    cnt["sDD"] += 16 * NCH

    vs = [L["sVs"]]

    @block.vector
    def _(vector):
        def vedge(res):
            vs[0] += 1
            res.then_inc(sem["sVs"], 1)
            vector.wait_ge(sem["sVs"], vs[0])

        for i in range(NCH):
            b = i % 2
            if i >= 2:
                vector.wait_ge(sem["sDD"], L["sDD"] + 16 * (i - 1))
            for hg in range(2):
                hgg = 2 * i + hg
                vector.wait_ge(sem["sAg"], L["sAg"] + hgg + 1)
                if hgg >= 1:
                    vector.wait_ge(sem["sPt"], L["sPt"] + 8 * hgg)
                for j in (1, 2, 3):
                    base = 256 * (j - 1)
                    vector.tensor_tensor(fq8[:, :, base:base + j], rqC[:, :, 0:j], rqC[:, :, 256 - j:256], ALU.mult)
                    res = vector.tensor_tensor(fq8[:, :, base + j:base + 256], rqC[:, :, j:256], rqC[:, :, 0:256 - j], ALU.mult)
                res.then_inc(sem["sVf"], 1)
                vector.wait_ge(sem["sVf"], L["sVf"] + hgg + 1)
                for hh in range(8):
                    res = vector.scalar_tensor_tensor(dumpC[:], fq8[:, hh, :], 1.0, fq8[:, hh, :],
                                                      ALU.mult, ALU.mult, accum_out=sq8[:, hh:hh + 1])
                vedge(res)
                vector.nop().then_inc(sem["sVq"], 1)
                vector.wait_ge(sem["sAr"], L["sAr"] + hgg + 1)
                vedge(vector.tensor_scalar(ne8[:], sqr8[:], 1e-12, 1e-8, ALU.max, ALU.mult))
                for hh in range(8):
                    vector.wait_ge(sem["sPn"], L["sPn"] + 16 * i + 8 * hg + hh + 1)
                    vector.tensor_copy(mh[:, b, 1024 * hg + 128 * hh:1024 * hg + 128 * (hh + 1)],
                                       psN2[hh % 2][:, 0:128])
                    res = vector.tensor_copy(den8[:, hh:hh + 1], psN2[hh % 2][:, 128:129])
                    res.then_inc(sem["sVn"], 1)
                vector.wait_ge(sem["sVn"], L["sVn"] + 16 * i + 8 * hg + 8)
                vedge(vector.tensor_tensor(d8[:], den8[:], ne8[:], ALU.add))
                vedge(vector.reciprocal(rc8[:], d8[:]))
                for hh in range(8):
                    res = vector.tensor_scalar(mh[:, b, 1024 * hg + 128 * hh:1024 * hg + 128 * (hh + 1)],
                                               mh[:, b, 1024 * hg + 128 * hh:1024 * hg + 128 * (hh + 1)],
                                               rc8[:, hh:hh + 1], None, ALU.mult)
                res.then_inc(sem["sVw"], 1)
    cnt["sVf"] += 2 * NCH
    cnt["sVq"] += 2 * NCH
    cnt["sVn"] += 16 * NCH
    cnt["sVw"] += 2 * NCH
    cnt["sVs"] = vs[0]

    st.close()


def _emit_C2(nc, block, sem, cnt, env, C):
    xnT_hbm, out_h, mh_hbm = env["xnT_hbm"], env["out_h"], env["mh_hbm"]
    wgT_h, bg_h = env["wgT_h"], env["bg_h"]
    onesb = env["onesb"]

    st = ExitStack()
    sb = lambda n, s, d: st.enter_context(nc.sbuf_tensor(n, s, d))
    ps = lambda n, s, d: st.enter_context(nc.psum_tensor(n, s, d))
    wg = sb("C2_wg", [128, KT, DM], BF16)
    bgb = sb("C2_bg", [1, DM], BF16)
    xnt = sb("C2_xnt", [128, 2, KT, 128], BF16)
    mhs = sb("C2_mh", [128, 2, DM], F32)
    gatef = sb("C2_gate", [128, 2, DM], F32)
    ctb = sb("C2_ctb", [128, 2, DM], F32)
    psG2 = [ps(f"C2_g{j}", [128, 512], F32) for j in range(2)]

    L = {k: cnt[k] for k in cnt}

    @block.gpsimd
    def _(gpsimd):
        gpsimd.wait_ge(sem["sPq"], L["sPq"])
        gpsimd.wait_ge(sem["sPn"], L["sPn"])
        for q in range(4):
            gpsimd.dma_start(out=wg[:, 4 * q:4 * (q + 1), :],
                             in_=wgT_h[4 * q:4 * (q + 1)].rearrange("k p m -> p k m")).then_inc(sem["sGD"], 16)
        gpsimd.dma_start(out=bgb[:], in_=bg_h).then_inc(sem["sGD"], 16)
        for i in range(NCH):
            b = i % 2
            gpsimd.wait_ge(sem["sVi"], L["sVi"] + i + 1)
            gpsimd.dma_start(out=out_h[128 * i:128 * (i + 1), :], in_=ctb[:, b, :],
                             accum_op=ALU.add).then_inc(sem["sOUT0" if b == 0 else "sOUT1"], 16)
    cnt["sGD"] += 16 * 5
    nGD = cnt["sGD"]
    cnt["sOUT0"] += 16 * (NCH // 2)
    cnt["sOUT1"] += 16 * (NCH // 2)

    @block.sync
    def _(sync):
        for i in range(NCH):
            b = i % 2
            if i == 0:
                sync.wait_ge(sem["sDD"], L["sDD"])
            if i >= 2:
                sync.wait_ge(sem["sPq"], L["sPq"] + 4 * (i - 1))
            sync.dma_start(out=xnt[:, b, :, :],
                           in_=xnT_hbm[:, :, i * 128:(i + 1) * 128].rearrange("k p t -> p k t")
                           ).then_inc(sem["sXL0" if b == 0 else "sXL1"], 16)
            if i >= 2:
                sync.wait_ge(sem["sVi"], L["sVi"] + i - 1)
            sync.dma_start(out=mhs[:, b, :], in_=mh_hbm[128 * i:128 * (i + 1), :]).then_inc(sem["sXL0" if b == 0 else "sXL1"], 16)
    cnt["sXL0"] += 32 * (NCH // 2)
    cnt["sXL1"] += 32 * (NCH // 2)

    @block.tensor
    def _(tensor):
        tensor.wait_ge(sem["sGD"], nGD)
        tensor.wait_ge(sem["sVn"], L["sVn"])
        tensor.wait_ge(sem["sAq"], L["sAq"])
        tensor.wait_ge(sem["sAt"], L["sAt"])
        for i in range(NCH):
            b = i % 2
            s_n = "sXL0" if b == 0 else "sXL1"
            tensor.wait_ge(sem[s_n], L[s_n] + 32 * (i // 2 + 1))
            for nq in range(4):
                if 4 * i + nq >= 2:
                    tensor.wait_ge(sem["sAq"], L["sAq"] + 4 * i + nq - 1)
                for kc in range(KT):
                    tensor.matmul(psG2[nq % 2][:], xnt[:, b, kc, :], wg[:, kc, 512 * nq:512 * (nq + 1)],
                                  start=(kc == 0), stop=False)
                tensor.matmul(psG2[nq % 2][:], onesb[:, :], bgb[:, 512 * nq:512 * (nq + 1)],
                              start=False, stop=True).then_inc(sem["sPq"], 1)
    cnt["sPq"] += 4 * NCH

    @block.scalar
    def _(scalar):
        for i in range(NCH):
            b = i % 2
            for nq in range(4):
                scalar.wait_ge(sem["sPq"], L["sPq"] + 4 * i + nq + 1)
                if i >= 1 and nq == 0:
                    scalar.wait_ge(sem["sVi"], L["sVi"] + i)
                scalar.activation(gatef[:, b, 512 * nq:512 * (nq + 1)], psG2[nq % 2][:], AF.Sigmoid).then_inc(sem["sAq"], 1)
    cnt["sAq"] += 4 * NCH

    @block.vector
    def _(vector):
        for i in range(NCH):
            b = i % 2
            s_n = "sXL0" if b == 0 else "sXL1"
            vector.wait_ge(sem["sAq"], L["sAq"] + 4 * (i + 1))
            vector.wait_ge(sem[s_n], L[s_n] + 32 * (i // 2 + 1))
            if i >= 2:
                vector.wait_ge(sem["sOUT0" if b == 0 else "sOUT1"],
                               L["sOUT0" if b == 0 else "sOUT1"] + 16 * (i // 2))
            vector.tensor_tensor(ctb[:, b, :], gatef[:, b, :], mhs[:, b, :], ALU.mult).then_inc(sem["sVi"], 1)
    cnt["sVi"] += NCH

    st.close()


def emit_dumps(nc, block, sem, cnt, env, dbg, phases):
    xnT_hbm = env["xnT_hbm"]
    combine = env["combine"]

    @block.sync
    def _(sync):
        base = cnt["sLD"]
        n = 0
        sync.wait_ge(sem["sXW0"], cnt["sXW0"])
        sync.wait_ge(sem["sXW1"], cnt["sXW1"])
        if "xnT" in dbg:
            for k in range(KT):
                sync.dma_start(out=dbg["xnT"][k], in_=xnT_hbm[k]).then_inc(sem["sLD"], 16)
                n += 16
        if "dccout" in dbg and "B" in phases:
            sync.wait_ge(sem["sCC"], cnt["sCC"])
            sync.dma_start(out=dbg["dccout"], in_=env["dcc_out"][:, :]).then_inc(sem["sLD"], 16)
            sync.dma_start(out=dbg["dccin"], in_=env["dcc_in"][:, :]).then_inc(sem["sLD"], 16)
            n += 32
            if "bb" in dbg:
                sync.dma_start(out=dbg["bb"], in_=dbg["bb_sb"][:]).then_inc(sem["sLD"], 16)
                n += 16
            if "fTm" in dbg and dbg.get("Bt") is not None:
                bt = dbg["Bt"]
                sync.dma_start(out=dbg["fTm"], in_=bt["fTm"][:]).then_inc(sem["sLD"], 16)
                sync.dma_start(out=dbg["vmb"][:, :, 0:128], in_=bt["v_sb"][:]).then_inc(sem["sLD"], 16)
                sync.dma_start(out=dbg["vmb"][:, :, 128:640], in_=bt["mb_sb"][:]).then_inc(sem["sLD"], 16)
                n += 48
        if "combine" in dbg:
            sync.dma_start(out=dbg["combine"], in_=combine[:]).then_inc(sem["sLD"], 16)
            n += 16
        sync.wait_ge(sem["sLD"], base + n)
        cnt["sLD"] += n


def _prep_inputs(inputs):
    f32 = np.float32
    c = np.ascontiguousarray
    d = {}
    d["wqT"] = c(inputs["L0_attn_Wq"].T.reshape(KT, 128, DM).astype(f32, copy=False))
    d["wgT"] = c(inputs["L0_attn_Wg"].T.reshape(KT, 128, DM).astype(f32, copy=False))
    d["wmbT"] = c(inputs["L0_attn_Wmb"].T.reshape(KT, 128, DM).astype(f32, copy=False))
    d["wkT"] = c(inputs["L0_attn_Wk"].T.reshape(KT, 128, 512).astype(f32, copy=False))
    d["wvT"] = c(inputs["L0_attn_Wv"].T.reshape(KT, 128, 512).astype(f32, copy=False))
    d["bq"] = c(inputs["L0_attn_bq"].reshape(1, DM).astype(f32, copy=False))
    d["bg"] = c(inputs["L0_attn_bg"].reshape(1, DM).astype(f32, copy=False))
    d["bmb"] = c(inputs["L0_attn_bmb"].reshape(1, DM).astype(f32, copy=False))
    d["bk"] = c(inputs["L0_attn_bk"].reshape(1, 512).astype(f32, copy=False))
    d["bv"] = c(inputs["L0_attn_bv"].reshape(1, 512).astype(f32, copy=False))
    mem = np.asarray(inputs["L0_attn_memory"], f32)
    nrm = np.asarray(inputs["L0_attn_norm"], f32)
    d["memcat"] = c(np.concatenate([mem, nrm[:, :, None]], axis=2))
    d["normr"] = c(nrm)
    d["gwT"] = c(inputs["L0_ffn_gate_w"].T.reshape(KT, 128, NE).astype(f32, copy=False))
    gup = np.asarray(inputs["L0_ffn_gate_up_proj"], f32)
    perm = np.empty(I2, np.int64)
    for j in range(II // 128):
        perm[256 * j:256 * j + 128] = np.arange(128 * j, 128 * (j + 1))
        perm[256 * j + 128:256 * j + 256] = II + np.arange(128 * j, 128 * (j + 1))
    d["wgu"] = c(gup[:, :, perm])
    d["wdn"] = c(np.asarray(inputs["L0_ffn_down_proj"], f32))
    d["identf"] = np.eye(128, dtype=f32)
    d["identb"] = np.eye(128, dtype=ml_dtypes.bfloat16)
    d["onesb"] = np.ones((1, 128), dtype=ml_dtypes.bfloat16)
    return d


def kernel(**inputs):
    shared = _prep_inputs(inputs)
    x = np.ascontiguousarray(np.asarray(inputs["L0_attn_hidden_states"], np.float32).reshape(B * S, DM))

    nc = build_program()
    in_maps = []
    for c in range(NCORES):
        m = dict(shared)
        m["x"] = x[c * T:(c + 1) * T]
        in_maps.append(m)
    res = run_bass_kernel_spmd(nc, in_maps, core_ids=list(range(NCORES)))
    out = np.concatenate([np.asarray(r["out"]) for r in res.results], axis=0)
    return out.reshape(B, S, DM)
